# version 11
# baseline (speedup 1.0000x reference)
import math
import numpy as np
import concourse.bass as bass
import concourse.mybir as mybir
from concourse.bass_utils import run_bass_kernel_spmd

F32 = mybir.dt.float32
BF16 = mybir.dt.bfloat16
AF = mybir.ActivationFunctionType
ALU = mybir.AluOpType
AX = mybir.AxisListType

NCORES = 8


def real_cfg():
    return dict(D=2048, DI=4096, DXB=1024, DST=16, DCONV=4, DTR=128,
                MI=8192, FI=5632, TOK=1024, EPS=1e-5)


class Sched:
    COMPUTE = ("tensor", "vector", "scalar", "gpsimd", "sync")

    def __init__(self):
        self.ops = []
        self.lastw = {}
        self.readers = {}

    def emit(self, eng, fn, r=(), w=(), dma=False, tag="", force=False, nmul=1):
        i = len(self.ops)
        deps = set()
        for t in r:
            if t in self.lastw:
                deps.add(self.lastw[t])
        for t in w:
            if t in self.lastw:
                deps.add(self.lastw[t])
            for j in self.readers.get(t, ()):
                deps.add(j)
        deps.discard(i)
        self.ops.append(dict(i=i, eng=eng, fn=fn, dma=dma, deps=deps, tag=tag,
                             need_inc=force or dma, seq=None, nmul=nmul))
        for t in r:
            self.readers.setdefault(t, []).append(i)
        for t in w:
            self.lastw[t] = i
            self.readers[t] = []
        return i

    def stream_of(self, rec):
        return rec["eng"] + ".q" if rec["dma"] else rec["eng"]

    NLANES = 8

    def finalize(self, nc):
        ops = self.ops
        for rec in ops:
            for d in rec["deps"]:
                ops[d]["need_inc"] = True
        seqno = {}
        lane_rot = {}
        for rec in ops:
            if not rec["need_inc"]:
                continue
            if rec["dma"]:
                q = rec["eng"] + ".q"
                lane = lane_rot.get(q, 0)
                lane_rot[q] = (lane + 1) % self.NLANES
                s = (q, lane)
            else:
                s = self.stream_of(rec)
            rec["stream"] = s
            prev = seqno.get(s, 0)
            rec["gate"] = prev if rec["dma"] else None
            seqno[s] = prev + rec.get("nmul", 1)
            rec["seq"] = seqno[s]
        sems = {}
        for s in sorted(seqno, key=str):
            nm = s if isinstance(s, str) else f"{s[0]}_{s[1]}"
            sems[s] = nc.semaphore("sem_" + nm.replace(".", "_")).__enter__()
        per_eng = {e: [] for e in self.COMPUTE}
        waited = {e: {} for e in self.COMPUTE}
        for rec in ops:
            waits = []
            for d in sorted(rec["deps"]):
                p = ops[d]
                if not p["need_inc"]:
                    continue
                s = p["stream"] if "stream" in p else self.stream_of(p)
                v = p["seq"] * (16 if p["dma"] else 1)
                if waited[rec["eng"]].get(s, 0) < v:
                    waited[rec["eng"]][s] = v
                    waits.append((s, v))
            if rec["dma"] and rec["gate"]:
                s = rec["stream"]
                v = rec["gate"] * 16
                if waited[rec["eng"]].get(s, 0) < v:
                    waited[rec["eng"]][s] = v
                    waits.append((s, v))
            per_eng[rec["eng"]].append((rec, waits))

        def mk(eng_name):
            def body(e):
                for rec, wl in per_eng[eng_name]:
                    for (s, v) in wl:
                        e.wait_ge(sems[s], v)
                    inst = rec["fn"](e)
                    if rec["need_inc"]:
                        assert inst is not None, f"op '{rec['tag']}' returned None"
                        amt = 16 if rec["dma"] else 1
                        insts = inst if isinstance(inst, list) else [inst]
                        assert len(insts) == rec.get("nmul", 1), rec["tag"]
                        for it in insts:
                            it.then_inc(sems[rec["stream"]], amt)
            return body

        with nc.Block() as block:
            for eng_name in self.COMPUTE:
                if per_eng[eng_name]:
                    getattr(block, eng_name)(mk(eng_name))

    def t(self, fn, r=(), w=(), tag=""):
        return self.emit("tensor", fn, r, w, tag=tag)

    def v(self, fn, r=(), w=(), tag=""):
        return self.emit("vector", fn, r, w, tag=tag)

    def s(self, fn, r=(), w=(), tag=""):
        return self.emit("scalar", fn, r, w, tag=tag)

    def g(self, fn, r=(), w=(), tag=""):
        return self.emit("gpsimd", fn, r, w, tag=tag)

    def dW(self, fn, r=(), w=(), tag="", nmul=1):
        return self.emit("sync", fn, r, w, dma=True, tag=tag, nmul=nmul)

    def dR(self, fn, r=(), w=(), tag="", nmul=1):
        return self.emit("scalar", fn, r, w, dma=True, tag=tag, nmul=nmul)

    def dS(self, fn, r=(), w=(), tag=""):
        return self.emit("gpsimd", fn, r, w, dma=True, tag=tag)


def ap4(base_ap, offset, dims):
    return bass.AP(base_ap.tensor, base_ap.offset + offset, [list(d) for d in dims])


def build_nc(cfg):
    D, DI, DXB, DST = cfg["D"], cfg["DI"], cfg["DXB"], cfg["DST"]
    DCONV, DTR, MI, FI = cfg["DCONV"], cfg["DTR"], cfg["MI"], cfg["FI"]
    TOK, EPS = cfg["TOK"], cfg["EPS"]
    TOKP = TOK + 3
    KD = D // 128
    NCT = DI // 128
    H = DI // DST
    REP = DI // DXB
    PROJ = 2 * DI + 2 * DXB + DTR
    NPT = PROJ // 128
    NXT = DXB // 128
    PT_Z0, PT_X0 = 0, DI // 128
    PT_B0 = PT_X0 + NXT
    PT_C0 = PT_B0 + NXT
    PT_DT = PT_C0 + DI // 128
    HPC = 128 // DST
    NHXB = max(1, HPC // REP)
    TC = min(cfg.get("TCAP", 512), TOK)
    NTC = TOK // TC
    TH = TC
    NTH = TOK // TH
    SC = min(cfg.get("SCAP", 64), TOK, TH)
    NSC = TH // SC
    NMI, NFI = MI // 128, FI // 128
    assert D % 128 == 0 and DI % 128 == 0 and DXB % 128 == 0 and PROJ % 128 == 0
    assert REP * DST * NHXB == 128 and HPC * DST == 128

    nc = bass.Bass("TRN2", target_bir_lowering=False, debug=False)

    def di(name, shape):
        return nc.dram_tensor(name, shape, F32, kind="ExternalInput").ap()

    x_in = di("x_in", [D, TOKP])
    w1p = di("w1p", [NPT * 128, KD * 128])
    dtwt = di("dtwt", [DTR, DI])
    dtb = di("dtb", [DI])
    convw = di("convw", [DI, DCONV])
    convb = di("convb", [DI])
    alog = di("alog", [DI, DST])
    dpar = di("dpar", [DI])
    woutt = di("woutt", [DI, D])
    ln1w = di("ln1w", [D])
    ln2w = di("ln2w", [D])
    wgp = di("wgp", [NMI * 128, KD * 128])
    wup = di("wup", [NMI * 128, KD * 128])
    wdt = di("wdt", [MI, D])
    fgp = di("fgp", [NFI * 128, KD * 128])
    fup = di("fup", [NFI * 128, KD * 128])
    fdt = di("fdt", [FI, D])
    fgs = di("fgs", [FI])
    fus = di("fus", [FI])
    fds = di("fds", [D])
    gmc = di("gmc", [128, 64])
    m2c = di("m2c", [128, 8])

    out_d = nc.dram_tensor("out", [D, TOK], F32, kind="ExternalOutput").ap()

    def dsc(name, shape, dt=F32):
        return nc.dram_tensor(name, shape, dt).ap()

    xp_d = dsc("xp_d", [DXB, TOKP])
    b_d = dsc("b_d", [DXB, TOK])
    z_d = dsc("z_d", [DI, TOK])
    cpl_d = dsc("cpl_d", [H, DST * TOK])
    dt_d = dsc("dt_d", [DI, TOK])
    a_d = dsc("a_d", [DI, TOK])
    xcd_d = dsc("xcd_d", [DI, TOK])
    h_d = dsc("h_d", [NCT * 128 * DST * TOK])
    midh_d = dsc("midh_d", [MI, TOK], BF16)
    midl_d = dsc("midl_d", [MI, TOK], BF16)
    fmh_d = dsc("fmh_d", [FI, TOK], BF16)
    fml_d = dsc("fml_d", [FI, TOK], BF16)
    wouth_d = dsc("wouth_d", [DI, D], BF16)
    woutl_d = dsc("woutl_d", [DI, D], BF16)
    wdh_d = dsc("wdh_d", [MI, D], BF16)
    wdl_d = dsc("wdl_d", [MI, D], BF16)
    fdh_d = dsc("fdh_d", [FI, D], BF16)
    fdl_d = dsc("fdl_d", [FI, D], BF16)
    wgh_d = dsc("wgh_d", [NMI * 128, KD * 128], BF16)
    wgl_d = dsc("wgl_d", [NMI * 128, KD * 128], BF16)
    wuh_d = dsc("wuh_d", [NMI * 128, KD * 128], BF16)
    wul_d = dsc("wul_d", [NMI * 128, KD * 128], BF16)
    fgh_d = dsc("fgh_d", [NFI * 128, KD * 128], BF16)
    fgl_d = dsc("fgl_d", [NFI * 128, KD * 128], BF16)
    fuh_d = dsc("fuh_d", [NFI * 128, KD * 128], BF16)
    ful_d = dsc("ful_d", [NFI * 128, KD * 128], BF16)
    CCN = DI * DST + DI
    cc_in = nc.dram_tensor("cc_in", [1, CCN], F32)
    cc_out = nc.dram_tensor("cc_out", [NCORES, CCN], F32)

    S = Sched()

    avail = (nc.sbuf_top - nc.sbuf_base - 512) & ~31
    guard = nc.alloc_sbuf_tensor("manual_region", [128, avail], mybir.dt.uint8)
    gbase = nc.lookup_mloc(guard).addr
    off = [0]

    def sb(name, shape, dt):
        t = nc.alloc_sbuf_tensor_at(name, shape, dt, offset=gbase + off[0])
        nb = 1
        for s_ in shape[1:]:
            nb *= s_
        nb *= 4 if dt == F32 else 2
        off[0] += (nb + 31) & ~31
        return t

    A1W = max(16 * TOK, 15 * TOK + TOKP)
    A1 = sb("A1", [128, A1W], F32)
    A2 = sb("A2", [128, 16 * TOKP], BF16)
    A3 = sb("A3", [128, 16 * TOKP], BF16)
    A4 = sb("A4", [128, max(TOKP, 2 * TH, 1032)], F32)
    A5 = sb("A5", [128, max(TOKP, 2 * TH, 1032)], F32)
    WF = sb("WF", [128, 2 * 2048], F32)
    WH = sb("WH", [128, 2 * 2048], BF16)
    WL = sb("WL", [128, 2 * 2048], BF16)
    ST1 = sb("ST1", [128, TOKP + 5], F32)
    ST2 = sb("ST2", [128, TOKP + 5], F32)
    PHS = sb("PHS", [128, TOK], BF16)
    PLS = sb("PLS", [128, TOK], BF16)
    ACOL = sb("ACOL", [128, NCT * DST], F32)
    HIN = sb("HIN", [128, NCT * DST], F32)
    SCO = sb("SCO", [128, max(NCT * DST, 8 * NCT)], F32)
    TK8 = SCO
    ATOT = sb("ATOT", [128, NCT], F32)
    GJ = sb("GJ", [128, NCT], F32)
    M2S = sb("M2S", [128, 8], F32)
    GMS = sb("GMS", [128, 64], F32)
    PK = sb("PK", [128, DCONV * NCT], F32)
    CB = sb("CB", [128, NCT], F32)
    DPC = sb("DPC", [128, NCT], F32)
    DTBC = sb("DTBC", [128, NCT], F32)
    L1C = sb("L1C", [128, KD], F32)
    L2C = sb("L2C", [128, KD], F32)
    FGSC = sb("FGSC", [128, NFI], F32)
    FUSC = sb("FUSC", [128, NFI], F32)
    FDSC = sb("FDSC", [128, KD], F32)
    ONE1 = sb("ONE1", [1, 128], F32)
    ONEC = sb("ONEC", [128, 1], F32)
    EPSC = sb("EPSC", [128, 1], F32)
    COLS = sb("COLS", [128, 44], F32)
    DTRH = sb("DTRH", [128, TOK], BF16)
    DTRL = sb("DTRL", [128, TOK], BF16)
    YCT = sb("YCT", [128, max(TH, NCT * DST)], F32)
    TRM = YCT
    RS = TOKP + 1
    ROWS = sb("ROWS", [1, 3 * RS], F32)
    R_A, R_B, R_C = 0, RS, 2 * RS
    assert off[0] <= avail, f"SBUF over budget: {off[0]} > {avail}"

    PS = [nc.psum_tensor(f"ps{i}", [128, 512], F32).__enter__() for i in range(8)]

    A1_dt = 0
    A1_xe = A1_dt + TOK
    A1_xc = A1_xe + TOKP
    A1_a = A1_xc + TOK
    A1_n0 = A1_a + TOK

    hres = lambda k: A1[:, k * TOK:(k + 1) * TOK]

    def midhi(par):
        return PHS[:, 0:TOK] if par == 0 else DTRH[:, 0:TOK]

    def midlo(par):
        return PLS[:, 0:TOK] if par == 0 else DTRL[:, 0:TOK]

    S.g(lambda e: e.memset(ONE1[:, :], 1.0), w=["one1"], tag="ones1")
    S.g(lambda e: e.memset(ONEC[:, :], 1.0), w=["onec"], tag="onec")
    S.g(lambda e: e.memset(HIN[:, :], 0.0), w=["hin"], tag="hin0")
    S.g(lambda e: e.memset(EPSC[:, :], float(EPS)), w=["epsc"], tag="epsc")
    S.g(lambda e: e.memset(COLS[:, :], 0.0), w=["cols"], tag="cols0")

    def packed(dst_ap, src_ap, n_ct, inner=1):
        def fn(e):
            s2 = ap4(src_ap, 0, [[inner, 128], [128 * inner, n_ct], [1, inner]])
            return e.dma_start(out=dst_ap, in_=s2, allow_slow_non_contiguous=True)
        return fn

    S.dW(packed(PK[:, :], convw, NCT, DCONV), w=["pk"], tag="pk")
    S.dW(packed(CB[:, :], convb, NCT), w=["cb"], tag="cb")
    S.dW(packed(DPC[:, :], dpar, NCT), w=["dpc"], tag="dpc")
    S.dW(packed(DTBC[:, :], dtb, NCT), w=["dtbc"], tag="dtbc")
    S.dW(packed(L1C[:, :], ln1w, KD), w=["l1c"], tag="l1c")
    S.dW(packed(L2C[:, :], ln2w, KD), w=["l2c"], tag="l2c")
    S.dW(packed(FGSC[:, :], fgs, NFI), w=["fgsc"], tag="fgsc")
    S.dW(packed(FUSC[:, :], fus, NFI), w=["fusc"], tag="fusc")
    S.dW(packed(FDSC[:, :], fds, KD), w=["fdsc"], tag="fdsc")
    S.dW(packed(ACOL[:, :], alog, NCT, DST), w=["acol"], tag="alog")
    S.dW(lambda e: e.dma_start(out=GMS[:, :], in_=gmc), w=["gms"], tag="gms")
    S.dW(lambda e: e.dma_start(out=M2S[:, :], in_=m2c), w=["m2s"], tag="m2s")
    S.s(lambda e: e.activation(ACOL[:, :], ACOL[:, :], AF.Exp),
        r=["acol"], w=["acol"], tag="acol_exp")
    S.v(lambda e: e.tensor_scalar(ACOL[:, :], ACOL[:, :], -1.0, None, ALU.mult),
        r=["acol"], w=["acol"], tag="acol_neg")

    schunks = list(range(0, TOKP, 128))
    nj = len(schunks)

    for k in range(KD):
        S.dW(lambda e, k=k: e.dma_start(out=ST2[:, 0:TOKP],
                                        in_=x_in[k * 128:(k + 1) * 128, :]),
             r=["st2f:0", "st2f:1"], w=["st2f:0", "st2f:1"], tag=f"p1.ldx{k}")
        S.s(lambda e: e.activation(ST1[:, 0:TOKP], ST2[:, 0:TOKP], AF.Square),
            r=["st2f:0", "st2f:1", "st1x"], w=["st1x"], tag=f"p1.sq{k}")
        def fmm(e, k=k):
            inst = None
            for j, c0 in enumerate(schunks):
                m = min(128, TOKP - c0)
                e.matmul(PS[6][0:m, j:j + 1], ST2[:, c0:c0 + m], ONEC[:, :],
                         start=(k == 0 and j == 0), stop=(k == KD - 1 and j == nj - 1),
                         skip_group_check=True)
                inst = e.matmul(PS[7][0:m, j:j + 1], ST1[:, c0:c0 + m], ONEC[:, :],
                                start=(k == 0 and j == 0), stop=(k == KD - 1 and j == nj - 1),
                                skip_group_check=True)
            return inst
        S.t(fmm, r=["st2f:0", "st2f:1", "st1x", "onec"], w=["ps6", "ps7"], tag=f"p1.mm{k}")

    def ev_cols(bank, coff, chlist, width, tag):
        def fn(e, bank=bank, coff=coff):
            inst = None
            for j, c0 in enumerate(chlist):
                m = min(128, width - c0)
                inst = e.tensor_copy(COLS[0:m, coff + j:coff + j + 1],
                                     PS[bank][0:m, j:j + 1])
            return inst
        S.v(fn, r=[f"ps{bank}", "cols"], w=["cols"], tag=tag)

    ev_cols(6, 0, schunks, TOKP, "p1.ev1")
    ev_cols(7, 32, schunks, TOKP, "p1.ev2")

    def col2row(dst_off, src_off, n, width, tag):
        def fn(e):
            insts = []
            for j in range(n):
                m = min(128, width - j * 128)
                insts.append(e.dma_start(
                    out=ROWS[0:1, dst_off + j * 128:dst_off + j * 128 + m],
                    in_=COLS[0:m, src_off + j:src_off + j + 1]))
            return insts
        return self_dS(fn, n, tag)

    def self_dS(fn, n, tag):
        return S.emit("gpsimd", fn, ("cols",), ("rows",), dma=True, tag=tag,
                      nmul=n)

    col2row(R_A, 0, nj, TOKP, "p1.ra")
    col2row(R_B, 32, nj, TOKP, "p1.rb")
    S.v(lambda e: e.tensor_scalar(ROWS[0:1, R_A:R_A + TOKP], ROWS[0:1, R_A:R_A + TOKP],
                                  1.0 / D, None, ALU.mult),
        r=["rows"], w=["rows"], tag="p1.mean")
    S.v(lambda e: e.tensor_tensor(ROWS[0:1, R_C:R_C + TOKP], ROWS[0:1, R_A:R_A + TOKP],
                                  ROWS[0:1, R_A:R_A + TOKP], ALU.mult),
        r=["rows"], w=["rows"], tag="p1.m2")
    S.v(lambda e: e.scalar_tensor_tensor(ROWS[0:1, R_B:R_B + TOKP], ROWS[0:1, R_B:R_B + TOKP],
                                         1.0 / D, ROWS[0:1, R_C:R_C + TOKP],
                                         op0=ALU.mult, op1=ALU.subtract),
        r=["rows"], w=["rows"], tag="p1.var")
    S.s(lambda e: e.activation(ROWS[0:1, R_C:R_C + TOKP], ROWS[0:1, R_B:R_B + TOKP],
                               AF.Ln, bias=EPSC[0:1, :]),
        r=["rows", "epsc"], w=["rows"], tag="p1.ln")
    S.s(lambda e: e.activation(ROWS[0:1, R_C:R_C + TOKP], ROWS[0:1, R_C:R_C + TOKP],
                               AF.Exp, scale=-0.5),
        r=["rows"], w=["rows"], tag="p1.s")
    S.v(lambda e: e.tensor_tensor(ROWS[0:1, R_B:R_B + TOKP], ROWS[0:1, R_B:R_B + TOKP],
                                  ROWS[0:1, R_C:R_C + TOKP], ALU.mult),
        r=["rows"], w=["rows"], tag="p1.vs")
    S.v(lambda e: e.tensor_tensor(ROWS[0:1, R_B:R_B + TOKP], ROWS[0:1, R_B:R_B + TOKP],
                                  ROWS[0:1, R_C:R_C + TOKP], ALU.mult),
        r=["rows"], w=["rows"], tag="p1.vss")
    S.s(lambda e: e.activation(ROWS[0:1, R_B:R_B + TOKP], ROWS[0:1, R_B:R_B + TOKP],
                               AF.Ln, bias=EPSC[0:1, :]),
        r=["rows", "epsc"], w=["rows"], tag="p1.ln2")
    S.s(lambda e: e.activation(ROWS[0:1, R_B:R_B + TOKP], ROWS[0:1, R_B:R_B + TOKP],
                               AF.Exp, scale=-0.5),
        r=["rows"], w=["rows"], tag="p1.rmsi")
    S.v(lambda e: e.tensor_tensor(ROWS[0:1, R_B:R_B + TOKP], ROWS[0:1, R_B:R_B + TOKP],
                                  ROWS[0:1, R_C:R_C + TOKP], ALU.mult),
        r=["rows"], w=["rows"], tag="p1.srr")

    def rep_row(row_off, dst, dst_tok, width, tag):
        mch = list(range(0, width, 512))
        for j, c0 in enumerate(mch):
            w_ = min(512, width - c0)
            bank = j % 2
            S.t(lambda e, c0=c0, w_=w_, bank=bank, row_off=row_off: e.matmul(
                    PS[bank][:, 0:w_], ONE1[:, :],
                    ROWS[0:1, row_off + c0:row_off + c0 + w_], start=True, stop=True),
                r=["rows", "one1"], w=[f"ps{bank}"], tag=f"{tag}.{j}")
            S.v(lambda e, c0=c0, w_=w_, bank=bank: e.tensor_copy(
                    dst[:, c0:c0 + w_], PS[bank][:, 0:w_]),
                r=[f"ps{bank}"], w=list(dst_tok), tag=f"{tag}.ev{j}")

    rep_row(R_A, A4, ("a4:0", "a4:1"), TOKP, "p1.repm")
    rep_row(R_B, A5, ("a5:0", "a5:1"), TOKP, "p1.reps")

    for k in range(KD):
        S.dW(lambda e, k=k: e.dma_start(out=ST2[:, 0:TOKP],
                                        in_=x_in[k * 128:(k + 1) * 128, :]),
             r=["st2f:0", "st2f:1"], w=["st2f:0", "st2f:1"], tag=f"p1b.ldx{k}")
        S.v(lambda e: e.tensor_tensor(ST1[:, 0:TOKP], ST2[:, 0:TOKP], A4[:, 0:TOKP],
                                      ALU.subtract),
            r=["st2f:0", "st2f:1", "a4:0", "a4:1", "st1x"], w=["st1x"], tag=f"p1b.sub{k}")
        S.v(lambda e, k=k: e.scalar_tensor_tensor(ST1[:, 0:TOKP], ST1[:, 0:TOKP],
                                                  L1C[:, k:k + 1], A5[:, 0:TOKP],
                                                  op0=ALU.mult, op1=ALU.mult),
            r=["st1x", "l1c", "a5:0", "a5:1"], w=["st1x"], tag=f"p1b.r1{k}")
        S.s(lambda e, k=k: e.activation(A2[:, k * TOKP:(k + 1) * TOKP], ST1[:, 0:TOKP],
                                        AF.Copy),
            r=["st1x"], w=["a2"], tag=f"p1b.hi{k}")
        S.v(lambda e, k=k: e.tensor_tensor(A3[:, k * TOKP:(k + 1) * TOKP], ST1[:, 0:TOKP],
                                           A2[:, k * TOKP:(k + 1) * TOKP], ALU.subtract),
            r=["st1x", "a2"], w=["a3"], tag=f"p1b.lo{k}")

    r1h = lambda k: A2[:, k * TOKP:(k + 1) * TOKP]
    r1l = lambda k: A3[:, k * TOKP:(k + 1) * TOKP]
    r1toks = ["a2", "a3"]

    A5B = A5[:, :].bitcast(BF16)
    psp_items = []

    def add_psp(w_ap, h_ap, l_ap, nrows):
        ncol = w_ap.shape[1]
        for r0 in range(0, nrows, 128):
            for c0 in range(0, ncol, 512):
                psp_items.append((w_ap, h_ap, l_ap, r0, c0, min(512, ncol - c0)))

    add_psp(woutt, wouth_d, woutl_d, DI)
    add_psp(wgp, wgh_d, wgl_d, NMI * 128)
    add_psp(wup, wuh_d, wul_d, NMI * 128)
    add_psp(wdt, wdh_d, wdl_d, MI)
    add_psp(fgp, fgh_d, fgl_d, NFI * 128)
    add_psp(fup, fuh_d, ful_d, NFI * 128)
    add_psp(fdt, fdh_d, fdl_d, FI)
    psp_pos = [0]

    def emit_presplit(k):
        for _ in range(k):
            i = psp_pos[0]
            if i >= len(psp_items):
                return
            psp_pos[0] = i + 1
            w_ap, h_ap, l_ap, r0, c0, w_ = psp_items[i]
            par = i % 2
            fsl = A4[:, par * 516:par * 516 + w_]
            hsl = A5B[:, par * 1032:par * 1032 + w_]
            lsl = A5B[:, par * 1032 + 512:par * 1032 + 512 + w_]
            ftok, btok = f"a4:{par}", f"a5:{par}"
            S.dR(lambda e, w_ap=w_ap, r0=r0, c0=c0, w_=w_, fsl=fsl: e.dma_start(
                    out=fsl, in_=w_ap[r0:r0 + 128, c0:c0 + w_]),
                 r=[ftok], w=[ftok], tag=f"psp.l{i}")
            S.s(lambda e, fsl=fsl, hsl=hsl: e.activation(hsl, fsl, AF.Copy),
                r=[ftok, btok], w=[btok], tag=f"psp.h{i}")
            S.emit("gpsimd", lambda e, fsl=fsl, hsl=hsl, lsl=lsl: e.tensor_tensor(
                    lsl, fsl, hsl, ALU.subtract),
                [ftok, btok], [ftok, btok], tag=f"psp.s{i}")
            S.dS(lambda e, h_ap=h_ap, r0=r0, c0=c0, w_=w_, hsl=hsl: e.dma_start(
                    out=h_ap[r0:r0 + 128, c0:c0 + w_], in_=hsl),
                 r=[btok], w=["psph"], tag=f"psp.sh{i}")
            S.dS(lambda e, l_ap=l_ap, r0=r0, c0=c0, w_=w_, lsl=lsl: e.dma_start(
                    out=l_ap[r0:r0 + 128, c0:c0 + w_], in_=lsl),
                 r=[btok], w=["pspl"], tag=f"psp.sl{i}")

    def load_panel_pcols(wap, pt, ncols_w, par, tag):
        def fn(e, par=par):
            return e.dma_start(out=WF[:, par * 2048:par * 2048 + KD * 128],
                               in_=wap[pt * 128:(pt + 1) * 128, :])
        return S.dW(fn, r=[f"wf:{par}"], w=[f"wf:{par}"], tag=tag)

    def split_panel(par, ncols, tag, lo_eng="vector", hi_eng="scalar"):
        if hi_eng == "scalar":
            S.s(lambda e, par=par, ncols=ncols: e.activation(
                    WH[:, par * 2048:par * 2048 + ncols],
                    WF[:, par * 2048:par * 2048 + ncols], AF.Copy),
                r=[f"wf:{par}", f"whl:{par}"], w=[f"whl:{par}"], tag=tag + ".hi")
        else:
            S.emit(hi_eng, lambda e, par=par, ncols=ncols: e.tensor_copy(
                    WH[:, par * 2048:par * 2048 + ncols],
                    WF[:, par * 2048:par * 2048 + ncols]),
                [f"wf:{par}", f"whl:{par}"], [f"whl:{par}"], tag=tag + ".hi")
        S.emit(lo_eng, lambda e, par=par, ncols=ncols: e.tensor_tensor(
                WL[:, par * 2048:par * 2048 + ncols],
                WF[:, par * 2048:par * 2048 + ncols],
                WH[:, par * 2048:par * 2048 + ncols], ALU.subtract),
            [f"wf:{par}", f"whl:{par}"], [f"whl:{par}", f"wf:{par}"], tag=tag + ".lo")

    def wtile(par, k):
        return (WH[:, par * 2048 + k * 128:par * 2048 + (k + 1) * 128],
                WL[:, par * 2048 + k * 128:par * 2048 + (k + 1) * 128])

    def mm3(e, psum_ap, wh, wl, ah, al, first, last):
        e.matmul(psum_ap, wh, ah, start=first, stop=False)
        e.matmul(psum_ap, wh, al, start=False, stop=False)
        return e.matmul(psum_ap, wl, ah, start=False, stop=last)

    def proj_mm(par, ah, al, atoks, c0, c1, bank, tag, nk=None):
        nk_ = nk or KD
        def fn(e):
            inst = None
            for k in range(nk_):
                wh, wl = wtile(par, k)
                inst = mm3(e, PS[bank][:, 0:c1 - c0], wh, wl,
                           ah(k)[:, c0:c1], al(k)[:, c0:c1], k == 0, k == nk_ - 1)
            return inst
        return S.t(fn, r=[f"whl:{par}"] + atoks, w=[f"ps{bank}"], tag=tag)

    stg = [0]

    def stage():
        stg[0] ^= 1
        return (ST1, ["st1x"]) if stg[0] else (ST2, ["st2f:0", "st2f:1"])

    bankrot = [0]

    def do_proj_tile(kind, pt):
        par = pt % 2
        load_panel_pcols(w1p, pt, KD * 128, par, f"p2.w{pt}")
        split_panel(par, KD * 128, f"p2.s{pt}", lo_eng="gpsimd", hi_eng="gpsimd")
        if kind == "x":
            chunks = [(c0, min(c0 + 512, TOKP)) for c0 in range(0, TOKP, 512)]
        else:
            chunks = [(3 + i * TC, 3 + (i + 1) * TC) for i in range(NTC)]
        for (c0, c1) in chunks:
            bank = bankrot[0] % 4
            bankrot[0] += 1
            proj_mm(par, r1h, r1l, r1toks, c0, c1, bank, f"p2.mm{pt}.{c0}")
            w_ = c1 - c0
            STG, stok = stage()
            S.s(lambda e, bank=bank, STG=STG, w_=w_: e.activation(
                    STG[:, 0:w_], PS[bank][:, 0:w_], AF.Copy),
                r=[f"ps{bank}"] + stok, w=stok, tag=f"p2.ev{pt}.{c0}")
            if kind == "x":
                xi = pt - PT_X0
                S.dS(lambda e, xi=xi, c0=c0, w_=w_, STG=STG: e.dma_start(
                        out=xp_d[xi * 128:(xi + 1) * 128, c0:c0 + w_], in_=STG[:, 0:w_]),
                     r=stok, w=[f"xp:{xi}"], tag=f"p2.sp{pt}.{c0}")
            elif kind == "B":
                bi = pt - PT_B0
                S.dS(lambda e, bi=bi, c0=c0, w_=w_, STG=STG: e.dma_start(
                        out=b_d[bi * 128:(bi + 1) * 128, c0 - 3:c0 - 3 + w_],
                        in_=STG[:, 0:w_]),
                     r=stok, w=[f"bd:{bi}"], tag=f"p2.sp{pt}.{c0}")
            elif kind == "z":
                zi = pt - PT_Z0
                S.dS(lambda e, zi=zi, c0=c0, w_=w_, STG=STG: e.dma_start(
                        out=z_d[zi * 128:(zi + 1) * 128, c0 - 3:c0 - 3 + w_],
                        in_=STG[:, 0:w_]),
                     r=stok, w=[f"zd:{zi}"], tag=f"p2.sp{pt}.{c0}")
            elif kind == "C":
                ci = pt - PT_C0
                def spc(e, ci=ci, c0=c0, w_=w_, STG=STG):
                    dst = ap4(cpl_d, (ci * HPC) * DST * TOK + (c0 - 3),
                              [[DST * TOK, HPC], [TOK, DST], [1, w_]])
                    return e.dma_start(out=dst, in_=STG[:, 0:w_])
                S.dS(spc, r=stok, w=[f"cpl:{ci}"], tag=f"p2.sp{pt}.{c0}")
            elif kind == "dtr":
                S.s(lambda e, c0=c0, w_=w_, STG=STG: e.activation(
                        DTRH[:, c0 - 3:c0 - 3 + w_], STG[:, 0:w_], AF.Copy),
                    r=stok + ["dtr"], w=["dtr"], tag=f"p2.dh.{c0}")
                S.v(lambda e, c0=c0, w_=w_, STG=STG: e.tensor_tensor(
                        DTRL[:, c0 - 3:c0 - 3 + w_], STG[:, 0:w_],
                        DTRH[:, c0 - 3:c0 - 3 + w_], ALU.subtract),
                    r=stok + ["dtr"], w=["dtr"], tag=f"p2.dl.{c0}")

    for i in range(NXT):
        do_proj_tile("x", PT_X0 + i)
    do_proj_tile("dtr", PT_DT)
    for i in range(NXT):
        do_proj_tile("B", PT_B0 + i)

    nhalf = NCT // 2
    for half in range(2):
        par = half
        def ldw(e, half=half):
            sap = ap4(dtwt, half * nhalf * 128, [[DI, 128], [1, nhalf * 128]])
            return e.dma_start(out=WF[:, half * 2048:half * 2048 + nhalf * 128],
                               in_=sap)
        S.dW(ldw, r=[f"wf:{half}"], w=[f"wf:{half}"], tag=f"p2d.w{half}")
        split_panel(par, nhalf * 128, f"p2d.s{half}")
        for j in range(nhalf):
            ct = half * nhalf + j
            for i in range(NTC):
                bank = 4 + ((ct * NTC + i) % 2)
                def fmm(e, par=par, j=j, bank=bank, i=i):
                    wh, wl = wtile(par, j)
                    return mm3(e, PS[bank][:, 0:TC], wh, wl,
                               DTRH[:, i * TC:(i + 1) * TC],
                               DTRL[:, i * TC:(i + 1) * TC], True, True)
                S.t(fmm, r=[f"whl:{par}", "dtr"], w=[f"ps{bank}"],
                    tag=f"p2d.mm{ct}.{i}")
                SA, atok = stage()
                SB, btok = stage()
                atok0, btok0 = atok[0], btok[0]
                S.v(lambda e, ct=ct, bank=bank, SA=SA: e.tensor_scalar(
                        SA[:, 0:TC], PS[bank][:, 0:TC], DTBC[:, ct:ct + 1], None,
                        ALU.add),
                    r=[f"ps{bank}", "dtbc"] + atok, w=atok, tag=f"p2d.t{ct}.{i}")
                S.s(lambda e, SA=SA, SB=SB: e.activation(SB[:, 0:TC], SA[:, 0:TC],
                                                         AF.Abs),
                    r=atok + btok, w=btok, tag=f"p2d.ab{ct}.{i}")
                S.s(lambda e, SB=SB: e.activation(SB[:, 0:TC], SB[:, 0:TC], AF.Exp,
                                                  scale=-1.0),
                    r=btok, w=btok, tag=f"p2d.ex{ct}.{i}")
                S.s(lambda e, SB=SB: e.activation(SB[:, 0:TC], SB[:, 0:TC], AF.Ln,
                                                  bias=ONEC[:, 0:1]),
                    r=btok + ["onec"], w=btok, tag=f"p2d.ln{ct}.{i}")
                S.s(lambda e, SA=SA: e.activation(SA[:, 0:TC], SA[:, 0:TC], AF.Relu),
                    r=atok, w=atok, tag=f"p2d.rl{ct}.{i}")
                S.v(lambda e, SA=SA, SB=SB: e.tensor_tensor(
                        SA[:, 0:TC], SA[:, 0:TC], SB[:, 0:TC], ALU.add),
                    r=atok + btok, w=atok, tag=f"p2d.sp{ct}.{i}")
                S.dS(lambda e, ct=ct, i=i, SA=SA: e.dma_start(
                        out=dt_d[ct * 128:(ct + 1) * 128, i * TC:(i + 1) * TC],
                        in_=SA[:, 0:TC]),
                     r=atok, w=[f"dtd:{ct}"], tag=f"p2d.spl{ct}.{i}")

    def scan_ct(ct):
        hxb0 = (ct * HPC) // REP
        xpt = (hxb0 * DST) // 128
        S.dW(lambda e, ct=ct: e.dma_start(out=A1[:, A1_dt:A1_dt + TOK],
                                          in_=dt_d[ct * 128:(ct + 1) * 128, :]),
             r=[f"dtd:{ct}", "a1dt"], w=["a1dt"], tag=f"sc.dt{ct}")
        PB = REP * DST
        def ldxe(e, hxb0=hxb0, PB=PB):
            insts = []
            for hx in range(NHXB):
                sap = ap4(xp_d, (hxb0 + hx) * DST * TOKP,
                          [[0, REP], [TOKP, DST], [1, TOKP]])
                insts.append(e.dma_start(
                    out=A1[hx * PB:(hx + 1) * PB, A1_xe:A1_xe + TOKP], in_=sap))
            return insts
        S.dW(ldxe, r=[f"xp:{xpt}", "a1xe"], w=["a1xe"], tag=f"sc.xe{ct}", nmul=NHXB)
        S.v(lambda e, ct=ct: e.tensor_scalar(A1[:, A1_xc:A1_xc + TOK],
                                             A1[:, A1_xe:A1_xe + TOK],
                                             PK[:, ct * DCONV:ct * DCONV + 1], None,
                                             ALU.mult),
            r=["a1xe", "pk", "a1xc"], w=["a1xc"], tag=f"sc.c0.{ct}")
        for kk in range(1, DCONV):
            S.v(lambda e, ct=ct, kk=kk: e.scalar_tensor_tensor(
                    A1[:, A1_xc:A1_xc + TOK], A1[:, A1_xe + kk:A1_xe + kk + TOK],
                    PK[:, ct * DCONV + kk:ct * DCONV + kk + 1],
                    A1[:, A1_xc:A1_xc + TOK], op0=ALU.mult, op1=ALU.add),
                r=["a1xe", "pk", "a1xc"], w=["a1xc"], tag=f"sc.c{kk}.{ct}")
        S.s(lambda e, ct=ct: e.activation(A1[:, A1_xc:A1_xc + TOK],
                                          A1[:, A1_xc:A1_xc + TOK], AF.Silu,
                                          bias=CB[:, ct:ct + 1]),
            r=["a1xc", "cb"], w=["a1xc"], tag=f"sc.silu{ct}")
        S.v(lambda e, ct=ct: e.tensor_scalar(A1[:, A1_xe:A1_xe + TOK],
                                             A1[:, A1_xc:A1_xc + TOK],
                                             DPC[:, ct:ct + 1], None, ALU.mult),
            r=["a1xc", "dpc", "a1xe"], w=["a1xe"], tag=f"sc.xcd{ct}")
        S.dS(lambda e, ct=ct: e.dma_start(out=xcd_d[ct * 128:(ct + 1) * 128, :],
                                          in_=A1[:, A1_xe:A1_xe + TOK]),
             r=["a1xe"], w=[f"xcd:{ct}"], tag=f"sc.spxcd{ct}")
        S.v(lambda e: e.tensor_tensor(A1[:, A1_xc:A1_xc + TOK], A1[:, A1_dt:A1_dt + TOK],
                                      A1[:, A1_xc:A1_xc + TOK], ALU.mult),
            r=["a1dt", "a1xc"], w=["a1xc"], tag=f"sc.dtx{ct}")
        S.v(lambda e: e.tensor_tensor_scan(A1[:, A1_a:A1_a + TOK], A1[:, A1_dt:A1_dt + TOK],
                                           A1[:, A1_dt:A1_dt + TOK], initial=0.0,
                                           op0=ALU.add, op1=ALU.bypass),
            r=["a1dt", "a1a"], w=["a1a"], tag=f"sc.cum{ct}")
        S.dS(lambda e, ct=ct: e.dma_start(out=a_d[ct * 128:(ct + 1) * 128, :],
                                          in_=A1[:, A1_a:A1_a + TOK]),
             r=["a1a"], w=[f"ad:{ct}"], tag=f"sc.spa{ct}")
        S.v(lambda e, ct=ct: e.tensor_copy(ATOT[:, ct:ct + 1],
                                           A1[:, A1_a + TOK - 1:A1_a + TOK]),
            r=["a1a", "atot"], w=["atot"], tag=f"sc.atot{ct}")
        PB = REP * DST
        for np_ in range(DST // 2):
            n0 = 2 * np_
            par = np_ % 3
            base = A1_n0 + par * 4 * TOK
            bA, bB = base, base + 2 * TOK
            bH = bA
            ntok = f"a1n:{par}"
            def dA2(e, ct=ct, n0=n0, bA=bA):
                e.activation(A1[:, bA:bA + TOK], A1[:, A1_dt:A1_dt + TOK], AF.Exp,
                             scale=ACOL[:, ct * DST + n0:ct * DST + n0 + 1])
                return e.activation(A1[:, bA + TOK:bA + 2 * TOK],
                                    A1[:, A1_dt:A1_dt + TOK], AF.Exp,
                                    scale=ACOL[:, ct * DST + n0 + 1:ct * DST + n0 + 2])
            S.s(dA2, r=["a1dt", "acol", ntok], w=[ntok], tag=f"sc.dA{ct}.{n0}")
            def ldb(e, n0=n0, hxb0=hxb0, bB=bB):
                insts = []
                for hx in range(NHXB):
                    sap = ap4(b_d, ((hxb0 + hx) * DST + n0) * TOK,
                              [[0, PB], [TOK, 2], [1, TOK]])
                    insts.append(e.dma_start(
                        out=A1[hx * PB:(hx + 1) * PB, bB:bB + 2 * TOK], in_=sap))
                return insts
            S.emit("gpsimd", ldb, [f"bd:{(hxb0 * DST) // 128}", ntok], [ntok],
                   dma=True, tag=f"sc.B{ct}.{n0}", nmul=NHXB)
            def dbx2(e, bB=bB):
                i0 = A1[:, A1_xc:A1_xc + TOK].unsqueeze(1).broadcast_to((128, 2, TOK))
                i1 = A1[:, bB:bB + 2 * TOK].rearrange("p (n t) -> p n t", n=2)
                o = A1[:, bB:bB + 2 * TOK].rearrange("p (n t) -> p n t", n=2)
                return e.tensor_tensor(o, i0, i1, ALU.mult)
            S.v(dbx2, r=["a1xc", ntok], w=[ntok], tag=f"sc.dBx{ct}.{n0}")
            for h2 in range(2):
                S.v(lambda e, bA=bA, bB=bB, bH=bH, h2=h2: e.tensor_tensor_scan(
                        A1[:, bH + h2 * TOK:bH + (h2 + 1) * TOK],
                        A1[:, bA + h2 * TOK:bA + (h2 + 1) * TOK],
                        A1[:, bB + h2 * TOK:bB + (h2 + 1) * TOK],
                        initial=0.0, op0=ALU.mult, op1=ALU.add),
                    r=[ntok], w=[ntok], tag=f"sc.tts{ct}.{n0 + h2}")
            def sph2(e, ct=ct, n0=n0, bH=bH):
                dst = ap4(h_d, ct * 128 * DST * TOK + n0 * TOK,
                          [[DST * TOK, 128], [TOK, 2], [1, TOK]])
                return e.dma_start(out=dst, in_=A1[:, bH:bH + 2 * TOK])
            S.dS(sph2, r=[ntok], w=[f"hd:{ct}"], tag=f"sc.sph{ct}.{n0}")
            def sco2(e, ct=ct, n0=n0, bH=bH):
                e.tensor_copy(SCO[:, ct * DST + n0:ct * DST + n0 + 1],
                              A1[:, bH + TOK - 1:bH + TOK])
                return e.tensor_copy(SCO[:, ct * DST + n0 + 1:ct * DST + n0 + 2],
                                     A1[:, bH + 2 * TOK - 1:bH + 2 * TOK])
            S.g(sco2, r=[ntok, "sco"], w=["sco"], tag=f"sc.sco{ct}.{n0}")

    zc_tiles = [("z", PT_Z0 + i) for i in range(DI // 128)] + \
               [("C", PT_C0 + i) for i in range(DI // 128)]
    zi = iter(zc_tiles)
    nper = (len(psp_items) + NCT - 1) // NCT
    for ct in range(NCT):
        for _ in range(2):
            nt = next(zi, None)
            if nt is not None:
                do_proj_tile(*nt)
        scan_ct(ct)
        emit_presplit(nper)
    for nt in zi:
        do_proj_tile(*nt)
    emit_presplit(len(psp_items))

    S.dS(lambda e: e.dma_start(
            out=ap4(cc_in.ap(), 0, [[NCT * DST, 128], [1, NCT * DST]]),
            in_=SCO[:, :]),
         r=["sco"], w=["ccin"], tag="p3.spS")
    S.dS(lambda e: e.dma_start(
            out=ap4(cc_in.ap(), DI * DST, [[NCT, 128], [1, NCT]]),
            in_=ATOT[:, :]),
         r=["atot"], w=["ccin"], tag="p3.spT")
    S.emit("gpsimd", lambda e: e.collective_compute(
            "AllGather", ALU.bypass, replica_groups=[list(range(NCORES))],
            ins=[cc_in.ap().opt()], outs=[cc_out.ap().opt()]),
        r=["ccin"], w=["ccout"], tag="p3.ag", force=True)
    for j in range(NCORES):
        def ldt(e, j=j):
            src = ap4(cc_out.ap(), j * CCN + DI * DST, [[NCT, 128], [1, NCT]])
            return e.dma_start(out=TK8[:, j * NCT:(j + 1) * NCT], in_=src)
        S.dR(ldt, r=["ccout", "sco"], w=["sco"], tag=f"p3.t{j}")
    for j in range(NCORES):
        S.v(lambda e, j=j: e.tensor_scalar(GJ[:, :], TK8[:, 0:NCT],
                                           GMS[:, j * 8:j * 8 + 1], None, ALU.mult),
            r=["sco", "gms", "gj"], w=["gj"], tag=f"p3.g0.{j}")
        for k in range(1, NCORES):
            S.v(lambda e, j=j, k=k: e.scalar_tensor_tensor(
                    GJ[:, :], TK8[:, k * NCT:(k + 1) * NCT],
                    GMS[:, j * 8 + k:j * 8 + k + 1], GJ[:, :],
                    op0=ALU.mult, op1=ALU.add),
                r=["sco", "gms", "gj"], w=["gj"], tag=f"p3.g{k}.{j}")
        S.v(lambda e: e.tensor_tensor(
                TRM[:, :].rearrange("p (c n) -> p c n", n=DST),
                ACOL[:, :].rearrange("p (c n) -> p c n", n=DST),
                GJ[:, :].unsqueeze(-1).broadcast_to((128, NCT, DST)), ALU.mult),
            r=["acol", "gj", "yct"], w=["yct"], tag=f"p3.arg{j}")
        S.s(lambda e: e.activation(TRM[:, :], TRM[:, :], AF.Exp),
            r=["yct"], w=["yct"], tag=f"p3.exp{j}")
        def lds(e, j=j):
            src = ap4(cc_out.ap(), j * CCN, [[NCT * DST, 128], [1, NCT * DST]])
            return e.dma_start(out=ST1[:, 0:NCT * DST], in_=src)
        S.dR(lds, r=["ccout", "st1x"], w=["st1x"], tag=f"p3.s{j}")
        S.v(lambda e, j=j: e.scalar_tensor_tensor(
                TRM[:, :], TRM[:, :], M2S[:, j:j + 1], ST1[:, 0:NCT * DST],
                op0=ALU.mult, op1=ALU.mult),
            r=["yct", "m2s", "st1x"], w=["yct"], tag=f"p3.msk{j}")
        S.v(lambda e: e.tensor_tensor(HIN[:, :], HIN[:, :], TRM[:, :], ALU.add),
            r=["hin", "yct"], w=["hin"], tag=f"p3.acc{j}")

    PGS = min(6, cfg.get("PGS", 6))
    npg = (KD + PGS - 1) // PGS

    def ldwout(e, kct, par):
        sap = ap4(woutt, kct * 128 * D, [[D, 128], [1, D]])
        return e.dma_start(out=WF[:, par * 2048:par * 2048 + D], in_=sap)

    def p4_wload(kct, th, pg):
        par = kct % 2
        S.dW(lambda e, kct=kct, par=par: e.dma_start(
                out=WH[:, par * 2048:par * 2048 + D],
                in_=wouth_d[kct * 128:(kct + 1) * 128, :]),
             r=["psph", f"whl:{par}"], w=[f"whl:{par}"], tag=f"p4.rwh{kct}.{pg}.{th}")
        S.dW(lambda e, kct=kct, par=par: e.dma_start(
                out=WL[:, par * 2048:par * 2048 + D],
                in_=woutl_d[kct * 128:(kct + 1) * 128, :]),
             r=["pspl", f"whl:{par}"], w=[f"whl:{par}"], tag=f"p4.rwl{kct}.{pg}.{th}")

    def p4_mm(kct, th, pg):
        par = kct % 2
        p_lo = pg * PGS
        p_n = min(PGS, KD - p_lo)
        def fmm(e, kct=kct, par=par, p_lo=p_lo, p_n=p_n):
            inst = None
            for pi in range(p_n):
                p = p_lo + pi
                wh = WH[:, par * 2048 + p * 128:par * 2048 + (p + 1) * 128]
                wl = WL[:, par * 2048 + p * 128:par * 2048 + (p + 1) * 128]
                inst = mm3(e, PS[pi][:, 0:TH], wh, wl,
                           A2[:, kct * TH:(kct + 1) * TH],
                           A3[:, kct * TH:(kct + 1) * TH],
                           kct == 0, kct == NCT - 1)
            return inst
        S.t(fmm, r=[f"whl:{par}", "a2", "a3"],
            w=[f"ps{pi}" for pi in range(p_n)], tag=f"p4.mm{pg}.{kct}.{th}")

    def p4_evac(th, pg):
        t0 = th * TH
        p_lo = pg * PGS
        p_n = min(PGS, KD - p_lo)
        for pi in range(p_n):
            p = p_lo + pi
            S.dR(lambda e, p=p, t0=t0: e.dma_start(
                    out=ST1[:, 0:TH],
                    in_=x_in[p * 128:(p + 1) * 128, 3 + t0:3 + t0 + TH]),
                 r=["st1x"], w=["st1x"], tag=f"p4.x{p}.{th}")
            S.v(lambda e: e.tensor_tensor(ST2[:, 0:TH], ST1[:, 0:TH], PS[6][:, 0:TH],
                                          ALU.subtract),
                r=["st1x", "ps6", "st2f:0", "st2f:1"], w=["st2f:0", "st2f:1"], tag=f"p4.sub{p}.{th}")
            S.v(lambda e: e.tensor_tensor(ST2[:, 0:TH], ST2[:, 0:TH], PS[7][:, 0:TH],
                                          ALU.mult),
                r=["st2f:0", "st2f:1", "ps7"], w=["st2f:0", "st2f:1"], tag=f"p4.lnx{p}.{th}")
            S.v(lambda e, pi=pi, p=p, t0=t0: e.tensor_tensor(
                    hres(p)[:, t0:t0 + TH], PS[pi][:, 0:TH], ST2[:, 0:TH], ALU.add),
                r=[f"ps{pi}", "st2f:0", "st2f:1", f"hr:{p}"], w=[f"hr:{p}"], tag=f"p4.hres{p}.{th}")

    for th in range(NTH):
        t0 = th * TH
        S.t(lambda e, t0=t0: e.matmul(PS[6][:, 0:TH], ONE1[:, :],
                                      ROWS[0:1, R_A + 3 + t0:R_A + 3 + t0 + TH],
                                      start=True, stop=True),
            r=["rows", "one1", "ps6"], w=["ps6"], tag=f"p4.repm.{th}")
        S.t(lambda e, t0=t0: e.matmul(PS[7][:, 0:TH], ONE1[:, :],
                                      ROWS[0:1, R_C + 3 + t0:R_C + 3 + t0 + TH],
                                      start=True, stop=True),
            r=["rows", "one1", "ps7"], w=["ps7"], tag=f"p4.reps.{th}")
        for ct in range(NCT):
            p4_wload(ct, th, 0)
            S.dR(lambda e, ct=ct, t0=t0: e.dma_start(
                    out=YCT[:, 0:TH], in_=xcd_d[ct * 128:(ct + 1) * 128, t0:t0 + TH]),
                 r=[f"xcd:{ct}", "yct"], w=["yct"], tag=f"p4.xcd{ct}.{th}")
            S.dR(lambda e, ct=ct, t0=t0: e.dma_start(
                    out=ST1[:, 0:TH], in_=a_d[ct * 128:(ct + 1) * 128, t0:t0 + TH]),
                 r=[f"ad:{ct}", "st1x"], w=["st1x"], tag=f"p4.a{ct}.{th}")
            hslots = [A4[:, 0:TH], A4[:, TH:2 * TH]]
            cslots = [A5[:, 0:TH], A5[:, TH:2 * TH]]
            htoks = ["a4:0", "a4:1"]
            ctoks = ["a5:0", "a5:1"]
            if TOK // 2 >= TH:
                hslots += [PHS[:, :].bitcast(F32)[:, 0:TH],
                           DTRH[:, :].bitcast(F32)[:, 0:TH]]
                cslots += [PLS[:, :].bitcast(F32)[:, 0:TH],
                           DTRL[:, :].bitcast(F32)[:, 0:TH]]
                htoks += ["ph:0", "ph:1"]
                ctoks += ["ph:0", "ph:1"]
            npar4 = len(hslots)
            for n in range(DST):
                par = n % npar4
                htok, ctok = htoks[par], ctoks[par]
                hsl = hslots[par]
                csl = cslots[par]
                def ldh(e, ct=ct, n=n, t0=t0, hsl=hsl):
                    sap = ap4(h_d, ct * 128 * DST * TOK + n * TOK + t0,
                              [[DST * TOK, 128], [1, TH]])
                    return e.dma_start(out=hsl, in_=sap)
                S.dW(ldh, r=[f"hd:{ct}", htok], w=[htok], tag=f"p4.h{ct}.{n}.{th}")
                def ldc(e, ct=ct, n=n, t0=t0, csl=csl):
                    sap = ap4(cpl_d, (ct * HPC) * DST * TOK + n * TOK + t0,
                              [[DST * TOK, HPC], [0, DST], [1, TH]])
                    return e.dma_start(out=csl, in_=sap)
                S.dW(ldc, r=[f"cpl:{ct}", ctok], w=[ctok], tag=f"p4.c{ct}.{n}.{th}")
                fof = (n % 2) * TH if 2 * TH <= ST2.shape[-1] else 0
                ftok = f"st2f:{n % 2}" if 2 * TH <= ST2.shape[-1] else "st2f:0"
                S.s(lambda e, ct=ct, n=n, fof=fof: e.activation(
                        ST2[:, fof:fof + TH], ST1[:, 0:TH], AF.Exp,
                        scale=ACOL[:, ct * DST + n:ct * DST + n + 1]),
                    r=["st1x", "acol", ftok], w=[ftok], tag=f"p4.f{ct}.{n}.{th}")
                S.v(lambda e, ct=ct, n=n, hsl=hsl, fof=fof: e.scalar_tensor_tensor(
                        hsl, ST2[:, fof:fof + TH], HIN[:, ct * DST + n:ct * DST + n + 1],
                        hsl, op0=ALU.mult, op1=ALU.add),
                    r=[ftok, "hin", htok], w=[htok], tag=f"p4.hf{ct}.{n}.{th}")
                S.v(lambda e, hsl=hsl, csl=csl: e.tensor_tensor(
                        csl, hsl, csl, ALU.mult),
                    r=[htok, ctok], w=[ctok], tag=f"p4.ch{ct}.{n}.{th}")
                S.v(lambda e, csl=csl: e.tensor_tensor(
                        YCT[:, 0:TH], YCT[:, 0:TH], csl, ALU.add),
                    r=["yct", ctok], w=["yct"], tag=f"p4.y{ct}.{n}.{th}")
            S.dR(lambda e, ct=ct, t0=t0: e.dma_start(
                    out=ST1[:, 0:TH], in_=z_d[ct * 128:(ct + 1) * 128, t0:t0 + TH]),
                 r=[f"zd:{ct}", "st1x"], w=["st1x"], tag=f"p4.z{ct}.{th}")
            S.s(lambda e: e.activation(ST1[:, 0:TH], ST1[:, 0:TH], AF.Silu),
                r=["st1x"], w=["st1x"], tag=f"p4.sz{ct}.{th}")
            S.v(lambda e: e.tensor_tensor(YCT[:, 0:TH], YCT[:, 0:TH], ST1[:, 0:TH],
                                          ALU.mult),
                r=["yct", "st1x"], w=["yct"], tag=f"p4.mul{ct}.{th}")
            S.s(lambda e, ct=ct: e.activation(A2[:, ct * TH:(ct + 1) * TH], YCT[:, 0:TH],
                                              AF.Copy),
                r=["yct"], w=["a2"], tag=f"p4.hi{ct}.{th}")
            S.v(lambda e, ct=ct: e.tensor_tensor(A3[:, ct * TH:(ct + 1) * TH],
                                                 YCT[:, 0:TH],
                                                 A2[:, ct * TH:(ct + 1) * TH],
                                                 ALU.subtract),
                r=["yct", "a2"], w=["a3"], tag=f"p4.lo{ct}.{th}")
            p4_mm(ct, th, 0)
        p4_evac(th, 0)
        for pg in range(1, npg):
            for kct in range(NCT):
                p4_wload(kct, th, pg)
                p4_mm(kct, th, pg)
            p4_evac(th, pg)

    sch2 = list(range(0, TOK, 128))
    nj2 = len(sch2)
    for k in range(KD):
        S.s(lambda e, k=k: e.activation(ST1[:, 0:TOK], hres(k)[:, 0:TOK], AF.Square),
            r=[f"hr:{k}", "st1x"], w=["st1x"], tag=f"p5.sq{k}")
        def fmm(e, k=k):
            inst = None
            for j, c0 in enumerate(sch2):
                m = min(128, TOK - c0)
                inst = e.matmul(PS[6][0:m, j:j + 1], ST1[:, c0:c0 + m], ONEC[:, :],
                                start=(k == 0 and j == 0), stop=(k == KD - 1 and j == nj2 - 1),
                                skip_group_check=True)
            return inst
        S.t(fmm, r=["st1x", "onec", "ps6"], w=["ps6"], tag=f"p5.mm{k}")
    ev_cols(6, 0, sch2, TOK, "p5.ev")
    col2row(R_B, 0, nj2, TOK, "p5.r")
    S.s(lambda e: e.activation(ROWS[0:1, R_B:R_B + TOK], ROWS[0:1, R_B:R_B + TOK],
                               AF.Ln, scale=1.0 / D, bias=EPSC[0:1, :]),
        r=["rows", "epsc"], w=["rows"], tag="p5.ln")
    S.s(lambda e: e.activation(ROWS[0:1, R_B:R_B + TOK], ROWS[0:1, R_B:R_B + TOK],
                               AF.Exp, scale=-0.5),
        r=["rows"], w=["rows"], tag="p5.rsq")
    for j in range(NTC):
        S.t(lambda e, j=j: e.matmul(PS[j][:, 0:TC], ONE1[:, :],
                                    ROWS[0:1, R_B + j * TC:R_B + (j + 1) * TC],
                                    start=True, stop=True),
            r=["rows", "one1", f"ps{j}"], w=[f"ps{j}"], tag=f"p5.rep{j}")
        S.v(lambda e, j=j: e.tensor_copy(A4[:, j * TC:(j + 1) * TC], PS[j][:, 0:TC]),
            r=[f"ps{j}", "a4:0", "a4:1"], w=["a4:0", "a4:1"], tag=f"p5.repev{j}")
    for k in range(KD):
        S.v(lambda e, k=k: e.scalar_tensor_tensor(ST2[:, 0:TOK], hres(k)[:, 0:TOK],
                                                  L2C[:, k:k + 1], A4[:, 0:TOK],
                                                  op0=ALU.mult, op1=ALU.mult),
            r=[f"hr:{k}", "l2c", "a4:0", "a4:1", "st2f:0", "st2f:1"], w=["st2f:0", "st2f:1"], tag=f"p5.r2{k}")
        S.s(lambda e, k=k: e.activation(A2[:, k * TOK:(k + 1) * TOK], ST2[:, 0:TOK],
                                        AF.Copy),
            r=["st2f:0", "st2f:1"], w=["a2"], tag=f"p5.hi{k}")
        S.v(lambda e, k=k: e.tensor_tensor(A3[:, k * TOK:(k + 1) * TOK], ST2[:, 0:TOK],
                                           A2[:, k * TOK:(k + 1) * TOK], ALU.subtract),
            r=["st2f:0", "st2f:1", "a2"], w=["a3"], tag=f"p5.lo{k}")
    r2h = lambda k: A2[:, k * TOK:(k + 1) * TOK]
    r2l = lambda k: A3[:, k * TOK:(k + 1) * TOK]
    r2toks = ["a2", "a3"]

    def gup_phase(wgh, wgl, wuh, wul, NP, spill_h, spill_l, sh_tok, sl_tok,
                  scaled, gsc, usc, pref):
        def ldhl(par, h_ap, l_ap, i, tag):
            S.dW(lambda e, par=par, h_ap=h_ap, i=i: e.dma_start(
                    out=WH[:, par * 2048:par * 2048 + KD * 128],
                    in_=h_ap[i * 128:(i + 1) * 128, :]),
                 r=["psph", f"whl:{par}"], w=[f"whl:{par}"], tag=tag + ".h")
            S.dW(lambda e, par=par, l_ap=l_ap, i=i: e.dma_start(
                    out=WL[:, par * 2048:par * 2048 + KD * 128],
                    in_=l_ap[i * 128:(i + 1) * 128, :]),
                 r=["pspl", f"whl:{par}"], w=[f"whl:{par}"], tag=tag + ".l")
        for i in range(NP):
            par = 0
            ldhl(par, wgh, wgl, i, f"{pref}.wg{i}")
            for tc in range(NTC):
                bank = 4 * (i % 2)
                proj_mm(par, r2h, r2l, r2toks, tc * TC, (tc + 1) * TC, bank,
                        f"{pref}.gmm{i}.{tc}")
                if scaled:
                    S.s(lambda e, bank=bank, tc=tc, i=i: e.activation(
                            ST1[:, tc * TC:(tc + 1) * TC], PS[bank][:, 0:TC], AF.Silu,
                            scale=gsc[:, i:i + 1]),
                        r=[f"ps{bank}", "fgsc", "st1x"], w=["st1x"],
                        tag=f"{pref}.si{i}.{tc}")
                else:
                    S.s(lambda e, bank=bank, tc=tc: e.activation(
                            ST1[:, tc * TC:(tc + 1) * TC], PS[bank][:, 0:TC], AF.Silu),
                        r=[f"ps{bank}", "st1x"], w=["st1x"], tag=f"{pref}.si{i}.{tc}")
            par = 1
            ldhl(par, wuh, wul, i, f"{pref}.wu{i}")
            for tc in range(NTC):
                bank = 4 * (i % 2) + 2
                proj_mm(par, r2h, r2l, r2toks, tc * TC, (tc + 1) * TC, bank,
                        f"{pref}.umm{i}.{tc}")
                if scaled:
                    S.v(lambda e, bank=bank, tc=tc, i=i: e.scalar_tensor_tensor(
                            ST2[:, tc * TC:(tc + 1) * TC], PS[bank][:, 0:TC],
                            usc[:, i:i + 1], ST1[:, tc * TC:(tc + 1) * TC],
                            op0=ALU.mult, op1=ALU.mult),
                        r=[f"ps{bank}", "fusc", "st1x", "st2f:0", "st2f:1"], w=["st2f:0", "st2f:1"],
                        tag=f"{pref}.pr{i}.{tc}")
                else:
                    S.v(lambda e, bank=bank, tc=tc: e.tensor_tensor(
                            ST2[:, tc * TC:(tc + 1) * TC], PS[bank][:, 0:TC],
                            ST1[:, tc * TC:(tc + 1) * TC], ALU.mult),
                        r=[f"ps{bank}", "st1x", "st2f:0", "st2f:1"], w=["st2f:0", "st2f:1"],
                        tag=f"{pref}.pr{i}.{tc}")
            ppar = i % 2
            ptok = f"ph:{ppar}"
            S.s(lambda e, ppar=ppar: e.activation(midhi(ppar), ST2[:, 0:TOK], AF.Copy),
                r=["st2f:0", "st2f:1", ptok], w=[ptok], tag=f"{pref}.phi{i}")
            S.v(lambda e, ppar=ppar: e.tensor_tensor(midlo(ppar), ST2[:, 0:TOK],
                                                     midhi(ppar), ALU.subtract),
                r=["st2f:0", "st2f:1", ptok], w=[ptok], tag=f"{pref}.plo{i}")
            S.dS(lambda e, i=i, ppar=ppar: e.dma_start(
                    out=spill_h[i * 128:(i + 1) * 128, :], in_=midhi(ppar)),
                 r=[ptok], w=[f"{sh_tok}:{i}"], tag=f"{pref}.sph{i}")
            S.dS(lambda e, i=i, ppar=ppar: e.dma_start(
                    out=spill_l[i * 128:(i + 1) * 128, :], in_=midlo(ppar)),
                 r=[ptok], w=[f"{sl_tok}:{i}"], tag=f"{pref}.spl{i}")

    gup_phase(wgh_d, wgl_d, wuh_d, wul_d, NMI, midh_d, midl_d, "mh", "ml", False, None, None, "p6")

    for k in range(KD):
        S.dR(lambda e, k=k: e.dma_start(out=ST1[:, 0:TOK],
                                        in_=x_in[k * 128:(k + 1) * 128, 3:3 + TOK]),
             r=["st1x"], w=["st1x"], tag=f"p6.x{k}")
        S.v(lambda e, k=k: e.tensor_tensor(hres(k)[:, 0:TOK], hres(k)[:, 0:TOK],
                                           ST1[:, 0:TOK], ALU.add),
            r=[f"hr:{k}", "st1x"], w=[f"hr:{k}"], tag=f"p6.hx{k}")

    def down_phase(wh_spill, wl_spill, NP, mh_tok, ml_tok, mh_dram, ml_dram,
                   evac, pref):
        PGS2 = min(8, cfg.get("PGS", 8))
        npg = (KD + PGS2 - 1) // PGS2
        for pg in range(npg):
            p_lo = pg * PGS2
            p_n = min(PGS2, KD - p_lo)
            for tc in range(NTC):
                for i in range(NP):
                    par = i % 2
                    S.dW(lambda e, i=i, par=par: e.dma_start(
                            out=WH[:, par * 2048:par * 2048 + D],
                            in_=wh_spill[i * 128:(i + 1) * 128, :]),
                         r=["psph", f"whl:{par}"], w=[f"whl:{par}"],
                         tag=f"{pref}.rwh{i}.{pg}.{tc}")
                    S.dW(lambda e, i=i, par=par: e.dma_start(
                            out=WL[:, par * 2048:par * 2048 + D],
                            in_=wl_spill[i * 128:(i + 1) * 128, :]),
                         r=["pspl", f"whl:{par}"], w=[f"whl:{par}"],
                         tag=f"{pref}.rwl{i}.{pg}.{tc}")
                    ppar = i % 2
                    ptok = f"ph:{ppar}"
                    S.dR(lambda e, i=i, tc=tc, ppar=ppar: e.dma_start(
                            out=midhi(ppar)[:, 0:TC],
                            in_=mh_dram[i * 128:(i + 1) * 128, tc * TC:(tc + 1) * TC]),
                         r=[f"{mh_tok}:{i}", ptok], w=[ptok],
                         tag=f"{pref}.rmh{i}.{pg}.{tc}")
                    S.dR(lambda e, i=i, tc=tc, ppar=ppar: e.dma_start(
                            out=midlo(ppar)[:, 0:TC],
                            in_=ml_dram[i * 128:(i + 1) * 128, tc * TC:(tc + 1) * TC]),
                         r=[f"{ml_tok}:{i}", ptok], w=[ptok],
                         tag=f"{pref}.rml{i}.{pg}.{tc}")
                    def fmm(e, i=i, par=par, ppar=ppar, p_lo=p_lo, p_n=p_n, NP=NP):
                        inst = None
                        for pi in range(p_n):
                            p = p_lo + pi
                            wh = WH[:, par * 2048 + p * 128:par * 2048 + (p + 1) * 128]
                            wl = WL[:, par * 2048 + p * 128:par * 2048 + (p + 1) * 128]
                            inst = mm3(e, PS[pi][:, 0:TC], wh, wl,
                                       midhi(ppar)[:, 0:TC], midlo(ppar)[:, 0:TC],
                                       i == 0, i == NP - 1)
                        return inst
                    S.t(fmm, r=[f"whl:{par}", ptok],
                        w=[f"ps{pi}" for pi in range(p_n)],
                        tag=f"{pref}.dmm{pg}.{tc}.{i}")
                for pi in range(p_n):
                    evac(p_lo + pi, pi, tc)

    def evac6(p, pi, tc):
        S.v(lambda e: e.tensor_tensor(
                hres(p)[:, tc * TC:(tc + 1) * TC], PS[pi][:, 0:TC],
                hres(p)[:, tc * TC:(tc + 1) * TC], ALU.add),
            r=[f"ps{pi}", f"hr:{p}"], w=[f"hr:{p}"], tag=f"p6.x1{p}.{tc}")

    down_phase(wdh_d, wdl_d, NMI, "mh", "ml", midh_d, midl_d, evac6, "p6d")

    for k in range(KD):
        S.s(lambda e, k=k: e.activation(ST1[:, 0:TOK], hres(k)[:, 0:TOK], AF.Square),
            r=[f"hr:{k}", "st1x"], w=["st1x"], tag=f"p7.sq{k}")
        def fmm(e, k=k):
            inst = None
            for j, c0 in enumerate(sch2):
                m = min(128, TOK - c0)
                e.matmul(PS[6][0:m, j:j + 1], hres(k)[:, c0:c0 + m], ONEC[:, :],
                         start=(k == 0 and j == 0), stop=(k == KD - 1 and j == nj2 - 1),
                         skip_group_check=True)
                inst = e.matmul(PS[7][0:m, j:j + 1], ST1[:, c0:c0 + m], ONEC[:, :],
                                start=(k == 0 and j == 0), stop=(k == KD - 1 and j == nj2 - 1),
                                skip_group_check=True)
            return inst
        S.t(fmm, r=[f"hr:{k}", "st1x", "onec", "ps6", "ps7"], w=["ps6", "ps7"],
            tag=f"p7.mm{k}")
    ev_cols(6, 0, sch2, TOK, "p7.ev1")
    ev_cols(7, 32, sch2, TOK, "p7.ev2")
    col2row(R_A, 0, nj2, TOK, "p7.ra")
    col2row(R_B, 32, nj2, TOK, "p7.rb")
    S.v(lambda e: e.tensor_scalar(ROWS[0:1, R_A:R_A + TOK], ROWS[0:1, R_A:R_A + TOK],
                                  1.0 / D, None, ALU.mult),
        r=["rows"], w=["rows"], tag="p7.mean")
    S.v(lambda e: e.tensor_tensor(ROWS[0:1, R_C:R_C + TOK], ROWS[0:1, R_A:R_A + TOK],
                                  ROWS[0:1, R_A:R_A + TOK], ALU.mult),
        r=["rows"], w=["rows"], tag="p7.m2")
    S.v(lambda e: e.scalar_tensor_tensor(ROWS[0:1, R_B:R_B + TOK], ROWS[0:1, R_B:R_B + TOK],
                                         1.0 / D, ROWS[0:1, R_C:R_C + TOK],
                                         op0=ALU.mult, op1=ALU.subtract),
        r=["rows"], w=["rows"], tag="p7.var")
    S.s(lambda e: e.activation(ROWS[0:1, R_C:R_C + TOK], ROWS[0:1, R_B:R_B + TOK],
                               AF.Ln, bias=EPSC[0:1, :]),
        r=["rows", "epsc"], w=["rows"], tag="p7.ln")
    S.s(lambda e: e.activation(ROWS[0:1, R_C:R_C + TOK], ROWS[0:1, R_C:R_C + TOK],
                               AF.Exp, scale=-0.5),
        r=["rows"], w=["rows"], tag="p7.s")
    for j in range(NTC):
        S.t(lambda e, j=j: e.matmul(PS[0][:, 0:TC], ONE1[:, :],
                                    ROWS[0:1, R_A + j * TC:R_A + (j + 1) * TC],
                                    start=True, stop=True),
            r=["rows", "one1", "ps0"], w=["ps0"], tag=f"p7.repm{j}")
        S.v(lambda e, j=j: e.tensor_copy(A4[:, j * TC:(j + 1) * TC], PS[0][:, 0:TC]),
            r=["ps0", "a4:0", "a4:1"], w=["a4:0", "a4:1"], tag=f"p7.evm{j}")
        S.t(lambda e, j=j: e.matmul(PS[1][:, 0:TC], ONE1[:, :],
                                    ROWS[0:1, R_C + j * TC:R_C + (j + 1) * TC],
                                    start=True, stop=True),
            r=["rows", "one1", "ps1"], w=["ps1"], tag=f"p7.repsd{j}")
        S.v(lambda e, j=j: e.tensor_copy(A5[:, j * TC:(j + 1) * TC], PS[1][:, 0:TC]),
            r=["ps1", "a5:0", "a5:1"], w=["a5:0", "a5:1"], tag=f"p7.evsd{j}")
    for k in range(KD):
        S.v(lambda e, k=k: e.tensor_tensor(ST2[:, 0:TOK], hres(k)[:, 0:TOK],
                                           A4[:, 0:TOK], ALU.subtract),
            r=[f"hr:{k}", "a4:0", "a4:1", "st2f:0", "st2f:1"], w=["st2f:0", "st2f:1"], tag=f"p7.sub{k}")
        S.v(lambda e: e.tensor_tensor(ST2[:, 0:TOK], ST2[:, 0:TOK], A5[:, 0:TOK],
                                      ALU.mult),
            r=["st2f:0", "st2f:1", "a5:0", "a5:1"], w=["st2f:0", "st2f:1"], tag=f"p7.l2{k}")
        S.s(lambda e, k=k: e.activation(A2[:, k * TOK:(k + 1) * TOK], ST2[:, 0:TOK],
                                        AF.Copy),
            r=["st2f:0", "st2f:1"], w=["a2"], tag=f"p7.hi{k}")
        S.v(lambda e, k=k: e.tensor_tensor(A3[:, k * TOK:(k + 1) * TOK], ST2[:, 0:TOK],
                                           A2[:, k * TOK:(k + 1) * TOK], ALU.subtract),
            r=["st2f:0", "st2f:1", "a2"], w=["a3"], tag=f"p7.lo{k}")

    gup_phase(fgh_d, fgl_d, fuh_d, ful_d, NFI, fmh_d, fml_d, "fh", "fl", True, FGSC, FUSC, "p7")

    outtoks = []

    def evac7(p, pi, tc):
        S.v(lambda e, pi=pi, p=p, tc=tc: e.scalar_tensor_tensor(
                ST2[:, 0:TC], PS[pi][:, 0:TC], FDSC[:, p:p + 1],
                hres(p)[:, tc * TC:(tc + 1) * TC], op0=ALU.mult, op1=ALU.add),
            r=[f"ps{pi}", "fdsc", f"hr:{p}", "st2f:0", "st2f:1"], w=["st2f:0", "st2f:1"],
            tag=f"p7.x2{p}.{tc}")
        ot = f"out:{p}:{tc}"
        S.dS(lambda e, p=p, tc=tc: e.dma_start(
                out=out_d[p * 128:(p + 1) * 128, tc * TC:(tc + 1) * TC],
                in_=ST2[:, 0:TC]),
             r=["st2f:0", "st2f:1"], w=[ot], tag=f"p7.out{p}.{tc}")
        outtoks.append(ot)

    down_phase(fdh_d, fdl_d, NFI, "fh", "fl", fmh_d, fml_d, evac7, "p7d")

    S.g(lambda e: e.memset(ONEC[:, :], 1.0), r=outtoks, w=["onec"], tag="final")

    S.finalize(nc)
    return nc


def _prep_inputs(cfg, A):
    D, TOK = cfg["D"], cfg["TOK"]
    x = np.asarray(A["L0_attn_hidden_states"], dtype=np.float32)
    B = x.shape[0]
    per = NCORES // B
    assert x.shape[1] == per * TOK
    ct = np.ascontiguousarray

    KD = D // 128

    def pack_panels(w):
        wt = np.asarray(w, np.float32).T
        Dd, P = wt.shape
        npt = P // 128
        kd = Dd // 128
        v = wt.reshape(kd, 128, npt, 128).transpose(2, 1, 0, 3)
        return np.ascontiguousarray(v.reshape(npt * 128, kd * 128))

    shared = dict(
        w1p=pack_panels(A["L0_attn_in_proj_w"]),
        dtwt=ct(np.asarray(A["L0_attn_dt_proj_w"], np.float32).T),
        dtb=ct(np.asarray(A["L0_attn_dt_proj_b"], np.float32)),
        convw=ct(np.asarray(A["L0_attn_conv_w"], np.float32).reshape(cfg["DI"], cfg["DCONV"])),
        convb=ct(np.asarray(A["L0_attn_conv_b"], np.float32)),
        alog=ct(np.asarray(A["L0_attn_A_log"], np.float32)),
        dpar=ct(np.asarray(A["L0_attn_D_param"], np.float32)),
        woutt=ct(np.asarray(A["L0_attn_out_proj_w"], np.float32).T),
        ln1w=ct(np.asarray(A["L0_attn_ln1_w"], np.float32)),
        ln2w=ct(np.asarray(A["L0_attn_ln2_w"], np.float32)),
        wgp=pack_panels(A["L0_attn_gate_w"]),
        wup=pack_panels(A["L0_attn_up_w"]),
        wdt=ct(np.asarray(A["L0_attn_down_w"], np.float32).T),
        fgp=pack_panels(A["L0_ffn_gate_w"]),
        fup=pack_panels(A["L0_ffn_up_w"]),
        fdt=ct(np.asarray(A["L0_ffn_down_w"], np.float32).T),
        fgs=ct(np.asarray(A["L0_ffn_gate_s"], np.float32)),
        fus=ct(np.asarray(A["L0_ffn_up_s"], np.float32)),
        fds=ct(np.asarray(A["L0_ffn_down_s"], np.float32)),
    )
    in_maps = []
    for q in range(NCORES):
        b, q4 = q // per, q % per
        lo = q4 * TOK
        x3 = np.zeros((3, D), np.float32) if q4 == 0 else x[b, lo - 3:lo]
        xs = np.concatenate([x3, x[b, lo:lo + TOK]], axis=0)
        gm = np.zeros((8, 8), np.float32)
        m2 = np.zeros((8,), np.float32)
        for j in range(per):
            jj = b * per + j
            if j < q4:
                m2[jj] = 1.0
                for k in range(j + 1, q4):
                    gm[jj, b * per + k] = 1.0
        im = dict(shared)
        im["x_in"] = ct(xs.T)
        im["gmc"] = ct(np.repeat(gm.reshape(1, 64), 128, axis=0))
        im["m2c"] = ct(np.repeat(m2.reshape(1, 8), 128, axis=0))
        in_maps.append(im)
    return in_maps


def _assemble(cfg, results):
    per = NCORES // 2
    outs = [np.asarray(r["out"]) for r in results]
    bs = [np.concatenate([outs[b * per + q].T for q in range(per)], axis=0)
          for b in range(2)]
    return np.stack(bs, axis=0).astype(np.float32)


def kernel(**inputs):
    cfg = real_cfg()
    in_maps = _prep_inputs(cfg, inputs)
    nc = build_nc(cfg)
    res = run_bass_kernel_spmd(nc, in_maps, core_ids=list(range(NCORES)))
    return _assemble(cfg, res.results)


# version 12
# speedup vs baseline: 1.2088x; 1.2088x over previous
import math
import numpy as np
import concourse.bass as bass
import concourse.mybir as mybir
from concourse.bass_utils import run_bass_kernel_spmd

F32 = mybir.dt.float32
BF16 = mybir.dt.bfloat16
AF = mybir.ActivationFunctionType
ALU = mybir.AluOpType
AX = mybir.AxisListType

NCORES = 8


def real_cfg():
    return dict(D=2048, DI=4096, DXB=1024, DST=16, DCONV=4, DTR=128,
                MI=8192, FI=5632, TOK=1024, EPS=1e-5)


class Sched:
    COMPUTE = ("tensor", "vector", "scalar", "gpsimd", "sync")

    def __init__(self):
        self.ops = []
        self.lastw = {}
        self.readers = {}

    def emit(self, eng, fn, r=(), w=(), dma=False, tag="", force=False, nmul=1):
        i = len(self.ops)
        deps = set()
        for t in r:
            if t in self.lastw:
                deps.add(self.lastw[t])
        for t in w:
            if t in self.lastw:
                deps.add(self.lastw[t])
            for j in self.readers.get(t, ()):
                deps.add(j)
        deps.discard(i)
        self.ops.append(dict(i=i, eng=eng, fn=fn, dma=dma, deps=deps, tag=tag,
                             need_inc=force or dma, seq=None, nmul=nmul))
        for t in r:
            self.readers.setdefault(t, []).append(i)
        for t in w:
            self.lastw[t] = i
            self.readers[t] = []
        return i

    def stream_of(self, rec):
        return rec["eng"] + ".q" if rec["dma"] else rec["eng"]

    NLANES = 8

    def finalize(self, nc):
        ops = self.ops
        for rec in ops:
            for d in rec["deps"]:
                ops[d]["need_inc"] = True
        seqno = {}
        lane_rot = {}
        for rec in ops:
            if not rec["need_inc"]:
                continue
            if rec["dma"]:
                q = rec["eng"] + ".q"
                lane = lane_rot.get(q, 0)
                lane_rot[q] = (lane + 1) % self.NLANES
                s = (q, lane)
            else:
                s = self.stream_of(rec)
            rec["stream"] = s
            prev = seqno.get(s, 0)
            rec["gate"] = prev if rec["dma"] else None
            seqno[s] = prev + rec.get("nmul", 1)
            rec["seq"] = seqno[s]
        sems = {}
        for s in sorted(seqno, key=str):
            nm = s if isinstance(s, str) else f"{s[0]}_{s[1]}"
            sems[s] = nc.semaphore("sem_" + nm.replace(".", "_")).__enter__()
        per_eng = {e: [] for e in self.COMPUTE}
        waited = {e: {} for e in self.COMPUTE}
        for rec in ops:
            waits = []
            for d in sorted(rec["deps"]):
                p = ops[d]
                if not p["need_inc"]:
                    continue
                s = p["stream"] if "stream" in p else self.stream_of(p)
                v = p["seq"] * (16 if p["dma"] else 1)
                if waited[rec["eng"]].get(s, 0) < v:
                    waited[rec["eng"]][s] = v
                    waits.append((s, v))
            if rec["dma"] and rec["gate"]:
                s = rec["stream"]
                v = rec["gate"] * 16
                if waited[rec["eng"]].get(s, 0) < v:
                    waited[rec["eng"]][s] = v
                    waits.append((s, v))
            per_eng[rec["eng"]].append((rec, waits))

        def mk(eng_name):
            def body(e):
                for rec, wl in per_eng[eng_name]:
                    for (s, v) in wl:
                        e.wait_ge(sems[s], v)
                    inst = rec["fn"](e)
                    if rec["need_inc"]:
                        assert inst is not None, f"op '{rec['tag']}' returned None"
                        amt = 16 if rec["dma"] else 1
                        insts = inst if isinstance(inst, list) else [inst]
                        assert len(insts) == rec.get("nmul", 1), rec["tag"]
                        for it in insts:
                            it.then_inc(sems[rec["stream"]], amt)
            return body

        with nc.Block() as block:
            for eng_name in self.COMPUTE:
                if per_eng[eng_name]:
                    getattr(block, eng_name)(mk(eng_name))

    def t(self, fn, r=(), w=(), tag=""):
        return self.emit("tensor", fn, r, w, tag=tag)

    def v(self, fn, r=(), w=(), tag=""):
        return self.emit("vector", fn, r, w, tag=tag)

    def s(self, fn, r=(), w=(), tag=""):
        return self.emit("scalar", fn, r, w, tag=tag)

    def g(self, fn, r=(), w=(), tag=""):
        return self.emit("gpsimd", fn, r, w, tag=tag)

    def dW(self, fn, r=(), w=(), tag="", nmul=1):
        return self.emit("sync", fn, r, w, dma=True, tag=tag, nmul=nmul)

    def dR(self, fn, r=(), w=(), tag="", nmul=1):
        return self.emit("scalar", fn, r, w, dma=True, tag=tag, nmul=nmul)

    def dS(self, fn, r=(), w=(), tag=""):
        return self.emit("gpsimd", fn, r, w, dma=True, tag=tag)


def ap4(base_ap, offset, dims):
    return bass.AP(base_ap.tensor, base_ap.offset + offset, [list(d) for d in dims])


def build_nc(cfg):
    D, DI, DXB, DST = cfg["D"], cfg["DI"], cfg["DXB"], cfg["DST"]
    DCONV, DTR, MI, FI = cfg["DCONV"], cfg["DTR"], cfg["MI"], cfg["FI"]
    TOK, EPS = cfg["TOK"], cfg["EPS"]
    TOKP = TOK + 3
    KD = D // 128
    NCT = DI // 128
    H = DI // DST
    REP = DI // DXB
    PROJ = 2 * DI + 2 * DXB + DTR
    NPT = PROJ // 128
    NXT = DXB // 128
    PT_Z0, PT_X0 = 0, DI // 128
    PT_B0 = PT_X0 + NXT
    PT_C0 = PT_B0 + NXT
    PT_DT = PT_C0 + DI // 128
    HPC = 128 // DST
    NHXB = max(1, HPC // REP)
    TC = min(cfg.get("TCAP", 512), TOK)
    NTC = TOK // TC
    TH = TC
    NTH = TOK // TH
    SC = min(cfg.get("SCAP", 64), TOK, TH)
    NSC = TH // SC
    NMI, NFI = MI // 128, FI // 128
    assert D % 128 == 0 and DI % 128 == 0 and DXB % 128 == 0 and PROJ % 128 == 0
    assert REP * DST * NHXB == 128 and HPC * DST == 128

    nc = bass.Bass("TRN2", target_bir_lowering=False, debug=False)

    def di(name, shape):
        return nc.dram_tensor(name, shape, F32, kind="ExternalInput").ap()

    x_in = di("x_in", [D, TOKP])
    w1p = di("w1p", [NPT * 128, KD * 128])
    dtwt = di("dtwt", [DTR, DI])
    dtb = di("dtb", [DI])
    convw = di("convw", [DI, DCONV])
    convb = di("convb", [DI])
    alog = di("alog", [DI, DST])
    dpar = di("dpar", [DI])
    woutt = di("woutt", [DI, D])
    ln1w = di("ln1w", [D])
    ln2w = di("ln2w", [D])
    wgp = di("wgp", [NMI * 128, KD * 128])
    wup = di("wup", [NMI * 128, KD * 128])
    wdt = di("wdt", [MI, D])
    fgp = di("fgp", [NFI * 128, KD * 128])
    fup = di("fup", [NFI * 128, KD * 128])
    fdt = di("fdt", [FI, D])
    fgs = di("fgs", [FI])
    fus = di("fus", [FI])
    fds = di("fds", [D])
    gmc = di("gmc", [128, 64])
    m2c = di("m2c", [128, 8])

    out_d = nc.dram_tensor("out", [D, TOK], F32, kind="ExternalOutput").ap()

    def dsc(name, shape, dt=F32):
        return nc.dram_tensor(name, shape, dt).ap()

    xp_d = dsc("xp_d", [DXB, TOKP])
    b_d = dsc("b_d", [DXB, TOK])
    z_d = dsc("z_d", [DI, TOK])
    cpl_d = dsc("cpl_d", [H, DST * TOK])
    dt_d = dsc("dt_d", [DI, TOK])
    a_d = dsc("a_d", [DI, TOK])
    xcd_d = dsc("xcd_d", [DI, TOK])
    h_d = dsc("h_d", [NCT * 128 * DST * TOK])
    midh_d = dsc("midh_d", [MI, TOK], BF16)
    midl_d = dsc("midl_d", [MI, TOK], BF16)
    fmh_d = dsc("fmh_d", [FI, TOK], BF16)
    fml_d = dsc("fml_d", [FI, TOK], BF16)
    wouth_d = dsc("wouth_d", [DI, D], BF16)
    woutl_d = dsc("woutl_d", [DI, D], BF16)
    wdh_d = dsc("wdh_d", [MI, D], BF16)
    wdl_d = dsc("wdl_d", [MI, D], BF16)
    fdh_d = dsc("fdh_d", [FI, D], BF16)
    fdl_d = dsc("fdl_d", [FI, D], BF16)
    CCN = DI * DST + DI
    cc_in = nc.dram_tensor("cc_in", [1, CCN], F32)
    cc_out = nc.dram_tensor("cc_out", [NCORES, CCN], F32)

    S = Sched()

    avail = (nc.sbuf_top - nc.sbuf_base - 512) & ~31
    guard = nc.alloc_sbuf_tensor("manual_region", [128, avail], mybir.dt.uint8)
    gbase = nc.lookup_mloc(guard).addr
    off = [0]

    def sb(name, shape, dt):
        t = nc.alloc_sbuf_tensor_at(name, shape, dt, offset=gbase + off[0])
        nb = 1
        for s_ in shape[1:]:
            nb *= s_
        nb *= 4 if dt == F32 else 2
        off[0] += (nb + 31) & ~31
        return t

    A1W = max(16 * TOK, 15 * TOK + TOKP)
    A1 = sb("A1", [128, A1W], F32)
    A2 = sb("A2", [128, 16 * TOKP], BF16)
    A3 = sb("A3", [128, 16 * TOKP], BF16)
    A4 = sb("A4", [128, max(DST * SC, TOKP, 2 * TH)], F32)
    A5 = sb("A5", [128, max(DST * SC, TOKP)], F32)
    WF = sb("WF", [128, 2 * 2048], F32)
    WH = sb("WH", [128, 2 * 2048], BF16)
    WL = sb("WL", [128, 2 * 2048], BF16)
    ST1 = sb("ST1", [128, TOKP + 5], F32)
    ST2 = sb("ST2", [128, TOKP + 5], F32)
    PHS = sb("PHS", [128, TOK], BF16)
    PLS = sb("PLS", [128, TOK], BF16)
    ACOL = sb("ACOL", [128, NCT * DST], F32)
    HIN = sb("HIN", [128, NCT * DST], F32)
    SCO = sb("SCO", [128, max(NCT * DST, 8 * NCT)], F32)
    TK8 = SCO
    ATOT = sb("ATOT", [128, NCT], F32)
    GJ = sb("GJ", [128, NCT], F32)
    M2S = sb("M2S", [128, 8], F32)
    GMS = sb("GMS", [128, 64], F32)
    PK = sb("PK", [128, DCONV * NCT], F32)
    CB = sb("CB", [128, NCT], F32)
    DPC = sb("DPC", [128, NCT], F32)
    DTBC = sb("DTBC", [128, NCT], F32)
    L1C = sb("L1C", [128, KD], F32)
    L2C = sb("L2C", [128, KD], F32)
    FGSC = sb("FGSC", [128, NFI], F32)
    FUSC = sb("FUSC", [128, NFI], F32)
    FDSC = sb("FDSC", [128, KD], F32)
    ONE1 = sb("ONE1", [1, 128], F32)
    ONEC = sb("ONEC", [128, 1], F32)
    EPSC = sb("EPSC", [128, 1], F32)
    COLS = sb("COLS", [128, 44], F32)
    DTRH = sb("DTRH", [128, TOK], BF16)
    DTRL = sb("DTRL", [128, TOK], BF16)
    YCT = sb("YCT", [128, max(TH, NCT * DST)], F32)
    TRM = YCT
    RS = TOKP + 1
    ROWS = sb("ROWS", [1, 3 * RS], F32)
    R_A, R_B, R_C = 0, RS, 2 * RS
    assert off[0] <= avail, f"SBUF over budget: {off[0]} > {avail}"

    PS = [nc.psum_tensor(f"ps{i}", [128, 512], F32).__enter__() for i in range(8)]

    A1_dt = 0
    A1_xe = A1_dt + TOK
    A1_xc = A1_xe + TOKP
    A1_a = A1_xc + TOK
    A1_n0 = A1_a + TOK

    hres = lambda k: A1[:, k * TOK:(k + 1) * TOK]

    def midhi(par):
        return PHS[:, 0:TOK] if par == 0 else DTRH[:, 0:TOK]

    def midlo(par):
        return PLS[:, 0:TOK] if par == 0 else DTRL[:, 0:TOK]

    S.g(lambda e: e.memset(ONE1[:, :], 1.0), w=["one1"], tag="ones1")
    S.g(lambda e: e.memset(ONEC[:, :], 1.0), w=["onec"], tag="onec")
    S.g(lambda e: e.memset(HIN[:, :], 0.0), w=["hin"], tag="hin0")
    S.g(lambda e: e.memset(EPSC[:, :], float(EPS)), w=["epsc"], tag="epsc")
    S.g(lambda e: e.memset(COLS[:, :], 0.0), w=["cols"], tag="cols0")

    def packed(dst_ap, src_ap, n_ct, inner=1):
        def fn(e):
            s2 = ap4(src_ap, 0, [[inner, 128], [128 * inner, n_ct], [1, inner]])
            return e.dma_start(out=dst_ap, in_=s2, allow_slow_non_contiguous=True)
        return fn

    S.dW(packed(PK[:, :], convw, NCT, DCONV), w=["pk"], tag="pk")
    S.dW(packed(CB[:, :], convb, NCT), w=["cb"], tag="cb")
    S.dW(packed(DPC[:, :], dpar, NCT), w=["dpc"], tag="dpc")
    S.dW(packed(DTBC[:, :], dtb, NCT), w=["dtbc"], tag="dtbc")
    S.dW(packed(L1C[:, :], ln1w, KD), w=["l1c"], tag="l1c")
    S.dW(packed(L2C[:, :], ln2w, KD), w=["l2c"], tag="l2c")
    S.dW(packed(FGSC[:, :], fgs, NFI), w=["fgsc"], tag="fgsc")
    S.dW(packed(FUSC[:, :], fus, NFI), w=["fusc"], tag="fusc")
    S.dW(packed(FDSC[:, :], fds, KD), w=["fdsc"], tag="fdsc")
    S.dW(packed(ACOL[:, :], alog, NCT, DST), w=["acol"], tag="alog")
    S.dW(lambda e: e.dma_start(out=GMS[:, :], in_=gmc), w=["gms"], tag="gms")
    S.dW(lambda e: e.dma_start(out=M2S[:, :], in_=m2c), w=["m2s"], tag="m2s")
    S.s(lambda e: e.activation(ACOL[:, :], ACOL[:, :], AF.Exp),
        r=["acol"], w=["acol"], tag="acol_exp")
    S.v(lambda e: e.tensor_scalar(ACOL[:, :], ACOL[:, :], -1.0, None, ALU.mult),
        r=["acol"], w=["acol"], tag="acol_neg")

    schunks = list(range(0, TOKP, 128))
    nj = len(schunks)

    for k in range(KD):
        S.dW(lambda e, k=k: e.dma_start(out=ST2[:, 0:TOKP],
                                        in_=x_in[k * 128:(k + 1) * 128, :]),
             r=["st2f"], w=["st2f"], tag=f"p1.ldx{k}")
        S.s(lambda e: e.activation(ST1[:, 0:TOKP], ST2[:, 0:TOKP], AF.Square),
            r=["st2f", "st1x"], w=["st1x"], tag=f"p1.sq{k}")
        def fmm(e, k=k):
            inst = None
            for j, c0 in enumerate(schunks):
                m = min(128, TOKP - c0)
                e.matmul(PS[6][0:m, j:j + 1], ST2[:, c0:c0 + m], ONEC[:, :],
                         start=(k == 0 and j == 0), stop=(k == KD - 1 and j == nj - 1),
                         skip_group_check=True)
                inst = e.matmul(PS[7][0:m, j:j + 1], ST1[:, c0:c0 + m], ONEC[:, :],
                                start=(k == 0 and j == 0), stop=(k == KD - 1 and j == nj - 1),
                                skip_group_check=True)
            return inst
        S.t(fmm, r=["st2f", "st1x", "onec"], w=["ps6", "ps7"], tag=f"p1.mm{k}")

    def ev_cols(bank, coff, chlist, width, tag):
        def fn(e, bank=bank, coff=coff):
            inst = None
            for j, c0 in enumerate(chlist):
                m = min(128, width - c0)
                inst = e.tensor_copy(COLS[0:m, coff + j:coff + j + 1],
                                     PS[bank][0:m, j:j + 1])
            return inst
        S.v(fn, r=[f"ps{bank}", "cols"], w=["cols"], tag=tag)

    ev_cols(6, 0, schunks, TOKP, "p1.ev1")
    ev_cols(7, 32, schunks, TOKP, "p1.ev2")

    def col2row(dst_off, src_off, n, width, tag):
        def fn(e):
            insts = []
            for j in range(n):
                m = min(128, width - j * 128)
                insts.append(e.dma_start(
                    out=ROWS[0:1, dst_off + j * 128:dst_off + j * 128 + m],
                    in_=COLS[0:m, src_off + j:src_off + j + 1]))
            return insts
        return self_dS(fn, n, tag)

    def self_dS(fn, n, tag):
        return S.emit("gpsimd", fn, ("cols",), ("rows",), dma=True, tag=tag,
                      nmul=n)

    col2row(R_A, 0, nj, TOKP, "p1.ra")
    col2row(R_B, 32, nj, TOKP, "p1.rb")
    S.v(lambda e: e.tensor_scalar(ROWS[0:1, R_A:R_A + TOKP], ROWS[0:1, R_A:R_A + TOKP],
                                  1.0 / D, None, ALU.mult),
        r=["rows"], w=["rows"], tag="p1.mean")
    S.v(lambda e: e.tensor_tensor(ROWS[0:1, R_C:R_C + TOKP], ROWS[0:1, R_A:R_A + TOKP],
                                  ROWS[0:1, R_A:R_A + TOKP], ALU.mult),
        r=["rows"], w=["rows"], tag="p1.m2")
    S.v(lambda e: e.scalar_tensor_tensor(ROWS[0:1, R_B:R_B + TOKP], ROWS[0:1, R_B:R_B + TOKP],
                                         1.0 / D, ROWS[0:1, R_C:R_C + TOKP],
                                         op0=ALU.mult, op1=ALU.subtract),
        r=["rows"], w=["rows"], tag="p1.var")
    S.s(lambda e: e.activation(ROWS[0:1, R_C:R_C + TOKP], ROWS[0:1, R_B:R_B + TOKP],
                               AF.Ln, bias=EPSC[0:1, :]),
        r=["rows", "epsc"], w=["rows"], tag="p1.ln")
    S.s(lambda e: e.activation(ROWS[0:1, R_C:R_C + TOKP], ROWS[0:1, R_C:R_C + TOKP],
                               AF.Exp, scale=-0.5),
        r=["rows"], w=["rows"], tag="p1.s")
    S.v(lambda e: e.tensor_tensor(ROWS[0:1, R_B:R_B + TOKP], ROWS[0:1, R_B:R_B + TOKP],
                                  ROWS[0:1, R_C:R_C + TOKP], ALU.mult),
        r=["rows"], w=["rows"], tag="p1.vs")
    S.v(lambda e: e.tensor_tensor(ROWS[0:1, R_B:R_B + TOKP], ROWS[0:1, R_B:R_B + TOKP],
                                  ROWS[0:1, R_C:R_C + TOKP], ALU.mult),
        r=["rows"], w=["rows"], tag="p1.vss")
    S.s(lambda e: e.activation(ROWS[0:1, R_B:R_B + TOKP], ROWS[0:1, R_B:R_B + TOKP],
                               AF.Ln, bias=EPSC[0:1, :]),
        r=["rows", "epsc"], w=["rows"], tag="p1.ln2")
    S.s(lambda e: e.activation(ROWS[0:1, R_B:R_B + TOKP], ROWS[0:1, R_B:R_B + TOKP],
                               AF.Exp, scale=-0.5),
        r=["rows"], w=["rows"], tag="p1.rmsi")
    S.v(lambda e: e.tensor_tensor(ROWS[0:1, R_B:R_B + TOKP], ROWS[0:1, R_B:R_B + TOKP],
                                  ROWS[0:1, R_C:R_C + TOKP], ALU.mult),
        r=["rows"], w=["rows"], tag="p1.srr")

    def rep_row(row_off, dst, dst_tok, width, tag):
        mch = list(range(0, width, 512))
        for j, c0 in enumerate(mch):
            w_ = min(512, width - c0)
            bank = j % 2
            S.t(lambda e, c0=c0, w_=w_, bank=bank, row_off=row_off: e.matmul(
                    PS[bank][:, 0:w_], ONE1[:, :],
                    ROWS[0:1, row_off + c0:row_off + c0 + w_], start=True, stop=True),
                r=["rows", "one1"], w=[f"ps{bank}"], tag=f"{tag}.{j}")
            S.v(lambda e, c0=c0, w_=w_, bank=bank: e.tensor_copy(
                    dst[:, c0:c0 + w_], PS[bank][:, 0:w_]),
                r=[f"ps{bank}"], w=list(dst_tok), tag=f"{tag}.ev{j}")

    rep_row(R_A, A4, ("a4:0", "a4:1"), TOKP, "p1.repm")
    rep_row(R_B, A5, ("a5:0", "a5:1"), TOKP, "p1.reps")

    for k in range(KD):
        S.dW(lambda e, k=k: e.dma_start(out=ST2[:, 0:TOKP],
                                        in_=x_in[k * 128:(k + 1) * 128, :]),
             r=["st2f"], w=["st2f"], tag=f"p1b.ldx{k}")
        S.v(lambda e: e.tensor_tensor(ST1[:, 0:TOKP], ST2[:, 0:TOKP], A4[:, 0:TOKP],
                                      ALU.subtract),
            r=["st2f", "a4:0", "a4:1", "st1x"], w=["st1x"], tag=f"p1b.sub{k}")
        S.v(lambda e, k=k: e.scalar_tensor_tensor(ST1[:, 0:TOKP], ST1[:, 0:TOKP],
                                                  L1C[:, k:k + 1], A5[:, 0:TOKP],
                                                  op0=ALU.mult, op1=ALU.mult),
            r=["st1x", "l1c", "a5:0", "a5:1"], w=["st1x"], tag=f"p1b.r1{k}")
        S.s(lambda e, k=k: e.activation(A2[:, k * TOKP:(k + 1) * TOKP], ST1[:, 0:TOKP],
                                        AF.Copy),
            r=["st1x"], w=["a2"], tag=f"p1b.hi{k}")
        S.v(lambda e, k=k: e.tensor_tensor(A3[:, k * TOKP:(k + 1) * TOKP], ST1[:, 0:TOKP],
                                           A2[:, k * TOKP:(k + 1) * TOKP], ALU.subtract),
            r=["st1x", "a2"], w=["a3"], tag=f"p1b.lo{k}")

    r1h = lambda k: A2[:, k * TOKP:(k + 1) * TOKP]
    r1l = lambda k: A3[:, k * TOKP:(k + 1) * TOKP]
    r1toks = ["a2", "a3"]

    def load_panel_pcols(wap, pt, ncols_w, par, tag):
        def fn(e, par=par):
            return e.dma_start(out=WF[:, par * 2048:par * 2048 + KD * 128],
                               in_=wap[pt * 128:(pt + 1) * 128, :])
        return S.dW(fn, r=[f"wf:{par}"], w=[f"wf:{par}"], tag=tag)

    def split_panel(par, ncols, tag, lo_eng="vector"):
        S.s(lambda e, par=par, ncols=ncols: e.activation(
                WH[:, par * 2048:par * 2048 + ncols],
                WF[:, par * 2048:par * 2048 + ncols], AF.Copy),
            r=[f"wf:{par}", f"whl:{par}"], w=[f"whl:{par}"], tag=tag + ".hi")
        S.emit(lo_eng, lambda e, par=par, ncols=ncols: e.tensor_tensor(
                WL[:, par * 2048:par * 2048 + ncols],
                WF[:, par * 2048:par * 2048 + ncols],
                WH[:, par * 2048:par * 2048 + ncols], ALU.subtract),
            [f"wf:{par}", f"whl:{par}"], [f"whl:{par}", f"wf:{par}"], tag=tag + ".lo")

    def wtile(par, k):
        return (WH[:, par * 2048 + k * 128:par * 2048 + (k + 1) * 128],
                WL[:, par * 2048 + k * 128:par * 2048 + (k + 1) * 128])

    def mm3(e, psum_ap, wh, wl, ah, al, first, last):
        e.matmul(psum_ap, wh, ah, start=first, stop=False)
        e.matmul(psum_ap, wh, al, start=False, stop=False)
        return e.matmul(psum_ap, wl, ah, start=False, stop=last)

    def proj_mm(par, ah, al, atoks, c0, c1, bank, tag, nk=None):
        nk_ = nk or KD
        def fn(e):
            inst = None
            for k in range(nk_):
                wh, wl = wtile(par, k)
                inst = mm3(e, PS[bank][:, 0:c1 - c0], wh, wl,
                           ah(k)[:, c0:c1], al(k)[:, c0:c1], k == 0, k == nk_ - 1)
            return inst
        return S.t(fn, r=[f"whl:{par}"] + atoks, w=[f"ps{bank}"], tag=tag)

    stg = [0]

    def stage():
        stg[0] ^= 1
        return (ST1, "st1x") if stg[0] else (ST2, "st2f")

    bankrot = [0]

    def do_proj_tile(kind, pt):
        par = pt % 2
        load_panel_pcols(w1p, pt, KD * 128, par, f"p2.w{pt}")
        split_panel(par, KD * 128, f"p2.s{pt}", lo_eng="gpsimd")
        if kind == "x":
            chunks = [(c0, min(c0 + 512, TOKP)) for c0 in range(0, TOKP, 512)]
        else:
            chunks = [(3 + i * TC, 3 + (i + 1) * TC) for i in range(NTC)]
        for (c0, c1) in chunks:
            bank = bankrot[0] % 4
            bankrot[0] += 1
            proj_mm(par, r1h, r1l, r1toks, c0, c1, bank, f"p2.mm{pt}.{c0}")
            w_ = c1 - c0
            STG, stok = stage()
            S.s(lambda e, bank=bank, STG=STG, w_=w_: e.activation(
                    STG[:, 0:w_], PS[bank][:, 0:w_], AF.Copy),
                r=[f"ps{bank}", stok], w=[stok], tag=f"p2.ev{pt}.{c0}")
            if kind == "x":
                xi = pt - PT_X0
                S.dS(lambda e, xi=xi, c0=c0, w_=w_, STG=STG: e.dma_start(
                        out=xp_d[xi * 128:(xi + 1) * 128, c0:c0 + w_], in_=STG[:, 0:w_]),
                     r=[stok], w=[f"xp:{xi}"], tag=f"p2.sp{pt}.{c0}")
            elif kind == "B":
                bi = pt - PT_B0
                S.dS(lambda e, bi=bi, c0=c0, w_=w_, STG=STG: e.dma_start(
                        out=b_d[bi * 128:(bi + 1) * 128, c0 - 3:c0 - 3 + w_],
                        in_=STG[:, 0:w_]),
                     r=[stok], w=[f"bd:{bi}"], tag=f"p2.sp{pt}.{c0}")
            elif kind == "z":
                zi = pt - PT_Z0
                S.dS(lambda e, zi=zi, c0=c0, w_=w_, STG=STG: e.dma_start(
                        out=z_d[zi * 128:(zi + 1) * 128, c0 - 3:c0 - 3 + w_],
                        in_=STG[:, 0:w_]),
                     r=[stok], w=[f"zd:{zi}"], tag=f"p2.sp{pt}.{c0}")
            elif kind == "C":
                ci = pt - PT_C0
                def spc(e, ci=ci, c0=c0, w_=w_, STG=STG):
                    dst = ap4(cpl_d, (ci * HPC) * DST * TOK + (c0 - 3),
                              [[DST * TOK, HPC], [TOK, DST], [1, w_]])
                    return e.dma_start(out=dst, in_=STG[:, 0:w_])
                S.dS(spc, r=[stok], w=[f"cpl:{ci}"], tag=f"p2.sp{pt}.{c0}")
            elif kind == "dtr":
                S.s(lambda e, c0=c0, w_=w_, STG=STG: e.activation(
                        DTRH[:, c0 - 3:c0 - 3 + w_], STG[:, 0:w_], AF.Copy),
                    r=[stok, "dtr"], w=["dtr"], tag=f"p2.dh.{c0}")
                S.v(lambda e, c0=c0, w_=w_, STG=STG: e.tensor_tensor(
                        DTRL[:, c0 - 3:c0 - 3 + w_], STG[:, 0:w_],
                        DTRH[:, c0 - 3:c0 - 3 + w_], ALU.subtract),
                    r=[stok, "dtr"], w=["dtr"], tag=f"p2.dl.{c0}")

    for i in range(NXT):
        do_proj_tile("x", PT_X0 + i)
    do_proj_tile("dtr", PT_DT)
    for i in range(NXT):
        do_proj_tile("B", PT_B0 + i)

    nhalf = NCT // 2
    for half in range(2):
        par = half
        def ldw(e, half=half):
            sap = ap4(dtwt, half * nhalf * 128, [[DI, 128], [1, nhalf * 128]])
            return e.dma_start(out=WF[:, half * 2048:half * 2048 + nhalf * 128],
                               in_=sap)
        S.dW(ldw, r=[f"wf:{half}"], w=[f"wf:{half}"], tag=f"p2d.w{half}")
        split_panel(par, nhalf * 128, f"p2d.s{half}")
        for j in range(nhalf):
            ct = half * nhalf + j
            for i in range(NTC):
                bank = 4 + ((ct * NTC + i) % 2)
                def fmm(e, par=par, j=j, bank=bank, i=i):
                    wh, wl = wtile(par, j)
                    return mm3(e, PS[bank][:, 0:TC], wh, wl,
                               DTRH[:, i * TC:(i + 1) * TC],
                               DTRL[:, i * TC:(i + 1) * TC], True, True)
                S.t(fmm, r=[f"whl:{par}", "dtr"], w=[f"ps{bank}"],
                    tag=f"p2d.mm{ct}.{i}")
                SA, atok = stage()
                SB, btok = stage()
                S.v(lambda e, ct=ct, bank=bank, SA=SA: e.tensor_scalar(
                        SA[:, 0:TC], PS[bank][:, 0:TC], DTBC[:, ct:ct + 1], None,
                        ALU.add),
                    r=[f"ps{bank}", "dtbc", atok], w=[atok], tag=f"p2d.t{ct}.{i}")
                S.s(lambda e, SA=SA, SB=SB: e.activation(SB[:, 0:TC], SA[:, 0:TC],
                                                         AF.Abs),
                    r=[atok, btok], w=[btok], tag=f"p2d.ab{ct}.{i}")
                S.s(lambda e, SB=SB: e.activation(SB[:, 0:TC], SB[:, 0:TC], AF.Exp,
                                                  scale=-1.0),
                    r=[btok], w=[btok], tag=f"p2d.ex{ct}.{i}")
                S.s(lambda e, SB=SB: e.activation(SB[:, 0:TC], SB[:, 0:TC], AF.Ln,
                                                  bias=ONEC[:, 0:1]),
                    r=[btok, "onec"], w=[btok], tag=f"p2d.ln{ct}.{i}")
                S.s(lambda e, SA=SA: e.activation(SA[:, 0:TC], SA[:, 0:TC], AF.Relu),
                    r=[atok], w=[atok], tag=f"p2d.rl{ct}.{i}")
                S.v(lambda e, SA=SA, SB=SB: e.tensor_tensor(
                        SA[:, 0:TC], SA[:, 0:TC], SB[:, 0:TC], ALU.add),
                    r=[atok, btok], w=[atok], tag=f"p2d.sp{ct}.{i}")
                S.dS(lambda e, ct=ct, i=i, SA=SA: e.dma_start(
                        out=dt_d[ct * 128:(ct + 1) * 128, i * TC:(i + 1) * TC],
                        in_=SA[:, 0:TC]),
                     r=[atok], w=[f"dtd:{ct}"], tag=f"p2d.spl{ct}.{i}")

    def scan_ct(ct):
        hxb0 = (ct * HPC) // REP
        xpt = (hxb0 * DST) // 128
        S.dW(lambda e, ct=ct: e.dma_start(out=A1[:, A1_dt:A1_dt + TOK],
                                          in_=dt_d[ct * 128:(ct + 1) * 128, :]),
             r=[f"dtd:{ct}", "a1dt"], w=["a1dt"], tag=f"sc.dt{ct}")
        PB = REP * DST
        def ldxe(e, hxb0=hxb0, PB=PB):
            insts = []
            for hx in range(NHXB):
                sap = ap4(xp_d, (hxb0 + hx) * DST * TOKP,
                          [[0, REP], [TOKP, DST], [1, TOKP]])
                insts.append(e.dma_start(
                    out=A1[hx * PB:(hx + 1) * PB, A1_xe:A1_xe + TOKP], in_=sap))
            return insts
        S.dW(ldxe, r=[f"xp:{xpt}", "a1xe"], w=["a1xe"], tag=f"sc.xe{ct}", nmul=NHXB)
        S.v(lambda e, ct=ct: e.tensor_scalar(A1[:, A1_xc:A1_xc + TOK],
                                             A1[:, A1_xe:A1_xe + TOK],
                                             PK[:, ct * DCONV:ct * DCONV + 1], None,
                                             ALU.mult),
            r=["a1xe", "pk", "a1xc"], w=["a1xc"], tag=f"sc.c0.{ct}")
        for kk in range(1, DCONV):
            S.v(lambda e, ct=ct, kk=kk: e.scalar_tensor_tensor(
                    A1[:, A1_xc:A1_xc + TOK], A1[:, A1_xe + kk:A1_xe + kk + TOK],
                    PK[:, ct * DCONV + kk:ct * DCONV + kk + 1],
                    A1[:, A1_xc:A1_xc + TOK], op0=ALU.mult, op1=ALU.add),
                r=["a1xe", "pk", "a1xc"], w=["a1xc"], tag=f"sc.c{kk}.{ct}")
        S.s(lambda e, ct=ct: e.activation(A1[:, A1_xc:A1_xc + TOK],
                                          A1[:, A1_xc:A1_xc + TOK], AF.Silu,
                                          bias=CB[:, ct:ct + 1]),
            r=["a1xc", "cb"], w=["a1xc"], tag=f"sc.silu{ct}")
        S.v(lambda e, ct=ct: e.tensor_scalar(A1[:, A1_xe:A1_xe + TOK],
                                             A1[:, A1_xc:A1_xc + TOK],
                                             DPC[:, ct:ct + 1], None, ALU.mult),
            r=["a1xc", "dpc", "a1xe"], w=["a1xe"], tag=f"sc.xcd{ct}")
        S.dS(lambda e, ct=ct: e.dma_start(out=xcd_d[ct * 128:(ct + 1) * 128, :],
                                          in_=A1[:, A1_xe:A1_xe + TOK]),
             r=["a1xe"], w=[f"xcd:{ct}"], tag=f"sc.spxcd{ct}")
        S.v(lambda e: e.tensor_tensor(A1[:, A1_xc:A1_xc + TOK], A1[:, A1_dt:A1_dt + TOK],
                                      A1[:, A1_xc:A1_xc + TOK], ALU.mult),
            r=["a1dt", "a1xc"], w=["a1xc"], tag=f"sc.dtx{ct}")
        S.v(lambda e: e.tensor_tensor_scan(A1[:, A1_a:A1_a + TOK], A1[:, A1_dt:A1_dt + TOK],
                                           A1[:, A1_dt:A1_dt + TOK], initial=0.0,
                                           op0=ALU.add, op1=ALU.bypass),
            r=["a1dt", "a1a"], w=["a1a"], tag=f"sc.cum{ct}")
        S.dS(lambda e, ct=ct: e.dma_start(out=a_d[ct * 128:(ct + 1) * 128, :],
                                          in_=A1[:, A1_a:A1_a + TOK]),
             r=["a1a"], w=[f"ad:{ct}"], tag=f"sc.spa{ct}")
        S.v(lambda e, ct=ct: e.tensor_copy(ATOT[:, ct:ct + 1],
                                           A1[:, A1_a + TOK - 1:A1_a + TOK]),
            r=["a1a", "atot"], w=["atot"], tag=f"sc.atot{ct}")
        PB = REP * DST
        for np_ in range(DST // 2):
            n0 = 2 * np_
            par = np_ % 3
            base = A1_n0 + par * 4 * TOK
            bA, bB = base, base + 2 * TOK
            bH = bA
            ntok = f"a1n:{par}"
            def dA2(e, ct=ct, n0=n0, bA=bA):
                e.activation(A1[:, bA:bA + TOK], A1[:, A1_dt:A1_dt + TOK], AF.Exp,
                             scale=ACOL[:, ct * DST + n0:ct * DST + n0 + 1])
                return e.activation(A1[:, bA + TOK:bA + 2 * TOK],
                                    A1[:, A1_dt:A1_dt + TOK], AF.Exp,
                                    scale=ACOL[:, ct * DST + n0 + 1:ct * DST + n0 + 2])
            S.s(dA2, r=["a1dt", "acol", ntok], w=[ntok], tag=f"sc.dA{ct}.{n0}")
            def ldb(e, n0=n0, hxb0=hxb0, bB=bB):
                insts = []
                for hx in range(NHXB):
                    sap = ap4(b_d, ((hxb0 + hx) * DST + n0) * TOK,
                              [[0, PB], [TOK, 2], [1, TOK]])
                    insts.append(e.dma_start(
                        out=A1[hx * PB:(hx + 1) * PB, bB:bB + 2 * TOK], in_=sap))
                return insts
            S.emit("gpsimd", ldb, [f"bd:{(hxb0 * DST) // 128}", ntok], [ntok],
                   dma=True, tag=f"sc.B{ct}.{n0}", nmul=NHXB)
            def dbx2(e, bB=bB):
                i0 = A1[:, A1_xc:A1_xc + TOK].unsqueeze(1).broadcast_to((128, 2, TOK))
                i1 = A1[:, bB:bB + 2 * TOK].rearrange("p (n t) -> p n t", n=2)
                o = A1[:, bB:bB + 2 * TOK].rearrange("p (n t) -> p n t", n=2)
                return e.tensor_tensor(o, i0, i1, ALU.mult)
            S.v(dbx2, r=["a1xc", ntok], w=[ntok], tag=f"sc.dBx{ct}.{n0}")
            for h2 in range(2):
                S.v(lambda e, bA=bA, bB=bB, bH=bH, h2=h2: e.tensor_tensor_scan(
                        A1[:, bH + h2 * TOK:bH + (h2 + 1) * TOK],
                        A1[:, bA + h2 * TOK:bA + (h2 + 1) * TOK],
                        A1[:, bB + h2 * TOK:bB + (h2 + 1) * TOK],
                        initial=0.0, op0=ALU.mult, op1=ALU.add),
                    r=[ntok], w=[ntok], tag=f"sc.tts{ct}.{n0 + h2}")
            def sph2(e, ct=ct, n0=n0, bH=bH):
                dst = ap4(h_d, ct * 128 * DST * TOK + n0 * TOK,
                          [[DST * TOK, 128], [TOK, 2], [1, TOK]])
                return e.dma_start(out=dst, in_=A1[:, bH:bH + 2 * TOK])
            S.dS(sph2, r=[ntok], w=[f"hd:{ct}"], tag=f"sc.sph{ct}.{n0}")
            def sco2(e, ct=ct, n0=n0, bH=bH):
                e.tensor_copy(SCO[:, ct * DST + n0:ct * DST + n0 + 1],
                              A1[:, bH + TOK - 1:bH + TOK])
                return e.tensor_copy(SCO[:, ct * DST + n0 + 1:ct * DST + n0 + 2],
                                     A1[:, bH + 2 * TOK - 1:bH + 2 * TOK])
            S.g(sco2, r=[ntok, "sco"], w=["sco"], tag=f"sc.sco{ct}.{n0}")

    zc_tiles = [("z", PT_Z0 + i) for i in range(DI // 128)] + \
               [("C", PT_C0 + i) for i in range(DI // 128)]
    zi = iter(zc_tiles)
    for ct in range(NCT):
        for _ in range(2):
            nt = next(zi, None)
            if nt is not None:
                do_proj_tile(*nt)
        scan_ct(ct)
    for nt in zi:
        do_proj_tile(*nt)

    S.dS(lambda e: e.dma_start(
            out=ap4(cc_in.ap(), 0, [[NCT * DST, 128], [1, NCT * DST]]),
            in_=SCO[:, :]),
         r=["sco"], w=["ccin"], tag="p3.spS")
    S.dS(lambda e: e.dma_start(
            out=ap4(cc_in.ap(), DI * DST, [[NCT, 128], [1, NCT]]),
            in_=ATOT[:, :]),
         r=["atot"], w=["ccin"], tag="p3.spT")
    S.emit("gpsimd", lambda e: e.collective_compute(
            "AllGather", ALU.bypass, replica_groups=[list(range(NCORES))],
            ins=[cc_in.ap().opt()], outs=[cc_out.ap().opt()]),
        r=["ccin"], w=["ccout"], tag="p3.ag", force=True)
    for j in range(NCORES):
        def ldt(e, j=j):
            src = ap4(cc_out.ap(), j * CCN + DI * DST, [[NCT, 128], [1, NCT]])
            return e.dma_start(out=TK8[:, j * NCT:(j + 1) * NCT], in_=src)
        S.dR(ldt, r=["ccout", "sco"], w=["sco"], tag=f"p3.t{j}")
    for j in range(NCORES):
        S.v(lambda e, j=j: e.tensor_scalar(GJ[:, :], TK8[:, 0:NCT],
                                           GMS[:, j * 8:j * 8 + 1], None, ALU.mult),
            r=["sco", "gms", "gj"], w=["gj"], tag=f"p3.g0.{j}")
        for k in range(1, NCORES):
            S.v(lambda e, j=j, k=k: e.scalar_tensor_tensor(
                    GJ[:, :], TK8[:, k * NCT:(k + 1) * NCT],
                    GMS[:, j * 8 + k:j * 8 + k + 1], GJ[:, :],
                    op0=ALU.mult, op1=ALU.add),
                r=["sco", "gms", "gj"], w=["gj"], tag=f"p3.g{k}.{j}")
        S.v(lambda e: e.tensor_tensor(
                TRM[:, :].rearrange("p (c n) -> p c n", n=DST),
                ACOL[:, :].rearrange("p (c n) -> p c n", n=DST),
                GJ[:, :].unsqueeze(-1).broadcast_to((128, NCT, DST)), ALU.mult),
            r=["acol", "gj", "yct"], w=["yct"], tag=f"p3.arg{j}")
        S.s(lambda e: e.activation(TRM[:, :], TRM[:, :], AF.Exp),
            r=["yct"], w=["yct"], tag=f"p3.exp{j}")
        def lds(e, j=j):
            src = ap4(cc_out.ap(), j * CCN, [[NCT * DST, 128], [1, NCT * DST]])
            return e.dma_start(out=ST1[:, 0:NCT * DST], in_=src)
        S.dR(lds, r=["ccout", "st1x"], w=["st1x"], tag=f"p3.s{j}")
        S.v(lambda e, j=j: e.scalar_tensor_tensor(
                TRM[:, :], TRM[:, :], M2S[:, j:j + 1], ST1[:, 0:NCT * DST],
                op0=ALU.mult, op1=ALU.mult),
            r=["yct", "m2s", "st1x"], w=["yct"], tag=f"p3.msk{j}")
        S.v(lambda e: e.tensor_tensor(HIN[:, :], HIN[:, :], TRM[:, :], ALU.add),
            r=["hin", "yct"], w=["hin"], tag=f"p3.acc{j}")

    PGS = min(6, cfg.get("PGS", 6))
    npg = (KD + PGS - 1) // PGS

    def ldwout(e, kct, par):
        sap = ap4(woutt, kct * 128 * D, [[D, 128], [1, D]])
        return e.dma_start(out=WF[:, par * 2048:par * 2048 + D], in_=sap)

    def p4_wload(kct, th, pg):
        par = kct % 2
        if th == 0 and pg == 0:
            S.dW(lambda e, kct=kct, par=par: ldwout(e, kct, par),
                 r=[f"wf:{par}"], w=[f"wf:{par}"], tag=f"p4.w{kct}")
            split_panel(par, D, f"p4.ws{kct}", lo_eng="gpsimd")
            if npg > 1 or NTH > 1:
                S.dS(lambda e, kct=kct, par=par: e.dma_start(
                        out=wouth_d[kct * 128:(kct + 1) * 128, :],
                        in_=WH[:, par * 2048:par * 2048 + D]),
                     r=[f"whl:{par}"], w=["woh"], tag=f"p4.swh{kct}")
                S.dS(lambda e, kct=kct, par=par: e.dma_start(
                        out=woutl_d[kct * 128:(kct + 1) * 128, :],
                        in_=WL[:, par * 2048:par * 2048 + D]),
                     r=[f"whl:{par}"], w=["wol"], tag=f"p4.swl{kct}")
        else:
            S.dW(lambda e, kct=kct, par=par: e.dma_start(
                    out=WH[:, par * 2048:par * 2048 + D],
                    in_=wouth_d[kct * 128:(kct + 1) * 128, :]),
                 r=["woh", f"whl:{par}"], w=[f"whl:{par}"], tag=f"p4.rwh{kct}.{pg}.{th}")
            S.dW(lambda e, kct=kct, par=par: e.dma_start(
                    out=WL[:, par * 2048:par * 2048 + D],
                    in_=woutl_d[kct * 128:(kct + 1) * 128, :]),
                 r=["wol", f"whl:{par}"], w=[f"whl:{par}"], tag=f"p4.rwl{kct}.{pg}.{th}")

    def p4_mm(kct, th, pg):
        par = kct % 2
        p_lo = pg * PGS
        p_n = min(PGS, KD - p_lo)
        def fmm(e, kct=kct, par=par, p_lo=p_lo, p_n=p_n):
            inst = None
            for pi in range(p_n):
                p = p_lo + pi
                wh = WH[:, par * 2048 + p * 128:par * 2048 + (p + 1) * 128]
                wl = WL[:, par * 2048 + p * 128:par * 2048 + (p + 1) * 128]
                inst = mm3(e, PS[pi][:, 0:TH], wh, wl,
                           A2[:, kct * TH:(kct + 1) * TH],
                           A3[:, kct * TH:(kct + 1) * TH],
                           kct == 0, kct == NCT - 1)
            return inst
        S.t(fmm, r=[f"whl:{par}", "a2", "a3"],
            w=[f"ps{pi}" for pi in range(p_n)], tag=f"p4.mm{pg}.{kct}.{th}")

    def p4_evac(th, pg):
        t0 = th * TH
        p_lo = pg * PGS
        p_n = min(PGS, KD - p_lo)
        for pi in range(p_n):
            p = p_lo + pi
            S.dR(lambda e, p=p, t0=t0: e.dma_start(
                    out=ST1[:, 0:TH],
                    in_=x_in[p * 128:(p + 1) * 128, 3 + t0:3 + t0 + TH]),
                 r=["st1x"], w=["st1x"], tag=f"p4.x{p}.{th}")
            S.v(lambda e: e.tensor_tensor(ST2[:, 0:TH], ST1[:, 0:TH], PS[6][:, 0:TH],
                                          ALU.subtract),
                r=["st1x", "ps6", "st2f"], w=["st2f"], tag=f"p4.sub{p}.{th}")
            S.v(lambda e: e.tensor_tensor(ST2[:, 0:TH], ST2[:, 0:TH], PS[7][:, 0:TH],
                                          ALU.mult),
                r=["st2f", "ps7"], w=["st2f"], tag=f"p4.lnx{p}.{th}")
            S.v(lambda e, pi=pi, p=p, t0=t0: e.tensor_tensor(
                    hres(p)[:, t0:t0 + TH], PS[pi][:, 0:TH], ST2[:, 0:TH], ALU.add),
                r=[f"ps{pi}", "st2f", f"hr:{p}"], w=[f"hr:{p}"], tag=f"p4.hres{p}.{th}")

    for th in range(NTH):
        t0 = th * TH
        S.t(lambda e, t0=t0: e.matmul(PS[6][:, 0:TH], ONE1[:, :],
                                      ROWS[0:1, R_A + 3 + t0:R_A + 3 + t0 + TH],
                                      start=True, stop=True),
            r=["rows", "one1", "ps6"], w=["ps6"], tag=f"p4.repm.{th}")
        S.t(lambda e, t0=t0: e.matmul(PS[7][:, 0:TH], ONE1[:, :],
                                      ROWS[0:1, R_C + 3 + t0:R_C + 3 + t0 + TH],
                                      start=True, stop=True),
            r=["rows", "one1", "ps7"], w=["ps7"], tag=f"p4.reps.{th}")
        for ct in range(NCT):
            p4_wload(ct, th, 0)
            S.dR(lambda e, ct=ct, t0=t0: e.dma_start(
                    out=YCT[:, 0:TH], in_=xcd_d[ct * 128:(ct + 1) * 128, t0:t0 + TH]),
                 r=[f"xcd:{ct}", "yct"], w=["yct"], tag=f"p4.xcd{ct}.{th}")
            S.dR(lambda e, ct=ct, t0=t0: e.dma_start(
                    out=ST1[:, 0:TH], in_=a_d[ct * 128:(ct + 1) * 128, t0:t0 + TH]),
                 r=[f"ad:{ct}", "st1x"], w=["st1x"], tag=f"p4.a{ct}.{th}")
            hslots = [A4[:, 0:TH], A4[:, TH:2 * TH]]
            cslots = [A5[:, 0:TH], A5[:, TH:2 * TH]]
            htoks = ["a4:0", "a4:1"]
            ctoks = ["a5:0", "a5:1"]
            if TOK // 2 >= TH:
                hslots += [PHS[:, :].bitcast(F32)[:, 0:TH],
                           DTRH[:, :].bitcast(F32)[:, 0:TH]]
                cslots += [PLS[:, :].bitcast(F32)[:, 0:TH],
                           DTRL[:, :].bitcast(F32)[:, 0:TH]]
                htoks += ["ph:0", "ph:1"]
                ctoks += ["ph:0", "ph:1"]
            npar4 = len(hslots)
            for n in range(DST):
                par = n % npar4
                htok, ctok = htoks[par], ctoks[par]
                hsl = hslots[par]
                csl = cslots[par]
                def ldh(e, ct=ct, n=n, t0=t0, hsl=hsl):
                    sap = ap4(h_d, ct * 128 * DST * TOK + n * TOK + t0,
                              [[DST * TOK, 128], [1, TH]])
                    return e.dma_start(out=hsl, in_=sap)
                S.dW(ldh, r=[f"hd:{ct}", htok], w=[htok], tag=f"p4.h{ct}.{n}.{th}")
                def ldc(e, ct=ct, n=n, t0=t0, csl=csl):
                    sap = ap4(cpl_d, (ct * HPC) * DST * TOK + n * TOK + t0,
                              [[DST * TOK, HPC], [0, DST], [1, TH]])
                    return e.dma_start(out=csl, in_=sap)
                S.dW(ldc, r=[f"cpl:{ct}", ctok], w=[ctok], tag=f"p4.c{ct}.{n}.{th}")
                S.s(lambda e, ct=ct, n=n: e.activation(
                        ST2[:, 0:TH], ST1[:, 0:TH], AF.Exp,
                        scale=ACOL[:, ct * DST + n:ct * DST + n + 1]),
                    r=["st1x", "acol", "st2f"], w=["st2f"], tag=f"p4.f{ct}.{n}.{th}")
                S.v(lambda e, ct=ct, n=n, hsl=hsl: e.scalar_tensor_tensor(
                        hsl, ST2[:, 0:TH], HIN[:, ct * DST + n:ct * DST + n + 1],
                        hsl, op0=ALU.mult, op1=ALU.add),
                    r=["st2f", "hin", htok], w=[htok], tag=f"p4.hf{ct}.{n}.{th}")
                S.g(lambda e, hsl=hsl, csl=csl: e.tensor_tensor(
                        csl, hsl, csl, ALU.mult),
                    r=[htok, ctok], w=[ctok], tag=f"p4.ch{ct}.{n}.{th}")
                S.v(lambda e, csl=csl: e.tensor_tensor(
                        YCT[:, 0:TH], YCT[:, 0:TH], csl, ALU.add),
                    r=["yct", ctok], w=["yct"], tag=f"p4.y{ct}.{n}.{th}")
            S.dR(lambda e, ct=ct, t0=t0: e.dma_start(
                    out=ST1[:, 0:TH], in_=z_d[ct * 128:(ct + 1) * 128, t0:t0 + TH]),
                 r=[f"zd:{ct}", "st1x"], w=["st1x"], tag=f"p4.z{ct}.{th}")
            S.s(lambda e: e.activation(ST1[:, 0:TH], ST1[:, 0:TH], AF.Silu),
                r=["st1x"], w=["st1x"], tag=f"p4.sz{ct}.{th}")
            S.v(lambda e: e.tensor_tensor(YCT[:, 0:TH], YCT[:, 0:TH], ST1[:, 0:TH],
                                          ALU.mult),
                r=["yct", "st1x"], w=["yct"], tag=f"p4.mul{ct}.{th}")
            S.s(lambda e, ct=ct: e.activation(A2[:, ct * TH:(ct + 1) * TH], YCT[:, 0:TH],
                                              AF.Copy),
                r=["yct"], w=["a2"], tag=f"p4.hi{ct}.{th}")
            S.v(lambda e, ct=ct: e.tensor_tensor(A3[:, ct * TH:(ct + 1) * TH],
                                                 YCT[:, 0:TH],
                                                 A2[:, ct * TH:(ct + 1) * TH],
                                                 ALU.subtract),
                r=["yct", "a2"], w=["a3"], tag=f"p4.lo{ct}.{th}")
            p4_mm(ct, th, 0)
        p4_evac(th, 0)
        for pg in range(1, npg):
            for kct in range(NCT):
                p4_wload(kct, th, pg)
                p4_mm(kct, th, pg)
            p4_evac(th, pg)

    sch2 = list(range(0, TOK, 128))
    nj2 = len(sch2)
    for k in range(KD):
        S.s(lambda e, k=k: e.activation(ST1[:, 0:TOK], hres(k)[:, 0:TOK], AF.Square),
            r=[f"hr:{k}", "st1x"], w=["st1x"], tag=f"p5.sq{k}")
        def fmm(e, k=k):
            inst = None
            for j, c0 in enumerate(sch2):
                m = min(128, TOK - c0)
                inst = e.matmul(PS[6][0:m, j:j + 1], ST1[:, c0:c0 + m], ONEC[:, :],
                                start=(k == 0 and j == 0), stop=(k == KD - 1 and j == nj2 - 1),
                                skip_group_check=True)
            return inst
        S.t(fmm, r=["st1x", "onec", "ps6"], w=["ps6"], tag=f"p5.mm{k}")
    ev_cols(6, 0, sch2, TOK, "p5.ev")
    col2row(R_B, 0, nj2, TOK, "p5.r")
    S.s(lambda e: e.activation(ROWS[0:1, R_B:R_B + TOK], ROWS[0:1, R_B:R_B + TOK],
                               AF.Ln, scale=1.0 / D, bias=EPSC[0:1, :]),
        r=["rows", "epsc"], w=["rows"], tag="p5.ln")
    S.s(lambda e: e.activation(ROWS[0:1, R_B:R_B + TOK], ROWS[0:1, R_B:R_B + TOK],
                               AF.Exp, scale=-0.5),
        r=["rows"], w=["rows"], tag="p5.rsq")
    for j in range(NTC):
        S.t(lambda e, j=j: e.matmul(PS[j][:, 0:TC], ONE1[:, :],
                                    ROWS[0:1, R_B + j * TC:R_B + (j + 1) * TC],
                                    start=True, stop=True),
            r=["rows", "one1", f"ps{j}"], w=[f"ps{j}"], tag=f"p5.rep{j}")
        S.v(lambda e, j=j: e.tensor_copy(A4[:, j * TC:(j + 1) * TC], PS[j][:, 0:TC]),
            r=[f"ps{j}", "a4:0", "a4:1"], w=["a4:0", "a4:1"], tag=f"p5.repev{j}")
    for k in range(KD):
        S.v(lambda e, k=k: e.scalar_tensor_tensor(ST2[:, 0:TOK], hres(k)[:, 0:TOK],
                                                  L2C[:, k:k + 1], A4[:, 0:TOK],
                                                  op0=ALU.mult, op1=ALU.mult),
            r=[f"hr:{k}", "l2c", "a4:0", "a4:1", "st2f"], w=["st2f"], tag=f"p5.r2{k}")
        S.s(lambda e, k=k: e.activation(A2[:, k * TOK:(k + 1) * TOK], ST2[:, 0:TOK],
                                        AF.Copy),
            r=["st2f"], w=["a2"], tag=f"p5.hi{k}")
        S.v(lambda e, k=k: e.tensor_tensor(A3[:, k * TOK:(k + 1) * TOK], ST2[:, 0:TOK],
                                           A2[:, k * TOK:(k + 1) * TOK], ALU.subtract),
            r=["st2f", "a2"], w=["a3"], tag=f"p5.lo{k}")
    r2h = lambda k: A2[:, k * TOK:(k + 1) * TOK]
    r2l = lambda k: A3[:, k * TOK:(k + 1) * TOK]
    r2toks = ["a2", "a3"]

    def gup_phase(wg_ap, wu_ap, NP, spill_h, spill_l, sh_tok, sl_tok, scaled,
                  gsc, usc, pref):
        for i in range(NP):
            par = 0
            load_panel_pcols(wg_ap, i, KD * 128, par, f"{pref}.wg{i}")
            split_panel(par, KD * 128, f"{pref}.sg{i}")
            for tc in range(NTC):
                bank = 4 * (i % 2)
                proj_mm(par, r2h, r2l, r2toks, tc * TC, (tc + 1) * TC, bank,
                        f"{pref}.gmm{i}.{tc}")
                if scaled:
                    S.s(lambda e, bank=bank, tc=tc, i=i: e.activation(
                            ST1[:, tc * TC:(tc + 1) * TC], PS[bank][:, 0:TC], AF.Silu,
                            scale=gsc[:, i:i + 1]),
                        r=[f"ps{bank}", "fgsc", "st1x"], w=["st1x"],
                        tag=f"{pref}.si{i}.{tc}")
                else:
                    S.s(lambda e, bank=bank, tc=tc: e.activation(
                            ST1[:, tc * TC:(tc + 1) * TC], PS[bank][:, 0:TC], AF.Silu),
                        r=[f"ps{bank}", "st1x"], w=["st1x"], tag=f"{pref}.si{i}.{tc}")
            par = 1
            load_panel_pcols(wu_ap, i, KD * 128, par, f"{pref}.wu{i}")
            split_panel(par, KD * 128, f"{pref}.su{i}")
            for tc in range(NTC):
                bank = 4 * (i % 2) + 2
                proj_mm(par, r2h, r2l, r2toks, tc * TC, (tc + 1) * TC, bank,
                        f"{pref}.umm{i}.{tc}")
                if scaled:
                    S.v(lambda e, bank=bank, tc=tc, i=i: e.scalar_tensor_tensor(
                            ST2[:, tc * TC:(tc + 1) * TC], PS[bank][:, 0:TC],
                            usc[:, i:i + 1], ST1[:, tc * TC:(tc + 1) * TC],
                            op0=ALU.mult, op1=ALU.mult),
                        r=[f"ps{bank}", "fusc", "st1x", "st2f"], w=["st2f"],
                        tag=f"{pref}.pr{i}.{tc}")
                else:
                    S.v(lambda e, bank=bank, tc=tc: e.tensor_tensor(
                            ST2[:, tc * TC:(tc + 1) * TC], PS[bank][:, 0:TC],
                            ST1[:, tc * TC:(tc + 1) * TC], ALU.mult),
                        r=[f"ps{bank}", "st1x", "st2f"], w=["st2f"],
                        tag=f"{pref}.pr{i}.{tc}")
            ppar = i % 2
            ptok = f"ph:{ppar}"
            S.s(lambda e, ppar=ppar: e.activation(midhi(ppar), ST2[:, 0:TOK], AF.Copy),
                r=["st2f", ptok], w=[ptok], tag=f"{pref}.phi{i}")
            S.v(lambda e, ppar=ppar: e.tensor_tensor(midlo(ppar), ST2[:, 0:TOK],
                                                     midhi(ppar), ALU.subtract),
                r=["st2f", ptok], w=[ptok], tag=f"{pref}.plo{i}")
            S.dS(lambda e, i=i, ppar=ppar: e.dma_start(
                    out=spill_h[i * 128:(i + 1) * 128, :], in_=midhi(ppar)),
                 r=[ptok], w=[f"{sh_tok}:{i}"], tag=f"{pref}.sph{i}")
            S.dS(lambda e, i=i, ppar=ppar: e.dma_start(
                    out=spill_l[i * 128:(i + 1) * 128, :], in_=midlo(ppar)),
                 r=[ptok], w=[f"{sl_tok}:{i}"], tag=f"{pref}.spl{i}")

    gup_phase(wgp, wup, NMI, midh_d, midl_d, "mh", "ml", False, None, None, "p6")

    for k in range(KD):
        S.dR(lambda e, k=k: e.dma_start(out=ST1[:, 0:TOK],
                                        in_=x_in[k * 128:(k + 1) * 128, 3:3 + TOK]),
             r=["st1x"], w=["st1x"], tag=f"p6.x{k}")
        S.v(lambda e, k=k: e.tensor_tensor(hres(k)[:, 0:TOK], hres(k)[:, 0:TOK],
                                           ST1[:, 0:TOK], ALU.add),
            r=[f"hr:{k}", "st1x"], w=[f"hr:{k}"], tag=f"p6.hx{k}")

    def down_phase(wd_ap, wh_spill, wl_spill, whtok, wltok, NP, mh_tok, ml_tok,
                   mh_dram, ml_dram, evac, pref):
        first = [True]
        PGS = cfg.get("PGS", 8)
        npg = (KD + PGS - 1) // PGS
        for pg in range(npg):
            p_lo = pg * PGS
            p_n = min(PGS, KD - p_lo)
            for tc in range(NTC):
                for i in range(NP):
                    par = i % 2
                    if first[0]:
                        def ldwd(e, i=i, par=par):
                            sap = ap4(wd_ap, i * 128 * D, [[D, 128], [1, D]])
                            return e.dma_start(out=WF[:, par * 2048:par * 2048 + D],
                                               in_=sap)
                        S.dW(ldwd, r=[f"wf:{par}"], w=[f"wf:{par}"], tag=f"{pref}.wd{i}")
                        split_panel(par, D, f"{pref}.wds{i}")
                        if npg * NTC > 1:
                            S.dS(lambda e, i=i, par=par: e.dma_start(
                                    out=wh_spill[i * 128:(i + 1) * 128, :],
                                    in_=WH[:, par * 2048:par * 2048 + D]),
                                 r=[f"whl:{par}"], w=[whtok], tag=f"{pref}.swh{i}")
                            S.dS(lambda e, i=i, par=par: e.dma_start(
                                    out=wl_spill[i * 128:(i + 1) * 128, :],
                                    in_=WL[:, par * 2048:par * 2048 + D]),
                                 r=[f"whl:{par}"], w=[wltok], tag=f"{pref}.swl{i}")
                    else:
                        S.dW(lambda e, i=i, par=par: e.dma_start(
                                out=WH[:, par * 2048:par * 2048 + D],
                                in_=wh_spill[i * 128:(i + 1) * 128, :]),
                             r=[whtok, f"whl:{par}"], w=[f"whl:{par}"],
                             tag=f"{pref}.rwh{i}.{pg}.{tc}")
                        S.dW(lambda e, i=i, par=par: e.dma_start(
                                out=WL[:, par * 2048:par * 2048 + D],
                                in_=wl_spill[i * 128:(i + 1) * 128, :]),
                             r=[wltok, f"whl:{par}"], w=[f"whl:{par}"],
                             tag=f"{pref}.rwl{i}.{pg}.{tc}")
                    ppar = i % 2
                    ptok = f"ph:{ppar}"
                    S.dR(lambda e, i=i, tc=tc, ppar=ppar: e.dma_start(
                            out=midhi(ppar)[:, 0:TC],
                            in_=mh_dram[i * 128:(i + 1) * 128, tc * TC:(tc + 1) * TC]),
                         r=[f"{mh_tok}:{i}", ptok], w=[ptok],
                         tag=f"{pref}.rmh{i}.{pg}.{tc}")
                    S.dR(lambda e, i=i, tc=tc, ppar=ppar: e.dma_start(
                            out=midlo(ppar)[:, 0:TC],
                            in_=ml_dram[i * 128:(i + 1) * 128, tc * TC:(tc + 1) * TC]),
                         r=[f"{ml_tok}:{i}", ptok], w=[ptok],
                         tag=f"{pref}.rml{i}.{pg}.{tc}")
                    def fmm(e, i=i, par=par, ppar=ppar, p_lo=p_lo, p_n=p_n, NP=NP):
                        inst = None
                        for pi in range(p_n):
                            p = p_lo + pi
                            wh = WH[:, par * 2048 + p * 128:par * 2048 + (p + 1) * 128]
                            wl = WL[:, par * 2048 + p * 128:par * 2048 + (p + 1) * 128]
                            inst = mm3(e, PS[pi][:, 0:TC], wh, wl,
                                       midhi(ppar)[:, 0:TC], midlo(ppar)[:, 0:TC],
                                       i == 0, i == NP - 1)
                        return inst
                    S.t(fmm, r=[f"whl:{par}", ptok],
                        w=[f"ps{pi}" for pi in range(p_n)],
                        tag=f"{pref}.dmm{pg}.{tc}.{i}")
                for pi in range(p_n):
                    evac(p_lo + pi, pi, tc)
                first[0] = False

    def evac6(p, pi, tc):
        S.v(lambda e: e.tensor_tensor(
                hres(p)[:, tc * TC:(tc + 1) * TC], PS[pi][:, 0:TC],
                hres(p)[:, tc * TC:(tc + 1) * TC], ALU.add),
            r=[f"ps{pi}", f"hr:{p}"], w=[f"hr:{p}"], tag=f"p6.x1{p}.{tc}")

    down_phase(wdt, wdh_d, wdl_d, "wdh", "wdl", NMI, "mh", "ml",
               midh_d, midl_d, evac6, "p6d")

    for k in range(KD):
        S.s(lambda e, k=k: e.activation(ST1[:, 0:TOK], hres(k)[:, 0:TOK], AF.Square),
            r=[f"hr:{k}", "st1x"], w=["st1x"], tag=f"p7.sq{k}")
        def fmm(e, k=k):
            inst = None
            for j, c0 in enumerate(sch2):
                m = min(128, TOK - c0)
                e.matmul(PS[6][0:m, j:j + 1], hres(k)[:, c0:c0 + m], ONEC[:, :],
                         start=(k == 0 and j == 0), stop=(k == KD - 1 and j == nj2 - 1),
                         skip_group_check=True)
                inst = e.matmul(PS[7][0:m, j:j + 1], ST1[:, c0:c0 + m], ONEC[:, :],
                                start=(k == 0 and j == 0), stop=(k == KD - 1 and j == nj2 - 1),
                                skip_group_check=True)
            return inst
        S.t(fmm, r=[f"hr:{k}", "st1x", "onec", "ps6", "ps7"], w=["ps6", "ps7"],
            tag=f"p7.mm{k}")
    ev_cols(6, 0, sch2, TOK, "p7.ev1")
    ev_cols(7, 32, sch2, TOK, "p7.ev2")
    col2row(R_A, 0, nj2, TOK, "p7.ra")
    col2row(R_B, 32, nj2, TOK, "p7.rb")
    S.v(lambda e: e.tensor_scalar(ROWS[0:1, R_A:R_A + TOK], ROWS[0:1, R_A:R_A + TOK],
                                  1.0 / D, None, ALU.mult),
        r=["rows"], w=["rows"], tag="p7.mean")
    S.v(lambda e: e.tensor_tensor(ROWS[0:1, R_C:R_C + TOK], ROWS[0:1, R_A:R_A + TOK],
                                  ROWS[0:1, R_A:R_A + TOK], ALU.mult),
        r=["rows"], w=["rows"], tag="p7.m2")
    S.v(lambda e: e.scalar_tensor_tensor(ROWS[0:1, R_B:R_B + TOK], ROWS[0:1, R_B:R_B + TOK],
                                         1.0 / D, ROWS[0:1, R_C:R_C + TOK],
                                         op0=ALU.mult, op1=ALU.subtract),
        r=["rows"], w=["rows"], tag="p7.var")
    S.s(lambda e: e.activation(ROWS[0:1, R_C:R_C + TOK], ROWS[0:1, R_B:R_B + TOK],
                               AF.Ln, bias=EPSC[0:1, :]),
        r=["rows", "epsc"], w=["rows"], tag="p7.ln")
    S.s(lambda e: e.activation(ROWS[0:1, R_C:R_C + TOK], ROWS[0:1, R_C:R_C + TOK],
                               AF.Exp, scale=-0.5),
        r=["rows"], w=["rows"], tag="p7.s")
    for j in range(NTC):
        S.t(lambda e, j=j: e.matmul(PS[0][:, 0:TC], ONE1[:, :],
                                    ROWS[0:1, R_A + j * TC:R_A + (j + 1) * TC],
                                    start=True, stop=True),
            r=["rows", "one1", "ps0"], w=["ps0"], tag=f"p7.repm{j}")
        S.v(lambda e, j=j: e.tensor_copy(A4[:, j * TC:(j + 1) * TC], PS[0][:, 0:TC]),
            r=["ps0", "a4:0", "a4:1"], w=["a4:0", "a4:1"], tag=f"p7.evm{j}")
        S.t(lambda e, j=j: e.matmul(PS[1][:, 0:TC], ONE1[:, :],
                                    ROWS[0:1, R_C + j * TC:R_C + (j + 1) * TC],
                                    start=True, stop=True),
            r=["rows", "one1", "ps1"], w=["ps1"], tag=f"p7.repsd{j}")
        S.v(lambda e, j=j: e.tensor_copy(A5[:, j * TC:(j + 1) * TC], PS[1][:, 0:TC]),
            r=["ps1", "a5:0", "a5:1"], w=["a5:0", "a5:1"], tag=f"p7.evsd{j}")
    for k in range(KD):
        S.v(lambda e, k=k: e.tensor_tensor(ST2[:, 0:TOK], hres(k)[:, 0:TOK],
                                           A4[:, 0:TOK], ALU.subtract),
            r=[f"hr:{k}", "a4:0", "a4:1", "st2f"], w=["st2f"], tag=f"p7.sub{k}")
        S.v(lambda e: e.tensor_tensor(ST2[:, 0:TOK], ST2[:, 0:TOK], A5[:, 0:TOK],
                                      ALU.mult),
            r=["st2f", "a5:0", "a5:1"], w=["st2f"], tag=f"p7.l2{k}")
        S.s(lambda e, k=k: e.activation(A2[:, k * TOK:(k + 1) * TOK], ST2[:, 0:TOK],
                                        AF.Copy),
            r=["st2f"], w=["a2"], tag=f"p7.hi{k}")
        S.v(lambda e, k=k: e.tensor_tensor(A3[:, k * TOK:(k + 1) * TOK], ST2[:, 0:TOK],
                                           A2[:, k * TOK:(k + 1) * TOK], ALU.subtract),
            r=["st2f", "a2"], w=["a3"], tag=f"p7.lo{k}")

    gup_phase(fgp, fup, NFI, fmh_d, fml_d, "fh", "fl", True, FGSC, FUSC, "p7")

    outtoks = []

    def evac7(p, pi, tc):
        S.v(lambda e, pi=pi, p=p, tc=tc: e.scalar_tensor_tensor(
                ST2[:, 0:TC], PS[pi][:, 0:TC], FDSC[:, p:p + 1],
                hres(p)[:, tc * TC:(tc + 1) * TC], op0=ALU.mult, op1=ALU.add),
            r=[f"ps{pi}", "fdsc", f"hr:{p}", "st2f"], w=["st2f"],
            tag=f"p7.x2{p}.{tc}")
        ot = f"out:{p}:{tc}"
        S.dS(lambda e, p=p, tc=tc: e.dma_start(
                out=out_d[p * 128:(p + 1) * 128, tc * TC:(tc + 1) * TC],
                in_=ST2[:, 0:TC]),
             r=["st2f"], w=[ot], tag=f"p7.out{p}.{tc}")
        outtoks.append(ot)

    down_phase(fdt, fdh_d, fdl_d, "fdh", "fdl", NFI, "fh", "fl",
               fmh_d, fml_d, evac7, "p7d")

    S.g(lambda e: e.memset(ONEC[:, :], 1.0), r=outtoks, w=["onec"], tag="final")

    S.finalize(nc)
    return nc


def _prep_inputs(cfg, A):
    D, TOK = cfg["D"], cfg["TOK"]
    x = np.asarray(A["L0_attn_hidden_states"], dtype=np.float32)
    B = x.shape[0]
    per = NCORES // B
    assert x.shape[1] == per * TOK
    ct = np.ascontiguousarray

    KD = D // 128

    def pack_panels(w):
        wt = np.asarray(w, np.float32).T
        Dd, P = wt.shape
        npt = P // 128
        kd = Dd // 128
        v = wt.reshape(kd, 128, npt, 128).transpose(2, 1, 0, 3)
        return np.ascontiguousarray(v.reshape(npt * 128, kd * 128))

    shared = dict(
        w1p=pack_panels(A["L0_attn_in_proj_w"]),
        dtwt=ct(np.asarray(A["L0_attn_dt_proj_w"], np.float32).T),
        dtb=ct(np.asarray(A["L0_attn_dt_proj_b"], np.float32)),
        convw=ct(np.asarray(A["L0_attn_conv_w"], np.float32).reshape(cfg["DI"], cfg["DCONV"])),
        convb=ct(np.asarray(A["L0_attn_conv_b"], np.float32)),
        alog=ct(np.asarray(A["L0_attn_A_log"], np.float32)),
        dpar=ct(np.asarray(A["L0_attn_D_param"], np.float32)),
        woutt=ct(np.asarray(A["L0_attn_out_proj_w"], np.float32).T),
        ln1w=ct(np.asarray(A["L0_attn_ln1_w"], np.float32)),
        ln2w=ct(np.asarray(A["L0_attn_ln2_w"], np.float32)),
        wgp=pack_panels(A["L0_attn_gate_w"]),
        wup=pack_panels(A["L0_attn_up_w"]),
        wdt=ct(np.asarray(A["L0_attn_down_w"], np.float32).T),
        fgp=pack_panels(A["L0_ffn_gate_w"]),
        fup=pack_panels(A["L0_ffn_up_w"]),
        fdt=ct(np.asarray(A["L0_ffn_down_w"], np.float32).T),
        fgs=ct(np.asarray(A["L0_ffn_gate_s"], np.float32)),
        fus=ct(np.asarray(A["L0_ffn_up_s"], np.float32)),
        fds=ct(np.asarray(A["L0_ffn_down_s"], np.float32)),
    )
    in_maps = []
    for q in range(NCORES):
        b, q4 = q // per, q % per
        lo = q4 * TOK
        x3 = np.zeros((3, D), np.float32) if q4 == 0 else x[b, lo - 3:lo]
        xs = np.concatenate([x3, x[b, lo:lo + TOK]], axis=0)
        gm = np.zeros((8, 8), np.float32)
        m2 = np.zeros((8,), np.float32)
        for j in range(per):
            jj = b * per + j
            if j < q4:
                m2[jj] = 1.0
                for k in range(j + 1, q4):
                    gm[jj, b * per + k] = 1.0
        im = dict(shared)
        im["x_in"] = ct(xs.T)
        im["gmc"] = ct(np.repeat(gm.reshape(1, 64), 128, axis=0))
        im["m2c"] = ct(np.repeat(m2.reshape(1, 8), 128, axis=0))
        in_maps.append(im)
    return in_maps


def _assemble(cfg, results):
    per = NCORES // 2
    outs = [np.asarray(r["out"]) for r in results]
    bs = [np.concatenate([outs[b * per + q].T for q in range(per)], axis=0)
          for b in range(2)]
    return np.stack(bs, axis=0).astype(np.float32)


def kernel(**inputs):
    cfg = real_cfg()
    in_maps = _prep_inputs(cfg, inputs)
    nc = build_nc(cfg)
    res = run_bass_kernel_spmd(nc, in_maps, core_ids=list(range(NCORES)))
    return _assemble(cfg, res.results)


# version 13
# speedup vs baseline: 1.3404x; 1.1089x over previous
import math
import numpy as np
import concourse.bass as bass
import concourse.mybir as mybir
from concourse.bass_utils import run_bass_kernel_spmd

F32 = mybir.dt.float32
BF16 = mybir.dt.bfloat16
AF = mybir.ActivationFunctionType
ALU = mybir.AluOpType
AX = mybir.AxisListType

NCORES = 8


def real_cfg():
    return dict(D=2048, DI=4096, DXB=1024, DST=16, DCONV=4, DTR=128,
                MI=8192, FI=5632, TOK=1024, EPS=1e-5)


class Sched:
    COMPUTE = ("tensor", "vector", "scalar", "gpsimd", "sync")

    def __init__(self):
        self.ops = []
        self.lastw = {}
        self.readers = {}

    def emit(self, eng, fn, r=(), w=(), dma=False, tag="", force=False, nmul=1):
        i = len(self.ops)
        deps = set()
        for t in r:
            if t in self.lastw:
                deps.add(self.lastw[t])
        for t in w:
            if t in self.lastw:
                deps.add(self.lastw[t])
            for j in self.readers.get(t, ()):
                deps.add(j)
        deps.discard(i)
        self.ops.append(dict(i=i, eng=eng, fn=fn, dma=dma, deps=deps, tag=tag,
                             need_inc=force or dma, seq=None, nmul=nmul))
        for t in r:
            self.readers.setdefault(t, []).append(i)
        for t in w:
            self.lastw[t] = i
            self.readers[t] = []
        return i

    def stream_of(self, rec):
        return rec["eng"] + ".q" if rec["dma"] else rec["eng"]

    NLANES = 8

    def finalize(self, nc):
        ops = self.ops
        for rec in ops:
            for d in rec["deps"]:
                ops[d]["need_inc"] = True
        seqno = {}
        lane_rot = {}
        for rec in ops:
            if not rec["need_inc"]:
                continue
            if rec["dma"]:
                q = rec["eng"] + ".q"
                lane = lane_rot.get(q, 0)
                lane_rot[q] = (lane + 1) % self.NLANES
                s = (q, lane)
            else:
                s = self.stream_of(rec)
            rec["stream"] = s
            prev = seqno.get(s, 0)
            rec["gate"] = prev if rec["dma"] else None
            seqno[s] = prev + rec.get("nmul", 1)
            rec["seq"] = seqno[s]
        sems = {}
        for s in sorted(seqno, key=str):
            nm = s if isinstance(s, str) else f"{s[0]}_{s[1]}"
            sems[s] = nc.semaphore("sem_" + nm.replace(".", "_")).__enter__()
        per_eng = {e: [] for e in self.COMPUTE}
        waited = {e: {} for e in self.COMPUTE}
        for rec in ops:
            waits = []
            for d in sorted(rec["deps"]):
                p = ops[d]
                if not p["need_inc"]:
                    continue
                s = p["stream"] if "stream" in p else self.stream_of(p)
                v = p["seq"] * (16 if p["dma"] else 1)
                if waited[rec["eng"]].get(s, 0) < v:
                    waited[rec["eng"]][s] = v
                    waits.append((s, v))
            if rec["dma"] and rec["gate"]:
                s = rec["stream"]
                v = rec["gate"] * 16
                if waited[rec["eng"]].get(s, 0) < v:
                    waited[rec["eng"]][s] = v
                    waits.append((s, v))
            per_eng[rec["eng"]].append((rec, waits))

        def mk(eng_name):
            def body(e):
                for rec, wl in per_eng[eng_name]:
                    for (s, v) in wl:
                        e.wait_ge(sems[s], v)
                    inst = rec["fn"](e)
                    if rec["need_inc"]:
                        assert inst is not None, f"op '{rec['tag']}' returned None"
                        amt = 16 if rec["dma"] else 1
                        insts = inst if isinstance(inst, list) else [inst]
                        assert len(insts) == rec.get("nmul", 1), rec["tag"]
                        for it in insts:
                            it.then_inc(sems[rec["stream"]], amt)
            return body

        with nc.Block() as block:
            for eng_name in self.COMPUTE:
                if per_eng[eng_name]:
                    getattr(block, eng_name)(mk(eng_name))

    def t(self, fn, r=(), w=(), tag=""):
        return self.emit("tensor", fn, r, w, tag=tag)

    def v(self, fn, r=(), w=(), tag=""):
        return self.emit("vector", fn, r, w, tag=tag)

    def s(self, fn, r=(), w=(), tag=""):
        return self.emit("scalar", fn, r, w, tag=tag)

    def g(self, fn, r=(), w=(), tag=""):
        return self.emit("gpsimd", fn, r, w, tag=tag)

    def dW(self, fn, r=(), w=(), tag="", nmul=1):
        return self.emit("sync", fn, r, w, dma=True, tag=tag, nmul=nmul)

    def dR(self, fn, r=(), w=(), tag="", nmul=1):
        return self.emit("scalar", fn, r, w, dma=True, tag=tag, nmul=nmul)

    def dS(self, fn, r=(), w=(), tag=""):
        return self.emit("gpsimd", fn, r, w, dma=True, tag=tag)


def ap4(base_ap, offset, dims):
    return bass.AP(base_ap.tensor, base_ap.offset + offset, [list(d) for d in dims])


def build_nc(cfg):
    D, DI, DXB, DST = cfg["D"], cfg["DI"], cfg["DXB"], cfg["DST"]
    DCONV, DTR, MI, FI = cfg["DCONV"], cfg["DTR"], cfg["MI"], cfg["FI"]
    TOK, EPS = cfg["TOK"], cfg["EPS"]
    TOKP = TOK + 3
    KD = D // 128
    NCT = DI // 128
    H = DI // DST
    REP = DI // DXB
    PROJ = 2 * DI + 2 * DXB + DTR
    NPT = PROJ // 128
    NXT = DXB // 128
    PT_Z0, PT_X0 = 0, DI // 128
    PT_B0 = PT_X0 + NXT
    PT_C0 = PT_B0 + NXT
    PT_DT = PT_C0 + DI // 128
    HPC = 128 // DST
    NHXB = max(1, HPC // REP)
    TC = min(cfg.get("TCAP", 512), TOK)
    NTC = TOK // TC
    TH = TC
    NTH = TOK // TH
    SC = min(cfg.get("SCAP", 64), TOK, TH)
    NSC = TH // SC
    NMI, NFI = MI // 128, FI // 128
    assert D % 128 == 0 and DI % 128 == 0 and DXB % 128 == 0 and PROJ % 128 == 0
    assert REP * DST * NHXB == 128 and HPC * DST == 128

    nc = bass.Bass("TRN2", target_bir_lowering=False, debug=False)

    def di(name, shape):
        return nc.dram_tensor(name, shape, F32, kind="ExternalInput").ap()

    x_in = di("x_in", [D, TOKP])
    w1p = di("w1p", [NPT * 128, KD * 128])
    dtwt = di("dtwt", [DTR, DI])
    dtb = di("dtb", [DI])
    convw = di("convw", [DI, DCONV])
    convb = di("convb", [DI])
    alog = di("alog", [DI, DST])
    dpar = di("dpar", [DI])
    woutt = di("woutt", [DI, D])
    ln1w = di("ln1w", [D])
    ln2w = di("ln2w", [D])
    wgp = di("wgp", [NMI * 128, KD * 128])
    wup = di("wup", [NMI * 128, KD * 128])
    wdt = di("wdt", [MI, D])
    fgp = di("fgp", [NFI * 128, KD * 128])
    fup = di("fup", [NFI * 128, KD * 128])
    fdt = di("fdt", [FI, D])
    fgs = di("fgs", [FI])
    fus = di("fus", [FI])
    fds = di("fds", [D])
    gmc = di("gmc", [128, 64])
    m2c = di("m2c", [128, 8])

    out_d = nc.dram_tensor("out", [D, TOK], F32, kind="ExternalOutput").ap()

    def dsc(name, shape, dt=F32):
        return nc.dram_tensor(name, shape, dt).ap()

    xp_d = dsc("xp_d", [DXB, TOKP])
    b_d = dsc("b_d", [DXB, TOK])
    z_d = dsc("z_d", [DI, TOK])
    cpl_d = dsc("cpl_d", [H, DST * TOK])
    dt_d = dsc("dt_d", [DI, TOK])
    a_d = dsc("a_d", [DI, TOK])
    xcd_d = dsc("xcd_d", [DI, TOK])
    h_d = dsc("h_d", [NCT * 128 * DST * TOK])
    midh_d = dsc("midh_d", [MI, TOK], BF16)
    midl_d = dsc("midl_d", [MI, TOK], BF16)
    fmh_d = dsc("fmh_d", [FI, TOK], BF16)
    fml_d = dsc("fml_d", [FI, TOK], BF16)
    wouth_d = dsc("wouth_d", [DI, D], BF16)
    woutl_d = dsc("woutl_d", [DI, D], BF16)
    wdh_d = dsc("wdh_d", [MI, D], BF16)
    wdl_d = dsc("wdl_d", [MI, D], BF16)
    fdh_d = dsc("fdh_d", [FI, D], BF16)
    fdl_d = dsc("fdl_d", [FI, D], BF16)
    CCN = DI * DST + DI
    cc_in = nc.dram_tensor("cc_in", [1, CCN], F32)
    cc_out = nc.dram_tensor("cc_out", [NCORES, CCN], F32)

    S = Sched()

    avail = (nc.sbuf_top - nc.sbuf_base - 512) & ~31
    guard = nc.alloc_sbuf_tensor("manual_region", [128, avail], mybir.dt.uint8)
    gbase = nc.lookup_mloc(guard).addr
    off = [0]

    def sb(name, shape, dt):
        t = nc.alloc_sbuf_tensor_at(name, shape, dt, offset=gbase + off[0])
        nb = 1
        for s_ in shape[1:]:
            nb *= s_
        nb *= 4 if dt == F32 else 2
        off[0] += (nb + 31) & ~31
        return t

    A1W = max(16 * TOK, 15 * TOK + TOKP)
    A1 = sb("A1", [128, A1W], F32)
    A2 = sb("A2", [128, 16 * TOKP], BF16)
    A3 = sb("A3", [128, 16 * TOKP], BF16)
    A4 = sb("A4", [128, max(DST * SC, TOKP, 2 * TH)], F32)
    A5 = sb("A5", [128, max(DST * SC, TOKP)], F32)
    WF = sb("WF", [128, 2 * 2048], F32)
    WH = sb("WH", [128, 2 * 2048], BF16)
    WL = sb("WL", [128, 2 * 2048], BF16)
    ST1 = sb("ST1", [128, TOKP + 5], F32)
    ST2 = sb("ST2", [128, TOKP + 5], F32)
    PHS = sb("PHS", [128, TOK], BF16)
    PLS = sb("PLS", [128, TOK], BF16)
    ACOL = sb("ACOL", [128, NCT * DST], F32)
    HIN = sb("HIN", [128, NCT * DST], F32)
    SCO = sb("SCO", [128, max(NCT * DST, 8 * NCT)], F32)
    TK8 = SCO
    ATOT = sb("ATOT", [128, NCT], F32)
    GJ = sb("GJ", [128, NCT], F32)
    M2S = sb("M2S", [128, 8], F32)
    GMS = sb("GMS", [128, 64], F32)
    PK = sb("PK", [128, DCONV * NCT], F32)
    CB = sb("CB", [128, NCT], F32)
    DPC = sb("DPC", [128, NCT], F32)
    DTBC = sb("DTBC", [128, NCT], F32)
    L1C = sb("L1C", [128, KD], F32)
    L2C = sb("L2C", [128, KD], F32)
    FGSC = sb("FGSC", [128, NFI], F32)
    FUSC = sb("FUSC", [128, NFI], F32)
    FDSC = sb("FDSC", [128, KD], F32)
    ONE1 = sb("ONE1", [1, 128], F32)
    ONEC = sb("ONEC", [128, 1], F32)
    EPSC = sb("EPSC", [128, 1], F32)
    COLS = sb("COLS", [128, 44], F32)
    DTRH = sb("DTRH", [128, TOK], BF16)
    DTRL = sb("DTRL", [128, TOK], BF16)
    YCT = sb("YCT", [128, max(TH, NCT * DST)], F32)
    TRM = YCT
    RS = TOKP + 1
    ROWS = sb("ROWS", [1, 3 * RS], F32)
    R_A, R_B, R_C = 0, RS, 2 * RS
    assert off[0] <= avail, f"SBUF over budget: {off[0]} > {avail}"

    PS = [nc.psum_tensor(f"ps{i}", [128, 512], F32).__enter__() for i in range(8)]

    A1_dt = 0
    A1_xe = A1_dt + TOK
    A1_xc = A1_xe + TOKP
    A1_a = A1_xc + TOK
    A1_n0 = A1_a + TOK

    hres = lambda k: A1[:, k * TOK:(k + 1) * TOK]

    def midhi(par):
        return PHS[:, 0:TOK] if par == 0 else DTRH[:, 0:TOK]

    def midlo(par):
        return PLS[:, 0:TOK] if par == 0 else DTRL[:, 0:TOK]

    S.g(lambda e: e.memset(ONE1[:, :], 1.0), w=["one1"], tag="ones1")
    S.g(lambda e: e.memset(ONEC[:, :], 1.0), w=["onec"], tag="onec")
    S.g(lambda e: e.memset(HIN[:, :], 0.0), w=["hin"], tag="hin0")
    S.g(lambda e: e.memset(EPSC[:, :], float(EPS)), w=["epsc"], tag="epsc")
    S.g(lambda e: e.memset(COLS[:, :], 0.0), w=["cols"], tag="cols0")

    def packed(dst_ap, src_ap, n_ct, inner=1):
        def fn(e):
            s2 = ap4(src_ap, 0, [[inner, 128], [128 * inner, n_ct], [1, inner]])
            return e.dma_start(out=dst_ap, in_=s2, allow_slow_non_contiguous=True)
        return fn

    S.dW(packed(PK[:, :], convw, NCT, DCONV), w=["pk"], tag="pk")
    S.dW(packed(CB[:, :], convb, NCT), w=["cb"], tag="cb")
    S.dW(packed(DPC[:, :], dpar, NCT), w=["dpc"], tag="dpc")
    S.dW(packed(DTBC[:, :], dtb, NCT), w=["dtbc"], tag="dtbc")
    S.dW(packed(L1C[:, :], ln1w, KD), w=["l1c"], tag="l1c")
    S.dW(packed(L2C[:, :], ln2w, KD), w=["l2c"], tag="l2c")
    S.dW(packed(FGSC[:, :], fgs, NFI), w=["fgsc"], tag="fgsc")
    S.dW(packed(FUSC[:, :], fus, NFI), w=["fusc"], tag="fusc")
    S.dW(packed(FDSC[:, :], fds, KD), w=["fdsc"], tag="fdsc")
    S.dW(packed(ACOL[:, :], alog, NCT, DST), w=["acol"], tag="alog")
    S.dW(lambda e: e.dma_start(out=GMS[:, :], in_=gmc), w=["gms"], tag="gms")
    S.dW(lambda e: e.dma_start(out=M2S[:, :], in_=m2c), w=["m2s"], tag="m2s")
    S.s(lambda e: e.activation(ACOL[:, :], ACOL[:, :], AF.Exp),
        r=["acol"], w=["acol"], tag="acol_exp")
    S.v(lambda e: e.tensor_scalar(ACOL[:, :], ACOL[:, :], -1.0, None, ALU.mult),
        r=["acol"], w=["acol"], tag="acol_neg")

    schunks = list(range(0, TOKP, 128))
    nj = len(schunks)

    for k in range(KD):
        S.dW(lambda e, k=k: e.dma_start(out=ST2[:, 0:TOKP],
                                        in_=x_in[k * 128:(k + 1) * 128, :]),
             r=["st2f"], w=["st2f"], tag=f"p1.ldx{k}")
        S.s(lambda e: e.activation(ST1[:, 0:TOKP], ST2[:, 0:TOKP], AF.Square),
            r=["st2f", "st1x"], w=["st1x"], tag=f"p1.sq{k}")
        def fmm(e, k=k):
            inst = None
            for j, c0 in enumerate(schunks):
                m = min(128, TOKP - c0)
                e.matmul(PS[6][0:m, j:j + 1], ST2[:, c0:c0 + m], ONEC[:, :],
                         start=(k == 0 and j == 0), stop=(k == KD - 1 and j == nj - 1),
                         skip_group_check=True)
                inst = e.matmul(PS[7][0:m, j:j + 1], ST1[:, c0:c0 + m], ONEC[:, :],
                                start=(k == 0 and j == 0), stop=(k == KD - 1 and j == nj - 1),
                                skip_group_check=True)
            return inst
        S.t(fmm, r=["st2f", "st1x", "onec"], w=["ps6", "ps7"], tag=f"p1.mm{k}")

    def ev_cols(bank, coff, chlist, width, tag):
        def fn(e, bank=bank, coff=coff):
            inst = None
            for j, c0 in enumerate(chlist):
                m = min(128, width - c0)
                inst = e.tensor_copy(COLS[0:m, coff + j:coff + j + 1],
                                     PS[bank][0:m, j:j + 1])
            return inst
        S.v(fn, r=[f"ps{bank}", "cols"], w=["cols"], tag=tag)

    ev_cols(6, 0, schunks, TOKP, "p1.ev1")
    ev_cols(7, 32, schunks, TOKP, "p1.ev2")

    def col2row(dst_off, src_off, n, width, tag):
        def fn(e):
            insts = []
            for j in range(n):
                m = min(128, width - j * 128)
                insts.append(e.dma_start(
                    out=ROWS[0:1, dst_off + j * 128:dst_off + j * 128 + m],
                    in_=COLS[0:m, src_off + j:src_off + j + 1]))
            return insts
        return self_dS(fn, n, tag)

    def self_dS(fn, n, tag):
        return S.emit("gpsimd", fn, ("cols",), ("rows",), dma=True, tag=tag,
                      nmul=n)

    col2row(R_A, 0, nj, TOKP, "p1.ra")
    col2row(R_B, 32, nj, TOKP, "p1.rb")
    S.v(lambda e: e.tensor_scalar(ROWS[0:1, R_A:R_A + TOKP], ROWS[0:1, R_A:R_A + TOKP],
                                  1.0 / D, None, ALU.mult),
        r=["rows"], w=["rows"], tag="p1.mean")
    S.v(lambda e: e.tensor_tensor(ROWS[0:1, R_C:R_C + TOKP], ROWS[0:1, R_A:R_A + TOKP],
                                  ROWS[0:1, R_A:R_A + TOKP], ALU.mult),
        r=["rows"], w=["rows"], tag="p1.m2")
    S.v(lambda e: e.scalar_tensor_tensor(ROWS[0:1, R_B:R_B + TOKP], ROWS[0:1, R_B:R_B + TOKP],
                                         1.0 / D, ROWS[0:1, R_C:R_C + TOKP],
                                         op0=ALU.mult, op1=ALU.subtract),
        r=["rows"], w=["rows"], tag="p1.var")
    S.s(lambda e: e.activation(ROWS[0:1, R_C:R_C + TOKP], ROWS[0:1, R_B:R_B + TOKP],
                               AF.Ln, bias=EPSC[0:1, :]),
        r=["rows", "epsc"], w=["rows"], tag="p1.ln")
    S.s(lambda e: e.activation(ROWS[0:1, R_C:R_C + TOKP], ROWS[0:1, R_C:R_C + TOKP],
                               AF.Exp, scale=-0.5),
        r=["rows"], w=["rows"], tag="p1.s")
    S.v(lambda e: e.tensor_tensor(ROWS[0:1, R_B:R_B + TOKP], ROWS[0:1, R_B:R_B + TOKP],
                                  ROWS[0:1, R_C:R_C + TOKP], ALU.mult),
        r=["rows"], w=["rows"], tag="p1.vs")
    S.v(lambda e: e.tensor_tensor(ROWS[0:1, R_B:R_B + TOKP], ROWS[0:1, R_B:R_B + TOKP],
                                  ROWS[0:1, R_C:R_C + TOKP], ALU.mult),
        r=["rows"], w=["rows"], tag="p1.vss")
    S.s(lambda e: e.activation(ROWS[0:1, R_B:R_B + TOKP], ROWS[0:1, R_B:R_B + TOKP],
                               AF.Ln, bias=EPSC[0:1, :]),
        r=["rows", "epsc"], w=["rows"], tag="p1.ln2")
    S.s(lambda e: e.activation(ROWS[0:1, R_B:R_B + TOKP], ROWS[0:1, R_B:R_B + TOKP],
                               AF.Exp, scale=-0.5),
        r=["rows"], w=["rows"], tag="p1.rmsi")
    S.v(lambda e: e.tensor_tensor(ROWS[0:1, R_B:R_B + TOKP], ROWS[0:1, R_B:R_B + TOKP],
                                  ROWS[0:1, R_C:R_C + TOKP], ALU.mult),
        r=["rows"], w=["rows"], tag="p1.srr")

    def rep_row(row_off, dst, dst_tok, width, tag):
        mch = list(range(0, width, 512))
        for j, c0 in enumerate(mch):
            w_ = min(512, width - c0)
            bank = j % 2
            S.t(lambda e, c0=c0, w_=w_, bank=bank, row_off=row_off: e.matmul(
                    PS[bank][:, 0:w_], ONE1[:, :],
                    ROWS[0:1, row_off + c0:row_off + c0 + w_], start=True, stop=True),
                r=["rows", "one1"], w=[f"ps{bank}"], tag=f"{tag}.{j}")
            S.v(lambda e, c0=c0, w_=w_, bank=bank: e.tensor_copy(
                    dst[:, c0:c0 + w_], PS[bank][:, 0:w_]),
                r=[f"ps{bank}"], w=list(dst_tok), tag=f"{tag}.ev{j}")

    rep_row(R_A, A4, ("a4:0", "a4:1"), TOKP, "p1.repm")
    rep_row(R_B, A5, ("a5:0", "a5:1"), TOKP, "p1.reps")

    for k in range(KD):
        S.dW(lambda e, k=k: e.dma_start(out=ST2[:, 0:TOKP],
                                        in_=x_in[k * 128:(k + 1) * 128, :]),
             r=["st2f"], w=["st2f"], tag=f"p1b.ldx{k}")
        S.v(lambda e: e.tensor_tensor(ST1[:, 0:TOKP], ST2[:, 0:TOKP], A4[:, 0:TOKP],
                                      ALU.subtract),
            r=["st2f", "a4:0", "a4:1", "st1x"], w=["st1x"], tag=f"p1b.sub{k}")
        S.v(lambda e, k=k: e.scalar_tensor_tensor(ST1[:, 0:TOKP], ST1[:, 0:TOKP],
                                                  L1C[:, k:k + 1], A5[:, 0:TOKP],
                                                  op0=ALU.mult, op1=ALU.mult),
            r=["st1x", "l1c", "a5:0", "a5:1"], w=["st1x"], tag=f"p1b.r1{k}")
        S.s(lambda e, k=k: e.activation(A2[:, k * TOKP:(k + 1) * TOKP], ST1[:, 0:TOKP],
                                        AF.Copy),
            r=["st1x"], w=["a2"], tag=f"p1b.hi{k}")
        S.v(lambda e, k=k: e.tensor_tensor(A3[:, k * TOKP:(k + 1) * TOKP], ST1[:, 0:TOKP],
                                           A2[:, k * TOKP:(k + 1) * TOKP], ALU.subtract),
            r=["st1x", "a2"], w=["a3"], tag=f"p1b.lo{k}")

    r1h = lambda k: A2[:, k * TOKP:(k + 1) * TOKP]
    r1l = lambda k: A3[:, k * TOKP:(k + 1) * TOKP]
    r1toks = ["a2", "a3"]

    def load_panel_pcols(wap, pt, ncols_w, par, tag):
        def fn(e, par=par):
            return e.dma_start(out=WF[:, par * 2048:par * 2048 + KD * 128],
                               in_=wap[pt * 128:(pt + 1) * 128, :])
        return S.dW(fn, r=[f"wf:{par}"], w=[f"wf:{par}"], tag=tag)

    def split_panel(par, ncols, tag, lo_eng="vector"):
        S.s(lambda e, par=par, ncols=ncols: e.activation(
                WH[:, par * 2048:par * 2048 + ncols],
                WF[:, par * 2048:par * 2048 + ncols], AF.Copy),
            r=[f"wf:{par}", f"whl:{par}"], w=[f"whl:{par}"], tag=tag + ".hi")
        S.emit(lo_eng, lambda e, par=par, ncols=ncols: e.tensor_tensor(
                WL[:, par * 2048:par * 2048 + ncols],
                WF[:, par * 2048:par * 2048 + ncols],
                WH[:, par * 2048:par * 2048 + ncols], ALU.subtract),
            [f"wf:{par}", f"whl:{par}"], [f"whl:{par}", f"wf:{par}"], tag=tag + ".lo")

    def wtile(par, k):
        return (WH[:, par * 2048 + k * 128:par * 2048 + (k + 1) * 128],
                WL[:, par * 2048 + k * 128:par * 2048 + (k + 1) * 128])

    def mm3(e, psum_ap, wh, wl, ah, al, first, last):
        e.matmul(psum_ap, wh, ah, start=first, stop=False)
        e.matmul(psum_ap, wh, al, start=False, stop=False)
        return e.matmul(psum_ap, wl, ah, start=False, stop=last)

    def proj_mm(par, ah, al, atoks, c0, c1, bank, tag, nk=None):
        nk_ = nk or KD
        def fn(e):
            inst = None
            for k in range(nk_):
                wh, wl = wtile(par, k)
                inst = mm3(e, PS[bank][:, 0:c1 - c0], wh, wl,
                           ah(k)[:, c0:c1], al(k)[:, c0:c1], k == 0, k == nk_ - 1)
            return inst
        return S.t(fn, r=[f"whl:{par}"] + atoks, w=[f"ps{bank}"], tag=tag)

    stg = [0]

    def stage():
        stg[0] ^= 1
        return (ST1, "st1x") if stg[0] else (ST2, "st2f")

    bankrot = [0]

    def load_proj_tile(pt):
        par = pt % 2
        load_panel_pcols(w1p, pt, KD * 128, par, f"p2.w{pt}")

    def do_proj_tile(kind, pt):
        par = pt % 2
        split_panel(par, KD * 128, f"p2.s{pt}", lo_eng="gpsimd")
        if kind == "x":
            chunks = [(c0, min(c0 + 512, TOKP)) for c0 in range(0, TOKP, 512)]
        else:
            chunks = [(3 + i * TC, 3 + (i + 1) * TC) for i in range(NTC)]
        for (c0, c1) in chunks:
            bank = bankrot[0] % 4
            bankrot[0] += 1
            proj_mm(par, r1h, r1l, r1toks, c0, c1, bank, f"p2.mm{pt}.{c0}")
            w_ = c1 - c0
            STG, stok = stage()
            S.s(lambda e, bank=bank, STG=STG, w_=w_: e.activation(
                    STG[:, 0:w_], PS[bank][:, 0:w_], AF.Copy),
                r=[f"ps{bank}", stok], w=[stok], tag=f"p2.ev{pt}.{c0}")
            if kind == "x":
                xi = pt - PT_X0
                S.dS(lambda e, xi=xi, c0=c0, w_=w_, STG=STG: e.dma_start(
                        out=xp_d[xi * 128:(xi + 1) * 128, c0:c0 + w_], in_=STG[:, 0:w_]),
                     r=[stok], w=[f"xp:{xi}"], tag=f"p2.sp{pt}.{c0}")
            elif kind == "B":
                bi = pt - PT_B0
                S.dS(lambda e, bi=bi, c0=c0, w_=w_, STG=STG: e.dma_start(
                        out=b_d[bi * 128:(bi + 1) * 128, c0 - 3:c0 - 3 + w_],
                        in_=STG[:, 0:w_]),
                     r=[stok], w=[f"bd:{bi}"], tag=f"p2.sp{pt}.{c0}")
            elif kind == "z":
                zi = pt - PT_Z0
                S.dS(lambda e, zi=zi, c0=c0, w_=w_, STG=STG: e.dma_start(
                        out=z_d[zi * 128:(zi + 1) * 128, c0 - 3:c0 - 3 + w_],
                        in_=STG[:, 0:w_]),
                     r=[stok], w=[f"zd:{zi}"], tag=f"p2.sp{pt}.{c0}")
            elif kind == "C":
                ci = pt - PT_C0
                def spc(e, ci=ci, c0=c0, w_=w_, STG=STG):
                    dst = ap4(cpl_d, (ci * HPC) * DST * TOK + (c0 - 3),
                              [[DST * TOK, HPC], [TOK, DST], [1, w_]])
                    return e.dma_start(out=dst, in_=STG[:, 0:w_])
                S.dS(spc, r=[stok], w=[f"cpl:{ci}"], tag=f"p2.sp{pt}.{c0}")
            elif kind == "dtr":
                S.s(lambda e, c0=c0, w_=w_, STG=STG: e.activation(
                        DTRH[:, c0 - 3:c0 - 3 + w_], STG[:, 0:w_], AF.Copy),
                    r=[stok, "dtr"], w=["dtr"], tag=f"p2.dh.{c0}")
                S.v(lambda e, c0=c0, w_=w_, STG=STG: e.tensor_tensor(
                        DTRL[:, c0 - 3:c0 - 3 + w_], STG[:, 0:w_],
                        DTRH[:, c0 - 3:c0 - 3 + w_], ALU.subtract),
                    r=[stok, "dtr"], w=["dtr"], tag=f"p2.dl.{c0}")

    for i in range(NXT):
        load_proj_tile(PT_X0 + i)
        do_proj_tile("x", PT_X0 + i)
    load_proj_tile(PT_DT)
    do_proj_tile("dtr", PT_DT)
    for i in range(NXT):
        load_proj_tile(PT_B0 + i)
        do_proj_tile("B", PT_B0 + i)

    nhalf = NCT // 2
    for half in range(2):
        par = half
        def ldw(e, half=half):
            sap = ap4(dtwt, half * nhalf * 128, [[DI, 128], [1, nhalf * 128]])
            return e.dma_start(out=WF[:, half * 2048:half * 2048 + nhalf * 128],
                               in_=sap)
        S.dW(ldw, r=[f"wf:{half}"], w=[f"wf:{half}"], tag=f"p2d.w{half}")
        split_panel(par, nhalf * 128, f"p2d.s{half}")
        for j in range(nhalf):
            ct = half * nhalf + j
            for i in range(NTC):
                bank = 4 + ((ct * NTC + i) % 2)
                def fmm(e, par=par, j=j, bank=bank, i=i):
                    wh, wl = wtile(par, j)
                    return mm3(e, PS[bank][:, 0:TC], wh, wl,
                               DTRH[:, i * TC:(i + 1) * TC],
                               DTRL[:, i * TC:(i + 1) * TC], True, True)
                S.t(fmm, r=[f"whl:{par}", "dtr"], w=[f"ps{bank}"],
                    tag=f"p2d.mm{ct}.{i}")
                SA, atok = stage()
                SB, btok = stage()
                S.v(lambda e, ct=ct, bank=bank, SA=SA: e.tensor_scalar(
                        SA[:, 0:TC], PS[bank][:, 0:TC], DTBC[:, ct:ct + 1], None,
                        ALU.add),
                    r=[f"ps{bank}", "dtbc", atok], w=[atok], tag=f"p2d.t{ct}.{i}")
                S.s(lambda e, SA=SA, SB=SB: e.activation(SB[:, 0:TC], SA[:, 0:TC],
                                                         AF.Abs),
                    r=[atok, btok], w=[btok], tag=f"p2d.ab{ct}.{i}")
                S.s(lambda e, SB=SB: e.activation(SB[:, 0:TC], SB[:, 0:TC], AF.Exp,
                                                  scale=-1.0),
                    r=[btok], w=[btok], tag=f"p2d.ex{ct}.{i}")
                S.s(lambda e, SB=SB: e.activation(SB[:, 0:TC], SB[:, 0:TC], AF.Ln,
                                                  bias=ONEC[:, 0:1]),
                    r=[btok, "onec"], w=[btok], tag=f"p2d.ln{ct}.{i}")
                S.s(lambda e, SA=SA: e.activation(SA[:, 0:TC], SA[:, 0:TC], AF.Relu),
                    r=[atok], w=[atok], tag=f"p2d.rl{ct}.{i}")
                S.v(lambda e, SA=SA, SB=SB: e.tensor_tensor(
                        SA[:, 0:TC], SA[:, 0:TC], SB[:, 0:TC], ALU.add),
                    r=[atok, btok], w=[atok], tag=f"p2d.sp{ct}.{i}")
                S.dS(lambda e, ct=ct, i=i, SA=SA: e.dma_start(
                        out=dt_d[ct * 128:(ct + 1) * 128, i * TC:(i + 1) * TC],
                        in_=SA[:, 0:TC]),
                     r=[atok], w=[f"dtd:{ct}"], tag=f"p2d.spl{ct}.{i}")

    def scan_ct(ct):
        hxb0 = (ct * HPC) // REP
        xpt = (hxb0 * DST) // 128
        S.dW(lambda e, ct=ct: e.dma_start(out=A1[:, A1_dt:A1_dt + TOK],
                                          in_=dt_d[ct * 128:(ct + 1) * 128, :]),
             r=[f"dtd:{ct}", "a1dt"], w=["a1dt"], tag=f"sc.dt{ct}")
        PB = REP * DST
        def ldxe(e, hxb0=hxb0, PB=PB):
            insts = []
            for hx in range(NHXB):
                sap = ap4(xp_d, (hxb0 + hx) * DST * TOKP,
                          [[0, REP], [TOKP, DST], [1, TOKP]])
                insts.append(e.dma_start(
                    out=A1[hx * PB:(hx + 1) * PB, A1_xe:A1_xe + TOKP], in_=sap))
            return insts
        S.dW(ldxe, r=[f"xp:{xpt}", "a1xe"], w=["a1xe"], tag=f"sc.xe{ct}", nmul=NHXB)
        S.v(lambda e, ct=ct: e.tensor_scalar(A1[:, A1_xc:A1_xc + TOK],
                                             A1[:, A1_xe:A1_xe + TOK],
                                             PK[:, ct * DCONV:ct * DCONV + 1], None,
                                             ALU.mult),
            r=["a1xe", "pk", "a1xc"], w=["a1xc"], tag=f"sc.c0.{ct}")
        for kk in range(1, DCONV):
            S.v(lambda e, ct=ct, kk=kk: e.scalar_tensor_tensor(
                    A1[:, A1_xc:A1_xc + TOK], A1[:, A1_xe + kk:A1_xe + kk + TOK],
                    PK[:, ct * DCONV + kk:ct * DCONV + kk + 1],
                    A1[:, A1_xc:A1_xc + TOK], op0=ALU.mult, op1=ALU.add),
                r=["a1xe", "pk", "a1xc"], w=["a1xc"], tag=f"sc.c{kk}.{ct}")
        S.s(lambda e, ct=ct: e.activation(A1[:, A1_xc:A1_xc + TOK],
                                          A1[:, A1_xc:A1_xc + TOK], AF.Silu,
                                          bias=CB[:, ct:ct + 1]),
            r=["a1xc", "cb"], w=["a1xc"], tag=f"sc.silu{ct}")
        S.v(lambda e, ct=ct: e.tensor_scalar(A1[:, A1_xe:A1_xe + TOK],
                                             A1[:, A1_xc:A1_xc + TOK],
                                             DPC[:, ct:ct + 1], None, ALU.mult),
            r=["a1xc", "dpc", "a1xe"], w=["a1xe"], tag=f"sc.xcd{ct}")
        S.dS(lambda e, ct=ct: e.dma_start(out=xcd_d[ct * 128:(ct + 1) * 128, :],
                                          in_=A1[:, A1_xe:A1_xe + TOK]),
             r=["a1xe"], w=[f"xcd:{ct}"], tag=f"sc.spxcd{ct}")
        S.v(lambda e: e.tensor_tensor(A1[:, A1_xc:A1_xc + TOK], A1[:, A1_dt:A1_dt + TOK],
                                      A1[:, A1_xc:A1_xc + TOK], ALU.mult),
            r=["a1dt", "a1xc"], w=["a1xc"], tag=f"sc.dtx{ct}")
        S.v(lambda e: e.tensor_tensor_scan(A1[:, A1_a:A1_a + TOK], A1[:, A1_dt:A1_dt + TOK],
                                           A1[:, A1_dt:A1_dt + TOK], initial=0.0,
                                           op0=ALU.add, op1=ALU.bypass),
            r=["a1dt", "a1a"], w=["a1a"], tag=f"sc.cum{ct}")
        S.dS(lambda e, ct=ct: e.dma_start(out=a_d[ct * 128:(ct + 1) * 128, :],
                                          in_=A1[:, A1_a:A1_a + TOK]),
             r=["a1a"], w=[f"ad:{ct}"], tag=f"sc.spa{ct}")
        S.v(lambda e, ct=ct: e.tensor_copy(ATOT[:, ct:ct + 1],
                                           A1[:, A1_a + TOK - 1:A1_a + TOK]),
            r=["a1a", "atot"], w=["atot"], tag=f"sc.atot{ct}")
        PB = REP * DST
        for np_ in range(DST // 2):
            n0 = 2 * np_
            par = np_ % 3
            base = A1_n0 + par * 4 * TOK
            bA, bB = base, base + 2 * TOK
            bH = bA
            ntA = f"a1nA:{par}"
            ntB = f"a1nB:{par}"
            def dA2(e, ct=ct, n0=n0, bA=bA):
                e.activation(A1[:, bA:bA + TOK], A1[:, A1_dt:A1_dt + TOK], AF.Exp,
                             scale=ACOL[:, ct * DST + n0:ct * DST + n0 + 1])
                return e.activation(A1[:, bA + TOK:bA + 2 * TOK],
                                    A1[:, A1_dt:A1_dt + TOK], AF.Exp,
                                    scale=ACOL[:, ct * DST + n0 + 1:ct * DST + n0 + 2])
            S.s(dA2, r=["a1dt", "acol", ntA], w=[ntA], tag=f"sc.dA{ct}.{n0}")
            def ldb(e, n0=n0, hxb0=hxb0, bB=bB):
                insts = []
                for hx in range(NHXB):
                    sap = ap4(b_d, ((hxb0 + hx) * DST + n0) * TOK,
                              [[0, PB], [TOK, 2], [1, TOK]])
                    insts.append(e.dma_start(
                        out=A1[hx * PB:(hx + 1) * PB, bB:bB + 2 * TOK], in_=sap))
                return insts
            S.emit("sync", ldb, [f"bd:{(hxb0 * DST) // 128}", ntB], [ntB],
                   dma=True, tag=f"sc.B{ct}.{n0}", nmul=NHXB)
            def dbx2(e, bB=bB):
                i0 = A1[:, A1_xc:A1_xc + TOK].unsqueeze(1).broadcast_to((128, 2, TOK))
                i1 = A1[:, bB:bB + 2 * TOK].rearrange("p (n t) -> p n t", n=2)
                o = A1[:, bB:bB + 2 * TOK].rearrange("p (n t) -> p n t", n=2)
                return e.tensor_tensor(o, i0, i1, ALU.mult)
            S.v(dbx2, r=["a1xc", ntB], w=[ntB], tag=f"sc.dBx{ct}.{n0}")
            for h2 in range(2):
                S.v(lambda e, bA=bA, bB=bB, bH=bH, h2=h2: e.tensor_tensor_scan(
                        A1[:, bH + h2 * TOK:bH + (h2 + 1) * TOK],
                        A1[:, bA + h2 * TOK:bA + (h2 + 1) * TOK],
                        A1[:, bB + h2 * TOK:bB + (h2 + 1) * TOK],
                        initial=0.0, op0=ALU.mult, op1=ALU.add),
                    r=[ntA, ntB], w=[ntA], tag=f"sc.tts{ct}.{n0 + h2}")
            def sph2(e, ct=ct, n0=n0, bH=bH):
                dst = ap4(h_d, ct * 128 * DST * TOK + n0 * TOK,
                          [[DST * TOK, 128], [TOK, 2], [1, TOK]])
                return e.dma_start(out=dst, in_=A1[:, bH:bH + 2 * TOK])
            S.dS(sph2, r=[ntA], w=[f"hd:{ct}"], tag=f"sc.sph{ct}.{n0}")
            def sco2(e, ct=ct, n0=n0, bH=bH):
                e.tensor_copy(SCO[:, ct * DST + n0:ct * DST + n0 + 1],
                              A1[:, bH + TOK - 1:bH + TOK])
                return e.tensor_copy(SCO[:, ct * DST + n0 + 1:ct * DST + n0 + 2],
                                     A1[:, bH + 2 * TOK - 1:bH + 2 * TOK])
            S.g(sco2, r=[ntA, "sco"], w=["sco"], tag=f"sc.sco{ct}.{n0}")

    zc_tiles = [("z", PT_Z0 + i) for i in range(DI // 128)] + \
               [("C", PT_C0 + i) for i in range(DI // 128)]
    for i in range(min(2, len(zc_tiles))):
        load_proj_tile(zc_tiles[i][1])
    gidx = [0]
    def emit_group():
        g = gidx[0]
        for j in range(2 * g, 2 * g + 2):
            if j < len(zc_tiles):
                do_proj_tile(*zc_tiles[j])
            jl = j + 2
            if jl < len(zc_tiles):
                load_proj_tile(zc_tiles[jl][1])
        gidx[0] += 1
    for ct in range(NCT):
        emit_group()
        scan_ct(ct)
    while 2 * gidx[0] < len(zc_tiles):
        emit_group()

    S.dS(lambda e: e.dma_start(
            out=ap4(cc_in.ap(), 0, [[NCT * DST, 128], [1, NCT * DST]]),
            in_=SCO[:, :]),
         r=["sco"], w=["ccin"], tag="p3.spS")
    S.dS(lambda e: e.dma_start(
            out=ap4(cc_in.ap(), DI * DST, [[NCT, 128], [1, NCT]]),
            in_=ATOT[:, :]),
         r=["atot"], w=["ccin"], tag="p3.spT")
    S.emit("gpsimd", lambda e: e.collective_compute(
            "AllGather", ALU.bypass, replica_groups=[list(range(NCORES))],
            ins=[cc_in.ap().opt()], outs=[cc_out.ap().opt()]),
        r=["ccin"], w=["ccout"], tag="p3.ag", force=True)
    for j in range(NCORES):
        def ldt(e, j=j):
            src = ap4(cc_out.ap(), j * CCN + DI * DST, [[NCT, 128], [1, NCT]])
            return e.dma_start(out=TK8[:, j * NCT:(j + 1) * NCT], in_=src)
        S.dR(ldt, r=["ccout", "sco"], w=["sco"], tag=f"p3.t{j}")
    for j in range(NCORES):
        S.v(lambda e, j=j: e.tensor_scalar(GJ[:, :], TK8[:, 0:NCT],
                                           GMS[:, j * 8:j * 8 + 1], None, ALU.mult),
            r=["sco", "gms", "gj"], w=["gj"], tag=f"p3.g0.{j}")
        for k in range(1, NCORES):
            S.v(lambda e, j=j, k=k: e.scalar_tensor_tensor(
                    GJ[:, :], TK8[:, k * NCT:(k + 1) * NCT],
                    GMS[:, j * 8 + k:j * 8 + k + 1], GJ[:, :],
                    op0=ALU.mult, op1=ALU.add),
                r=["sco", "gms", "gj"], w=["gj"], tag=f"p3.g{k}.{j}")
        S.v(lambda e: e.tensor_tensor(
                TRM[:, :].rearrange("p (c n) -> p c n", n=DST),
                ACOL[:, :].rearrange("p (c n) -> p c n", n=DST),
                GJ[:, :].unsqueeze(-1).broadcast_to((128, NCT, DST)), ALU.mult),
            r=["acol", "gj", "yct"], w=["yct"], tag=f"p3.arg{j}")
        S.s(lambda e: e.activation(TRM[:, :], TRM[:, :], AF.Exp),
            r=["yct"], w=["yct"], tag=f"p3.exp{j}")
        def lds(e, j=j):
            src = ap4(cc_out.ap(), j * CCN, [[NCT * DST, 128], [1, NCT * DST]])
            return e.dma_start(out=ST1[:, 0:NCT * DST], in_=src)
        S.dR(lds, r=["ccout", "st1x"], w=["st1x"], tag=f"p3.s{j}")
        S.v(lambda e, j=j: e.scalar_tensor_tensor(
                TRM[:, :], TRM[:, :], M2S[:, j:j + 1], ST1[:, 0:NCT * DST],
                op0=ALU.mult, op1=ALU.mult),
            r=["yct", "m2s", "st1x"], w=["yct"], tag=f"p3.msk{j}")
        S.v(lambda e: e.tensor_tensor(HIN[:, :], HIN[:, :], TRM[:, :], ALU.add),
            r=["hin", "yct"], w=["hin"], tag=f"p3.acc{j}")

    PGS = min(6, cfg.get("PGS", 6))
    npg = (KD + PGS - 1) // PGS

    def ldwout(e, kct, par):
        sap = ap4(woutt, kct * 128 * D, [[D, 128], [1, D]])
        return e.dma_start(out=WF[:, par * 2048:par * 2048 + D], in_=sap)

    def p4_wload(kct, th, pg):
        par = kct % 2
        if th == 0 and pg == 0:
            S.dW(lambda e, kct=kct, par=par: ldwout(e, kct, par),
                 r=[f"wf:{par}"], w=[f"wf:{par}"], tag=f"p4.w{kct}")
            split_panel(par, D, f"p4.ws{kct}", lo_eng="gpsimd")
            if npg > 1 or NTH > 1:
                S.dS(lambda e, kct=kct, par=par: e.dma_start(
                        out=wouth_d[kct * 128:(kct + 1) * 128, :],
                        in_=WH[:, par * 2048:par * 2048 + D]),
                     r=[f"whl:{par}"], w=["woh"], tag=f"p4.swh{kct}")
                S.dS(lambda e, kct=kct, par=par: e.dma_start(
                        out=woutl_d[kct * 128:(kct + 1) * 128, :],
                        in_=WL[:, par * 2048:par * 2048 + D]),
                     r=[f"whl:{par}"], w=["wol"], tag=f"p4.swl{kct}")
        else:
            S.dW(lambda e, kct=kct, par=par: e.dma_start(
                    out=WH[:, par * 2048:par * 2048 + D],
                    in_=wouth_d[kct * 128:(kct + 1) * 128, :]),
                 r=["woh", f"whl:{par}"], w=[f"whl:{par}"], tag=f"p4.rwh{kct}.{pg}.{th}")
            S.dW(lambda e, kct=kct, par=par: e.dma_start(
                    out=WL[:, par * 2048:par * 2048 + D],
                    in_=woutl_d[kct * 128:(kct + 1) * 128, :]),
                 r=["wol", f"whl:{par}"], w=[f"whl:{par}"], tag=f"p4.rwl{kct}.{pg}.{th}")

    def p4_mm(kct, th, pg):
        par = kct % 2
        p_lo = pg * PGS
        p_n = min(PGS, KD - p_lo)
        def fmm(e, kct=kct, par=par, p_lo=p_lo, p_n=p_n):
            inst = None
            for pi in range(p_n):
                p = p_lo + pi
                wh = WH[:, par * 2048 + p * 128:par * 2048 + (p + 1) * 128]
                wl = WL[:, par * 2048 + p * 128:par * 2048 + (p + 1) * 128]
                inst = mm3(e, PS[pi][:, 0:TH], wh, wl,
                           A2[:, kct * TH:(kct + 1) * TH],
                           A3[:, kct * TH:(kct + 1) * TH],
                           kct == 0, kct == NCT - 1)
            return inst
        S.t(fmm, r=[f"whl:{par}", "a2", "a3"],
            w=[f"ps{pi}" for pi in range(p_n)], tag=f"p4.mm{pg}.{kct}.{th}")

    def p4_evac(th, pg):
        t0 = th * TH
        p_lo = pg * PGS
        p_n = min(PGS, KD - p_lo)
        for pi in range(p_n):
            p = p_lo + pi
            S.dR(lambda e, p=p, t0=t0: e.dma_start(
                    out=ST1[:, 0:TH],
                    in_=x_in[p * 128:(p + 1) * 128, 3 + t0:3 + t0 + TH]),
                 r=["st1x"], w=["st1x"], tag=f"p4.x{p}.{th}")
            S.v(lambda e: e.tensor_tensor(ST2[:, 0:TH], ST1[:, 0:TH], PS[6][:, 0:TH],
                                          ALU.subtract),
                r=["st1x", "ps6", "st2f"], w=["st2f"], tag=f"p4.sub{p}.{th}")
            S.v(lambda e: e.tensor_tensor(ST2[:, 0:TH], ST2[:, 0:TH], PS[7][:, 0:TH],
                                          ALU.mult),
                r=["st2f", "ps7"], w=["st2f"], tag=f"p4.lnx{p}.{th}")
            S.v(lambda e, pi=pi, p=p, t0=t0: e.tensor_tensor(
                    hres(p)[:, t0:t0 + TH], PS[pi][:, 0:TH], ST2[:, 0:TH], ALU.add),
                r=[f"ps{pi}", "st2f", f"hr:{p}"], w=[f"hr:{p}"], tag=f"p4.hres{p}.{th}")

    for th in range(NTH):
        t0 = th * TH
        S.t(lambda e, t0=t0: e.matmul(PS[6][:, 0:TH], ONE1[:, :],
                                      ROWS[0:1, R_A + 3 + t0:R_A + 3 + t0 + TH],
                                      start=True, stop=True),
            r=["rows", "one1", "ps6"], w=["ps6"], tag=f"p4.repm.{th}")
        S.t(lambda e, t0=t0: e.matmul(PS[7][:, 0:TH], ONE1[:, :],
                                      ROWS[0:1, R_C + 3 + t0:R_C + 3 + t0 + TH],
                                      start=True, stop=True),
            r=["rows", "one1", "ps7"], w=["ps7"], tag=f"p4.reps.{th}")
        for ct in range(NCT):
            p4_wload(ct, th, 0)
            S.dR(lambda e, ct=ct, t0=t0: e.dma_start(
                    out=YCT[:, 0:TH], in_=xcd_d[ct * 128:(ct + 1) * 128, t0:t0 + TH]),
                 r=[f"xcd:{ct}", "yct"], w=["yct"], tag=f"p4.xcd{ct}.{th}")
            S.dR(lambda e, ct=ct, t0=t0: e.dma_start(
                    out=ST1[:, 0:TH], in_=a_d[ct * 128:(ct + 1) * 128, t0:t0 + TH]),
                 r=[f"ad:{ct}", "st1x"], w=["st1x"], tag=f"p4.a{ct}.{th}")
            hslots = [A4[:, 0:TH], A4[:, TH:2 * TH]]
            cslots = [A5[:, 0:TH], A5[:, TH:2 * TH]]
            htoks = ["a4:0", "a4:1"]
            ctoks = ["a5:0", "a5:1"]
            if TOK // 2 >= TH:
                hslots += [PHS[:, :].bitcast(F32)[:, 0:TH],
                           DTRH[:, :].bitcast(F32)[:, 0:TH]]
                cslots += [PLS[:, :].bitcast(F32)[:, 0:TH],
                           DTRL[:, :].bitcast(F32)[:, 0:TH]]
                htoks += ["ph:0", "ph:1"]
                ctoks += ["ph:0", "ph:1"]
            npar4 = len(hslots)
            for n in range(DST):
                par = n % npar4
                htok, ctok = htoks[par], ctoks[par]
                hsl = hslots[par]
                csl = cslots[par]
                def ldh(e, ct=ct, n=n, t0=t0, hsl=hsl):
                    sap = ap4(h_d, ct * 128 * DST * TOK + n * TOK + t0,
                              [[DST * TOK, 128], [1, TH]])
                    return e.dma_start(out=hsl, in_=sap)
                S.dW(ldh, r=[f"hd:{ct}", htok], w=[htok], tag=f"p4.h{ct}.{n}.{th}")
                def ldc(e, ct=ct, n=n, t0=t0, csl=csl):
                    sap = ap4(cpl_d, (ct * HPC) * DST * TOK + n * TOK + t0,
                              [[DST * TOK, HPC], [0, DST], [1, TH]])
                    return e.dma_start(out=csl, in_=sap)
                S.dW(ldc, r=[f"cpl:{ct}", ctok], w=[ctok], tag=f"p4.c{ct}.{n}.{th}")
                S.s(lambda e, ct=ct, n=n: e.activation(
                        ST2[:, 0:TH], ST1[:, 0:TH], AF.Exp,
                        scale=ACOL[:, ct * DST + n:ct * DST + n + 1]),
                    r=["st1x", "acol", "st2f"], w=["st2f"], tag=f"p4.f{ct}.{n}.{th}")
                S.v(lambda e, ct=ct, n=n, hsl=hsl: e.scalar_tensor_tensor(
                        hsl, ST2[:, 0:TH], HIN[:, ct * DST + n:ct * DST + n + 1],
                        hsl, op0=ALU.mult, op1=ALU.add),
                    r=["st2f", "hin", htok], w=[htok], tag=f"p4.hf{ct}.{n}.{th}")
                S.g(lambda e, hsl=hsl, csl=csl: e.tensor_tensor(
                        csl, hsl, csl, ALU.mult),
                    r=[htok, ctok], w=[ctok], tag=f"p4.ch{ct}.{n}.{th}")
                S.v(lambda e, csl=csl: e.tensor_tensor(
                        YCT[:, 0:TH], YCT[:, 0:TH], csl, ALU.add),
                    r=["yct", ctok], w=["yct"], tag=f"p4.y{ct}.{n}.{th}")
            S.dR(lambda e, ct=ct, t0=t0: e.dma_start(
                    out=ST1[:, 0:TH], in_=z_d[ct * 128:(ct + 1) * 128, t0:t0 + TH]),
                 r=[f"zd:{ct}", "st1x"], w=["st1x"], tag=f"p4.z{ct}.{th}")
            S.s(lambda e: e.activation(ST1[:, 0:TH], ST1[:, 0:TH], AF.Silu),
                r=["st1x"], w=["st1x"], tag=f"p4.sz{ct}.{th}")
            S.v(lambda e: e.tensor_tensor(YCT[:, 0:TH], YCT[:, 0:TH], ST1[:, 0:TH],
                                          ALU.mult),
                r=["yct", "st1x"], w=["yct"], tag=f"p4.mul{ct}.{th}")
            S.s(lambda e, ct=ct: e.activation(A2[:, ct * TH:(ct + 1) * TH], YCT[:, 0:TH],
                                              AF.Copy),
                r=["yct"], w=["a2"], tag=f"p4.hi{ct}.{th}")
            S.v(lambda e, ct=ct: e.tensor_tensor(A3[:, ct * TH:(ct + 1) * TH],
                                                 YCT[:, 0:TH],
                                                 A2[:, ct * TH:(ct + 1) * TH],
                                                 ALU.subtract),
                r=["yct", "a2"], w=["a3"], tag=f"p4.lo{ct}.{th}")
            p4_mm(ct, th, 0)
        p4_evac(th, 0)
        for pg in range(1, npg):
            for kct in range(NCT):
                p4_wload(kct, th, pg)
                p4_mm(kct, th, pg)
            p4_evac(th, pg)

    sch2 = list(range(0, TOK, 128))
    nj2 = len(sch2)
    for k in range(KD):
        S.s(lambda e, k=k: e.activation(ST1[:, 0:TOK], hres(k)[:, 0:TOK], AF.Square),
            r=[f"hr:{k}", "st1x"], w=["st1x"], tag=f"p5.sq{k}")
        def fmm(e, k=k):
            inst = None
            for j, c0 in enumerate(sch2):
                m = min(128, TOK - c0)
                inst = e.matmul(PS[6][0:m, j:j + 1], ST1[:, c0:c0 + m], ONEC[:, :],
                                start=(k == 0 and j == 0), stop=(k == KD - 1 and j == nj2 - 1),
                                skip_group_check=True)
            return inst
        S.t(fmm, r=["st1x", "onec", "ps6"], w=["ps6"], tag=f"p5.mm{k}")
    ev_cols(6, 0, sch2, TOK, "p5.ev")
    col2row(R_B, 0, nj2, TOK, "p5.r")
    S.s(lambda e: e.activation(ROWS[0:1, R_B:R_B + TOK], ROWS[0:1, R_B:R_B + TOK],
                               AF.Ln, scale=1.0 / D, bias=EPSC[0:1, :]),
        r=["rows", "epsc"], w=["rows"], tag="p5.ln")
    S.s(lambda e: e.activation(ROWS[0:1, R_B:R_B + TOK], ROWS[0:1, R_B:R_B + TOK],
                               AF.Exp, scale=-0.5),
        r=["rows"], w=["rows"], tag="p5.rsq")
    for j in range(NTC):
        S.t(lambda e, j=j: e.matmul(PS[j][:, 0:TC], ONE1[:, :],
                                    ROWS[0:1, R_B + j * TC:R_B + (j + 1) * TC],
                                    start=True, stop=True),
            r=["rows", "one1", f"ps{j}"], w=[f"ps{j}"], tag=f"p5.rep{j}")
        S.v(lambda e, j=j: e.tensor_copy(A4[:, j * TC:(j + 1) * TC], PS[j][:, 0:TC]),
            r=[f"ps{j}", "a4:0", "a4:1"], w=["a4:0", "a4:1"], tag=f"p5.repev{j}")
    for k in range(KD):
        S.v(lambda e, k=k: e.scalar_tensor_tensor(ST2[:, 0:TOK], hres(k)[:, 0:TOK],
                                                  L2C[:, k:k + 1], A4[:, 0:TOK],
                                                  op0=ALU.mult, op1=ALU.mult),
            r=[f"hr:{k}", "l2c", "a4:0", "a4:1", "st2f"], w=["st2f"], tag=f"p5.r2{k}")
        S.s(lambda e, k=k: e.activation(A2[:, k * TOK:(k + 1) * TOK], ST2[:, 0:TOK],
                                        AF.Copy),
            r=["st2f"], w=["a2"], tag=f"p5.hi{k}")
        S.v(lambda e, k=k: e.tensor_tensor(A3[:, k * TOK:(k + 1) * TOK], ST2[:, 0:TOK],
                                           A2[:, k * TOK:(k + 1) * TOK], ALU.subtract),
            r=["st2f", "a2"], w=["a3"], tag=f"p5.lo{k}")
    r2h = lambda k: A2[:, k * TOK:(k + 1) * TOK]
    r2l = lambda k: A3[:, k * TOK:(k + 1) * TOK]
    r2toks = ["a2", "a3"]

    def gup_phase(wg_ap, wu_ap, NP, spill_h, spill_l, sh_tok, sl_tok, scaled,
                  gsc, usc, pref):
        for i in range(NP):
            par = 0
            load_panel_pcols(wg_ap, i, KD * 128, par, f"{pref}.wg{i}")
            split_panel(par, KD * 128, f"{pref}.sg{i}")
            for tc in range(NTC):
                bank = 4 * (i % 2)
                proj_mm(par, r2h, r2l, r2toks, tc * TC, (tc + 1) * TC, bank,
                        f"{pref}.gmm{i}.{tc}")
                if scaled:
                    S.s(lambda e, bank=bank, tc=tc, i=i: e.activation(
                            ST1[:, tc * TC:(tc + 1) * TC], PS[bank][:, 0:TC], AF.Silu,
                            scale=gsc[:, i:i + 1]),
                        r=[f"ps{bank}", "fgsc", "st1x"], w=["st1x"],
                        tag=f"{pref}.si{i}.{tc}")
                else:
                    S.s(lambda e, bank=bank, tc=tc: e.activation(
                            ST1[:, tc * TC:(tc + 1) * TC], PS[bank][:, 0:TC], AF.Silu),
                        r=[f"ps{bank}", "st1x"], w=["st1x"], tag=f"{pref}.si{i}.{tc}")
            par = 1
            load_panel_pcols(wu_ap, i, KD * 128, par, f"{pref}.wu{i}")
            split_panel(par, KD * 128, f"{pref}.su{i}")
            for tc in range(NTC):
                bank = 4 * (i % 2) + 2
                proj_mm(par, r2h, r2l, r2toks, tc * TC, (tc + 1) * TC, bank,
                        f"{pref}.umm{i}.{tc}")
                if scaled:
                    S.v(lambda e, bank=bank, tc=tc, i=i: e.scalar_tensor_tensor(
                            ST2[:, tc * TC:(tc + 1) * TC], PS[bank][:, 0:TC],
                            usc[:, i:i + 1], ST1[:, tc * TC:(tc + 1) * TC],
                            op0=ALU.mult, op1=ALU.mult),
                        r=[f"ps{bank}", "fusc", "st1x", "st2f"], w=["st2f"],
                        tag=f"{pref}.pr{i}.{tc}")
                else:
                    S.v(lambda e, bank=bank, tc=tc: e.tensor_tensor(
                            ST2[:, tc * TC:(tc + 1) * TC], PS[bank][:, 0:TC],
                            ST1[:, tc * TC:(tc + 1) * TC], ALU.mult),
                        r=[f"ps{bank}", "st1x", "st2f"], w=["st2f"],
                        tag=f"{pref}.pr{i}.{tc}")
            ppar = i % 2
            ptok = f"ph:{ppar}"
            S.s(lambda e, ppar=ppar: e.activation(midhi(ppar), ST2[:, 0:TOK], AF.Copy),
                r=["st2f", ptok], w=[ptok], tag=f"{pref}.phi{i}")
            S.v(lambda e, ppar=ppar: e.tensor_tensor(midlo(ppar), ST2[:, 0:TOK],
                                                     midhi(ppar), ALU.subtract),
                r=["st2f", ptok], w=[ptok], tag=f"{pref}.plo{i}")
            S.dS(lambda e, i=i, ppar=ppar: e.dma_start(
                    out=spill_h[i * 128:(i + 1) * 128, :], in_=midhi(ppar)),
                 r=[ptok], w=[f"{sh_tok}:{i}"], tag=f"{pref}.sph{i}")
            S.dS(lambda e, i=i, ppar=ppar: e.dma_start(
                    out=spill_l[i * 128:(i + 1) * 128, :], in_=midlo(ppar)),
                 r=[ptok], w=[f"{sl_tok}:{i}"], tag=f"{pref}.spl{i}")

    gup_phase(wgp, wup, NMI, midh_d, midl_d, "mh", "ml", False, None, None, "p6")

    for k in range(KD):
        S.dR(lambda e, k=k: e.dma_start(out=ST1[:, 0:TOK],
                                        in_=x_in[k * 128:(k + 1) * 128, 3:3 + TOK]),
             r=["st1x"], w=["st1x"], tag=f"p6.x{k}")
        S.v(lambda e, k=k: e.tensor_tensor(hres(k)[:, 0:TOK], hres(k)[:, 0:TOK],
                                           ST1[:, 0:TOK], ALU.add),
            r=[f"hr:{k}", "st1x"], w=[f"hr:{k}"], tag=f"p6.hx{k}")

    def down_phase(wd_ap, wh_spill, wl_spill, whtok, wltok, NP, mh_tok, ml_tok,
                   mh_dram, ml_dram, evac, pref):
        first = [True]
        PGS = cfg.get("PGS", 8)
        npg = (KD + PGS - 1) // PGS
        for pg in range(npg):
            p_lo = pg * PGS
            p_n = min(PGS, KD - p_lo)
            for tc in range(NTC):
                for i in range(NP):
                    par = i % 2
                    if first[0]:
                        def ldwd(e, i=i, par=par):
                            sap = ap4(wd_ap, i * 128 * D, [[D, 128], [1, D]])
                            return e.dma_start(out=WF[:, par * 2048:par * 2048 + D],
                                               in_=sap)
                        S.dW(ldwd, r=[f"wf:{par}"], w=[f"wf:{par}"], tag=f"{pref}.wd{i}")
                        split_panel(par, D, f"{pref}.wds{i}")
                        if npg * NTC > 1:
                            S.dS(lambda e, i=i, par=par: e.dma_start(
                                    out=wh_spill[i * 128:(i + 1) * 128, :],
                                    in_=WH[:, par * 2048:par * 2048 + D]),
                                 r=[f"whl:{par}"], w=[whtok], tag=f"{pref}.swh{i}")
                            S.dS(lambda e, i=i, par=par: e.dma_start(
                                    out=wl_spill[i * 128:(i + 1) * 128, :],
                                    in_=WL[:, par * 2048:par * 2048 + D]),
                                 r=[f"whl:{par}"], w=[wltok], tag=f"{pref}.swl{i}")
                    else:
                        S.dW(lambda e, i=i, par=par: e.dma_start(
                                out=WH[:, par * 2048:par * 2048 + D],
                                in_=wh_spill[i * 128:(i + 1) * 128, :]),
                             r=[whtok, f"whl:{par}"], w=[f"whl:{par}"],
                             tag=f"{pref}.rwh{i}.{pg}.{tc}")
                        S.dW(lambda e, i=i, par=par: e.dma_start(
                                out=WL[:, par * 2048:par * 2048 + D],
                                in_=wl_spill[i * 128:(i + 1) * 128, :]),
                             r=[wltok, f"whl:{par}"], w=[f"whl:{par}"],
                             tag=f"{pref}.rwl{i}.{pg}.{tc}")
                    ppar = i % 2
                    ptok = f"ph:{ppar}"
                    S.dR(lambda e, i=i, tc=tc, ppar=ppar: e.dma_start(
                            out=midhi(ppar)[:, 0:TC],
                            in_=mh_dram[i * 128:(i + 1) * 128, tc * TC:(tc + 1) * TC]),
                         r=[f"{mh_tok}:{i}", ptok], w=[ptok],
                         tag=f"{pref}.rmh{i}.{pg}.{tc}")
                    S.dR(lambda e, i=i, tc=tc, ppar=ppar: e.dma_start(
                            out=midlo(ppar)[:, 0:TC],
                            in_=ml_dram[i * 128:(i + 1) * 128, tc * TC:(tc + 1) * TC]),
                         r=[f"{ml_tok}:{i}", ptok], w=[ptok],
                         tag=f"{pref}.rml{i}.{pg}.{tc}")
                    def fmm(e, i=i, par=par, ppar=ppar, p_lo=p_lo, p_n=p_n, NP=NP):
                        inst = None
                        for pi in range(p_n):
                            p = p_lo + pi
                            wh = WH[:, par * 2048 + p * 128:par * 2048 + (p + 1) * 128]
                            wl = WL[:, par * 2048 + p * 128:par * 2048 + (p + 1) * 128]
                            inst = mm3(e, PS[pi][:, 0:TC], wh, wl,
                                       midhi(ppar)[:, 0:TC], midlo(ppar)[:, 0:TC],
                                       i == 0, i == NP - 1)
                        return inst
                    S.t(fmm, r=[f"whl:{par}", ptok],
                        w=[f"ps{pi}" for pi in range(p_n)],
                        tag=f"{pref}.dmm{pg}.{tc}.{i}")
                for pi in range(p_n):
                    evac(p_lo + pi, pi, tc)
                first[0] = False

    def evac6(p, pi, tc):
        S.v(lambda e: e.tensor_tensor(
                hres(p)[:, tc * TC:(tc + 1) * TC], PS[pi][:, 0:TC],
                hres(p)[:, tc * TC:(tc + 1) * TC], ALU.add),
            r=[f"ps{pi}", f"hr:{p}"], w=[f"hr:{p}"], tag=f"p6.x1{p}.{tc}")

    down_phase(wdt, wdh_d, wdl_d, "wdh", "wdl", NMI, "mh", "ml",
               midh_d, midl_d, evac6, "p6d")

    for k in range(KD):
        S.s(lambda e, k=k: e.activation(ST1[:, 0:TOK], hres(k)[:, 0:TOK], AF.Square),
            r=[f"hr:{k}", "st1x"], w=["st1x"], tag=f"p7.sq{k}")
        def fmm(e, k=k):
            inst = None
            for j, c0 in enumerate(sch2):
                m = min(128, TOK - c0)
                e.matmul(PS[6][0:m, j:j + 1], hres(k)[:, c0:c0 + m], ONEC[:, :],
                         start=(k == 0 and j == 0), stop=(k == KD - 1 and j == nj2 - 1),
                         skip_group_check=True)
                inst = e.matmul(PS[7][0:m, j:j + 1], ST1[:, c0:c0 + m], ONEC[:, :],
                                start=(k == 0 and j == 0), stop=(k == KD - 1 and j == nj2 - 1),
                                skip_group_check=True)
            return inst
        S.t(fmm, r=[f"hr:{k}", "st1x", "onec", "ps6", "ps7"], w=["ps6", "ps7"],
            tag=f"p7.mm{k}")
    ev_cols(6, 0, sch2, TOK, "p7.ev1")
    ev_cols(7, 32, sch2, TOK, "p7.ev2")
    col2row(R_A, 0, nj2, TOK, "p7.ra")
    col2row(R_B, 32, nj2, TOK, "p7.rb")
    S.v(lambda e: e.tensor_scalar(ROWS[0:1, R_A:R_A + TOK], ROWS[0:1, R_A:R_A + TOK],
                                  1.0 / D, None, ALU.mult),
        r=["rows"], w=["rows"], tag="p7.mean")
    S.v(lambda e: e.tensor_tensor(ROWS[0:1, R_C:R_C + TOK], ROWS[0:1, R_A:R_A + TOK],
                                  ROWS[0:1, R_A:R_A + TOK], ALU.mult),
        r=["rows"], w=["rows"], tag="p7.m2")
    S.v(lambda e: e.scalar_tensor_tensor(ROWS[0:1, R_B:R_B + TOK], ROWS[0:1, R_B:R_B + TOK],
                                         1.0 / D, ROWS[0:1, R_C:R_C + TOK],
                                         op0=ALU.mult, op1=ALU.subtract),
        r=["rows"], w=["rows"], tag="p7.var")
    S.s(lambda e: e.activation(ROWS[0:1, R_C:R_C + TOK], ROWS[0:1, R_B:R_B + TOK],
                               AF.Ln, bias=EPSC[0:1, :]),
        r=["rows", "epsc"], w=["rows"], tag="p7.ln")
    S.s(lambda e: e.activation(ROWS[0:1, R_C:R_C + TOK], ROWS[0:1, R_C:R_C + TOK],
                               AF.Exp, scale=-0.5),
        r=["rows"], w=["rows"], tag="p7.s")
    for j in range(NTC):
        S.t(lambda e, j=j: e.matmul(PS[0][:, 0:TC], ONE1[:, :],
                                    ROWS[0:1, R_A + j * TC:R_A + (j + 1) * TC],
                                    start=True, stop=True),
            r=["rows", "one1", "ps0"], w=["ps0"], tag=f"p7.repm{j}")
        S.v(lambda e, j=j: e.tensor_copy(A4[:, j * TC:(j + 1) * TC], PS[0][:, 0:TC]),
            r=["ps0", "a4:0", "a4:1"], w=["a4:0", "a4:1"], tag=f"p7.evm{j}")
        S.t(lambda e, j=j: e.matmul(PS[1][:, 0:TC], ONE1[:, :],
                                    ROWS[0:1, R_C + j * TC:R_C + (j + 1) * TC],
                                    start=True, stop=True),
            r=["rows", "one1", "ps1"], w=["ps1"], tag=f"p7.repsd{j}")
        S.v(lambda e, j=j: e.tensor_copy(A5[:, j * TC:(j + 1) * TC], PS[1][:, 0:TC]),
            r=["ps1", "a5:0", "a5:1"], w=["a5:0", "a5:1"], tag=f"p7.evsd{j}")
    for k in range(KD):
        S.v(lambda e, k=k: e.tensor_tensor(ST2[:, 0:TOK], hres(k)[:, 0:TOK],
                                           A4[:, 0:TOK], ALU.subtract),
            r=[f"hr:{k}", "a4:0", "a4:1", "st2f"], w=["st2f"], tag=f"p7.sub{k}")
        S.v(lambda e: e.tensor_tensor(ST2[:, 0:TOK], ST2[:, 0:TOK], A5[:, 0:TOK],
                                      ALU.mult),
            r=["st2f", "a5:0", "a5:1"], w=["st2f"], tag=f"p7.l2{k}")
        S.s(lambda e, k=k: e.activation(A2[:, k * TOK:(k + 1) * TOK], ST2[:, 0:TOK],
                                        AF.Copy),
            r=["st2f"], w=["a2"], tag=f"p7.hi{k}")
        S.v(lambda e, k=k: e.tensor_tensor(A3[:, k * TOK:(k + 1) * TOK], ST2[:, 0:TOK],
                                           A2[:, k * TOK:(k + 1) * TOK], ALU.subtract),
            r=["st2f", "a2"], w=["a3"], tag=f"p7.lo{k}")

    gup_phase(fgp, fup, NFI, fmh_d, fml_d, "fh", "fl", True, FGSC, FUSC, "p7")

    outtoks = []

    def evac7(p, pi, tc):
        S.v(lambda e, pi=pi, p=p, tc=tc: e.scalar_tensor_tensor(
                ST2[:, 0:TC], PS[pi][:, 0:TC], FDSC[:, p:p + 1],
                hres(p)[:, tc * TC:(tc + 1) * TC], op0=ALU.mult, op1=ALU.add),
            r=[f"ps{pi}", "fdsc", f"hr:{p}", "st2f"], w=["st2f"],
            tag=f"p7.x2{p}.{tc}")
        ot = f"out:{p}:{tc}"
        S.dS(lambda e, p=p, tc=tc: e.dma_start(
                out=out_d[p * 128:(p + 1) * 128, tc * TC:(tc + 1) * TC],
                in_=ST2[:, 0:TC]),
             r=["st2f"], w=[ot], tag=f"p7.out{p}.{tc}")
        outtoks.append(ot)

    down_phase(fdt, fdh_d, fdl_d, "fdh", "fdl", NFI, "fh", "fl",
               fmh_d, fml_d, evac7, "p7d")

    S.g(lambda e: e.memset(ONEC[:, :], 1.0), r=outtoks, w=["onec"], tag="final")

    S.finalize(nc)
    return nc


def _prep_inputs(cfg, A):
    D, TOK = cfg["D"], cfg["TOK"]
    x = np.asarray(A["L0_attn_hidden_states"], dtype=np.float32)
    B = x.shape[0]
    per = NCORES // B
    assert x.shape[1] == per * TOK
    ct = np.ascontiguousarray

    KD = D // 128

    def pack_panels(w):
        wt = np.asarray(w, np.float32).T
        Dd, P = wt.shape
        npt = P // 128
        kd = Dd // 128
        v = wt.reshape(kd, 128, npt, 128).transpose(2, 1, 0, 3)
        return np.ascontiguousarray(v.reshape(npt * 128, kd * 128))

    shared = dict(
        w1p=pack_panels(A["L0_attn_in_proj_w"]),
        dtwt=ct(np.asarray(A["L0_attn_dt_proj_w"], np.float32).T),
        dtb=ct(np.asarray(A["L0_attn_dt_proj_b"], np.float32)),
        convw=ct(np.asarray(A["L0_attn_conv_w"], np.float32).reshape(cfg["DI"], cfg["DCONV"])),
        convb=ct(np.asarray(A["L0_attn_conv_b"], np.float32)),
        alog=ct(np.asarray(A["L0_attn_A_log"], np.float32)),
        dpar=ct(np.asarray(A["L0_attn_D_param"], np.float32)),
        woutt=ct(np.asarray(A["L0_attn_out_proj_w"], np.float32).T),
        ln1w=ct(np.asarray(A["L0_attn_ln1_w"], np.float32)),
        ln2w=ct(np.asarray(A["L0_attn_ln2_w"], np.float32)),
        wgp=pack_panels(A["L0_attn_gate_w"]),
        wup=pack_panels(A["L0_attn_up_w"]),
        wdt=ct(np.asarray(A["L0_attn_down_w"], np.float32).T),
        fgp=pack_panels(A["L0_ffn_gate_w"]),
        fup=pack_panels(A["L0_ffn_up_w"]),
        fdt=ct(np.asarray(A["L0_ffn_down_w"], np.float32).T),
        fgs=ct(np.asarray(A["L0_ffn_gate_s"], np.float32)),
        fus=ct(np.asarray(A["L0_ffn_up_s"], np.float32)),
        fds=ct(np.asarray(A["L0_ffn_down_s"], np.float32)),
    )
    in_maps = []
    for q in range(NCORES):
        b, q4 = q // per, q % per
        lo = q4 * TOK
        x3 = np.zeros((3, D), np.float32) if q4 == 0 else x[b, lo - 3:lo]
        xs = np.concatenate([x3, x[b, lo:lo + TOK]], axis=0)
        gm = np.zeros((8, 8), np.float32)
        m2 = np.zeros((8,), np.float32)
        for j in range(per):
            jj = b * per + j
            if j < q4:
                m2[jj] = 1.0
                for k in range(j + 1, q4):
                    gm[jj, b * per + k] = 1.0
        im = dict(shared)
        im["x_in"] = ct(xs.T)
        im["gmc"] = ct(np.repeat(gm.reshape(1, 64), 128, axis=0))
        im["m2c"] = ct(np.repeat(m2.reshape(1, 8), 128, axis=0))
        in_maps.append(im)
    return in_maps


def _assemble(cfg, results):
    per = NCORES // 2
    outs = [np.asarray(r["out"]) for r in results]
    bs = [np.concatenate([outs[b * per + q].T for q in range(per)], axis=0)
          for b in range(2)]
    return np.stack(bs, axis=0).astype(np.float32)


def kernel(**inputs):
    cfg = real_cfg()
    in_maps = _prep_inputs(cfg, inputs)
    nc = build_nc(cfg)
    res = run_bass_kernel_spmd(nc, in_maps, core_ids=list(range(NCORES)))
    return _assemble(cfg, res.results)


# version 14
# speedup vs baseline: 1.3623x; 1.0163x over previous
import math
import numpy as np
import concourse.bass as bass
import concourse.mybir as mybir
from concourse.bass_utils import run_bass_kernel_spmd

F32 = mybir.dt.float32
BF16 = mybir.dt.bfloat16
AF = mybir.ActivationFunctionType
ALU = mybir.AluOpType
AX = mybir.AxisListType

NCORES = 8


def real_cfg():
    return dict(D=2048, DI=4096, DXB=1024, DST=16, DCONV=4, DTR=128,
                MI=8192, FI=5632, TOK=1024, EPS=1e-5)


class Sched:
    COMPUTE = ("tensor", "vector", "scalar", "gpsimd", "sync")

    def __init__(self):
        self.ops = []
        self.lastw = {}
        self.readers = {}

    def emit(self, eng, fn, r=(), w=(), dma=False, tag="", force=False, nmul=1):
        i = len(self.ops)
        deps = set()
        for t in r:
            if t in self.lastw:
                deps.add(self.lastw[t])
        for t in w:
            if t in self.lastw:
                deps.add(self.lastw[t])
            for j in self.readers.get(t, ()):
                deps.add(j)
        deps.discard(i)
        self.ops.append(dict(i=i, eng=eng, fn=fn, dma=dma, deps=deps, tag=tag,
                             need_inc=force or dma, seq=None, nmul=nmul))
        for t in r:
            self.readers.setdefault(t, []).append(i)
        for t in w:
            self.lastw[t] = i
            self.readers[t] = []
        return i

    def stream_of(self, rec):
        return rec["eng"] + ".q" if rec["dma"] else rec["eng"]

    NLANES = 8

    def finalize(self, nc):
        ops = self.ops
        for rec in ops:
            for d in rec["deps"]:
                ops[d]["need_inc"] = True
        seqno = {}
        lane_rot = {}
        for rec in ops:
            if not rec["need_inc"]:
                continue
            if rec["dma"]:
                q = rec["eng"] + ".q"
                lane = lane_rot.get(q, 0)
                lane_rot[q] = (lane + 1) % self.NLANES
                s = (q, lane)
            else:
                s = self.stream_of(rec)
            rec["stream"] = s
            prev = seqno.get(s, 0)
            rec["gate"] = prev if rec["dma"] else None
            seqno[s] = prev + rec.get("nmul", 1)
            rec["seq"] = seqno[s]
        sems = {}
        for s in sorted(seqno, key=str):
            nm = s if isinstance(s, str) else f"{s[0]}_{s[1]}"
            sems[s] = nc.semaphore("sem_" + nm.replace(".", "_")).__enter__()
        per_eng = {e: [] for e in self.COMPUTE}
        waited = {e: {} for e in self.COMPUTE}
        for rec in ops:
            waits = []
            for d in sorted(rec["deps"]):
                p = ops[d]
                if not p["need_inc"]:
                    continue
                s = p["stream"] if "stream" in p else self.stream_of(p)
                v = p["seq"] * (16 if p["dma"] else 1)
                if waited[rec["eng"]].get(s, 0) < v:
                    waited[rec["eng"]][s] = v
                    waits.append((s, v))
            if rec["dma"] and rec["gate"]:
                s = rec["stream"]
                v = rec["gate"] * 16
                if waited[rec["eng"]].get(s, 0) < v:
                    waited[rec["eng"]][s] = v
                    waits.append((s, v))
            per_eng[rec["eng"]].append((rec, waits))

        def mk(eng_name):
            def body(e):
                for rec, wl in per_eng[eng_name]:
                    for (s, v) in wl:
                        e.wait_ge(sems[s], v)
                    inst = rec["fn"](e)
                    if rec["need_inc"]:
                        assert inst is not None, f"op '{rec['tag']}' returned None"
                        amt = 16 if rec["dma"] else 1
                        insts = inst if isinstance(inst, list) else [inst]
                        assert len(insts) == rec.get("nmul", 1), rec["tag"]
                        for it in insts:
                            it.then_inc(sems[rec["stream"]], amt)
            return body

        with nc.Block() as block:
            for eng_name in self.COMPUTE:
                if per_eng[eng_name]:
                    getattr(block, eng_name)(mk(eng_name))

    def t(self, fn, r=(), w=(), tag=""):
        return self.emit("tensor", fn, r, w, tag=tag)

    def v(self, fn, r=(), w=(), tag=""):
        return self.emit("vector", fn, r, w, tag=tag)

    def s(self, fn, r=(), w=(), tag=""):
        return self.emit("scalar", fn, r, w, tag=tag)

    def g(self, fn, r=(), w=(), tag=""):
        return self.emit("gpsimd", fn, r, w, tag=tag)

    def dW(self, fn, r=(), w=(), tag="", nmul=1):
        return self.emit("sync", fn, r, w, dma=True, tag=tag, nmul=nmul)

    def dR(self, fn, r=(), w=(), tag="", nmul=1):
        return self.emit("scalar", fn, r, w, dma=True, tag=tag, nmul=nmul)

    def dS(self, fn, r=(), w=(), tag=""):
        return self.emit("gpsimd", fn, r, w, dma=True, tag=tag)


def ap4(base_ap, offset, dims):
    return bass.AP(base_ap.tensor, base_ap.offset + offset, [list(d) for d in dims])


def build_nc(cfg):
    D, DI, DXB, DST = cfg["D"], cfg["DI"], cfg["DXB"], cfg["DST"]
    DCONV, DTR, MI, FI = cfg["DCONV"], cfg["DTR"], cfg["MI"], cfg["FI"]
    TOK, EPS = cfg["TOK"], cfg["EPS"]
    TOKP = TOK + 3
    KD = D // 128
    NCT = DI // 128
    H = DI // DST
    REP = DI // DXB
    PROJ = 2 * DI + 2 * DXB + DTR
    NPT = PROJ // 128
    NXT = DXB // 128
    PT_Z0, PT_X0 = 0, DI // 128
    PT_B0 = PT_X0 + NXT
    PT_C0 = PT_B0 + NXT
    PT_DT = PT_C0 + DI // 128
    HPC = 128 // DST
    NHXB = max(1, HPC // REP)
    TC = min(cfg.get("TCAP", 512), TOK)
    NTC = TOK // TC
    TH = TC
    NTH = TOK // TH
    SC = min(cfg.get("SCAP", 64), TOK, TH)
    NSC = TH // SC
    NMI, NFI = MI // 128, FI // 128
    assert D % 128 == 0 and DI % 128 == 0 and DXB % 128 == 0 and PROJ % 128 == 0
    assert REP * DST * NHXB == 128 and HPC * DST == 128

    nc = bass.Bass("TRN2", target_bir_lowering=False, debug=False)

    def di(name, shape):
        return nc.dram_tensor(name, shape, F32, kind="ExternalInput").ap()

    x_in = di("x_in", [D, TOKP])
    w1p = di("w1p", [NPT * 128, KD * 128])
    dtwt = di("dtwt", [DTR, DI])
    dtb = di("dtb", [DI])
    convw = di("convw", [DI, DCONV])
    convb = di("convb", [DI])
    alog = di("alog", [DI, DST])
    dpar = di("dpar", [DI])
    woutt = di("woutt", [DI, D])
    ln1w = di("ln1w", [D])
    ln2w = di("ln2w", [D])
    wgp = di("wgp", [NMI * 128, KD * 128])
    wup = di("wup", [NMI * 128, KD * 128])
    wdt = di("wdt", [MI, D])
    fgp = di("fgp", [NFI * 128, KD * 128])
    fup = di("fup", [NFI * 128, KD * 128])
    fdt = di("fdt", [FI, D])
    fgs = di("fgs", [FI])
    fus = di("fus", [FI])
    fds = di("fds", [D])
    gmc = di("gmc", [128, 64])
    m2c = di("m2c", [128, 8])

    out_d = nc.dram_tensor("out", [D, TOK], F32, kind="ExternalOutput").ap()

    def dsc(name, shape, dt=F32):
        return nc.dram_tensor(name, shape, dt).ap()

    xp_d = dsc("xp_d", [DXB, TOKP])
    b_d = dsc("b_d", [DXB, TOK])
    z_d = dsc("z_d", [DI, TOK])
    cpl_d = dsc("cpl_d", [H, DST * TOK])
    dt_d = dsc("dt_d", [DI, TOK])
    a_d = dsc("a_d", [DI, TOK])
    xcd_d = dsc("xcd_d", [DI, TOK])
    h_d = dsc("h_d", [NCT * 128 * DST * TOK])
    midh_d = dsc("midh_d", [MI, TOK], BF16)
    midl_d = dsc("midl_d", [MI, TOK], BF16)
    fmh_d = dsc("fmh_d", [FI, TOK], BF16)
    fml_d = dsc("fml_d", [FI, TOK], BF16)
    wouth_d = dsc("wouth_d", [DI, D], BF16)
    woutl_d = dsc("woutl_d", [DI, D], BF16)
    wdh_d = dsc("wdh_d", [MI, D], BF16)
    wdl_d = dsc("wdl_d", [MI, D], BF16)
    fdh_d = dsc("fdh_d", [FI, D], BF16)
    fdl_d = dsc("fdl_d", [FI, D], BF16)
    CCN = DI * DST + DI
    cc_in = nc.dram_tensor("cc_in", [1, CCN], F32)
    cc_out = nc.dram_tensor("cc_out", [NCORES, CCN], F32)

    S = Sched()

    avail = (nc.sbuf_top - nc.sbuf_base - 512) & ~31
    guard = nc.alloc_sbuf_tensor("manual_region", [128, avail], mybir.dt.uint8)
    gbase = nc.lookup_mloc(guard).addr
    off = [0]

    def sb(name, shape, dt):
        t = nc.alloc_sbuf_tensor_at(name, shape, dt, offset=gbase + off[0])
        nb = 1
        for s_ in shape[1:]:
            nb *= s_
        nb *= 4 if dt == F32 else 2
        off[0] += (nb + 31) & ~31
        return t

    A1W = max(16 * TOK, 15 * TOK + TOKP)
    A1 = sb("A1", [128, A1W], F32)
    A2 = sb("A2", [128, 16 * TOKP], BF16)
    A3 = sb("A3", [128, 16 * TOKP], BF16)
    A4 = sb("A4", [128, max(DST * SC, TOKP, 2 * TH)], F32)
    A5 = sb("A5", [128, max(DST * SC, TOKP)], F32)
    WF = sb("WF", [128, 2 * 2048], F32)
    WH = sb("WH", [128, 2 * 2048], BF16)
    WL = sb("WL", [128, 2 * 2048], BF16)
    ST1 = sb("ST1", [128, TOKP + 5], F32)
    ST2 = sb("ST2", [128, TOKP + 5], F32)
    PHS = sb("PHS", [128, TOK], BF16)
    PLS = sb("PLS", [128, TOK], BF16)
    ACOL = sb("ACOL", [128, NCT * DST], F32)
    HIN = sb("HIN", [128, NCT * DST], F32)
    SCO = sb("SCO", [128, max(NCT * DST, 8 * NCT)], F32)
    TK8 = SCO
    ATOT = sb("ATOT", [128, NCT], F32)
    GJ = sb("GJ", [128, NCT], F32)
    M2S = sb("M2S", [128, 8], F32)
    GMS = sb("GMS", [128, 64], F32)
    PK = sb("PK", [128, DCONV * NCT], F32)
    CB = sb("CB", [128, NCT], F32)
    DPC = sb("DPC", [128, NCT], F32)
    DTBC = sb("DTBC", [128, NCT], F32)
    L1C = sb("L1C", [128, KD], F32)
    L2C = sb("L2C", [128, KD], F32)
    FGSC = sb("FGSC", [128, NFI], F32)
    FUSC = sb("FUSC", [128, NFI], F32)
    FDSC = sb("FDSC", [128, KD], F32)
    ONE1 = sb("ONE1", [1, 128], F32)
    ONEC = sb("ONEC", [128, 1], F32)
    EPSC = sb("EPSC", [128, 1], F32)
    COLS = sb("COLS", [128, 44], F32)
    DTRH = sb("DTRH", [128, TOK], BF16)
    DTRL = sb("DTRL", [128, TOK], BF16)
    YCT = sb("YCT", [128, max(TH, NCT * DST)], F32)
    TRM = YCT
    RS = TOKP + 1
    ROWS = sb("ROWS", [1, 3 * RS], F32)
    R_A, R_B, R_C = 0, RS, 2 * RS
    assert off[0] <= avail, f"SBUF over budget: {off[0]} > {avail}"

    PS = [nc.psum_tensor(f"ps{i}", [128, 512], F32).__enter__() for i in range(8)]

    A1_dt = 0
    A1_xe = A1_dt + TOK
    A1_xc = A1_xe + TOKP
    A1_a = A1_xc + TOK
    A1_n0 = A1_a + TOK

    hres = lambda k: A1[:, k * TOK:(k + 1) * TOK]

    def midhi(par):
        return PHS[:, 0:TOK] if par == 0 else DTRH[:, 0:TOK]

    def midlo(par):
        return PLS[:, 0:TOK] if par == 0 else DTRL[:, 0:TOK]

    S.g(lambda e: e.memset(ONE1[:, :], 1.0), w=["one1"], tag="ones1")
    S.g(lambda e: e.memset(ONEC[:, :], 1.0), w=["onec"], tag="onec")
    S.g(lambda e: e.memset(HIN[:, :], 0.0), w=["hin"], tag="hin0")
    S.g(lambda e: e.memset(EPSC[:, :], float(EPS)), w=["epsc"], tag="epsc")
    S.g(lambda e: e.memset(COLS[:, :], 0.0), w=["cols"], tag="cols0")

    def packed(dst_ap, src_ap, n_ct, inner=1):
        def fn(e):
            s2 = ap4(src_ap, 0, [[inner, 128], [128 * inner, n_ct], [1, inner]])
            return e.dma_start(out=dst_ap, in_=s2, allow_slow_non_contiguous=True)
        return fn

    S.dW(packed(PK[:, :], convw, NCT, DCONV), w=["pk"], tag="pk")
    S.dW(packed(CB[:, :], convb, NCT), w=["cb"], tag="cb")
    S.dW(packed(DPC[:, :], dpar, NCT), w=["dpc"], tag="dpc")
    S.dW(packed(DTBC[:, :], dtb, NCT), w=["dtbc"], tag="dtbc")
    S.dW(packed(L1C[:, :], ln1w, KD), w=["l1c"], tag="l1c")
    S.dW(packed(L2C[:, :], ln2w, KD), w=["l2c"], tag="l2c")
    S.dW(packed(FGSC[:, :], fgs, NFI), w=["fgsc"], tag="fgsc")
    S.dW(packed(FUSC[:, :], fus, NFI), w=["fusc"], tag="fusc")
    S.dW(packed(FDSC[:, :], fds, KD), w=["fdsc"], tag="fdsc")
    S.dW(packed(ACOL[:, :], alog, NCT, DST), w=["acol"], tag="alog")
    S.dW(lambda e: e.dma_start(out=GMS[:, :], in_=gmc), w=["gms"], tag="gms")
    S.dW(lambda e: e.dma_start(out=M2S[:, :], in_=m2c), w=["m2s"], tag="m2s")
    S.s(lambda e: e.activation(ACOL[:, :], ACOL[:, :], AF.Exp),
        r=["acol"], w=["acol"], tag="acol_exp")
    S.v(lambda e: e.tensor_scalar(ACOL[:, :], ACOL[:, :], -1.0, None, ALU.mult),
        r=["acol"], w=["acol"], tag="acol_neg")

    schunks = list(range(0, TOKP, 128))
    nj = len(schunks)

    for k in range(KD):
        S.dW(lambda e, k=k: e.dma_start(out=ST2[:, 0:TOKP],
                                        in_=x_in[k * 128:(k + 1) * 128, :]),
             r=["st2f"], w=["st2f"], tag=f"p1.ldx{k}")
        S.s(lambda e: e.activation(ST1[:, 0:TOKP], ST2[:, 0:TOKP], AF.Square),
            r=["st2f", "st1x"], w=["st1x"], tag=f"p1.sq{k}")
        def fmm(e, k=k):
            inst = None
            for j, c0 in enumerate(schunks):
                m = min(128, TOKP - c0)
                e.matmul(PS[6][0:m, j:j + 1], ST2[:, c0:c0 + m], ONEC[:, :],
                         start=(k == 0 and j == 0), stop=(k == KD - 1 and j == nj - 1),
                         skip_group_check=True)
                inst = e.matmul(PS[7][0:m, j:j + 1], ST1[:, c0:c0 + m], ONEC[:, :],
                                start=(k == 0 and j == 0), stop=(k == KD - 1 and j == nj - 1),
                                skip_group_check=True)
            return inst
        S.t(fmm, r=["st2f", "st1x", "onec"], w=["ps6", "ps7"], tag=f"p1.mm{k}")

    def ev_cols(bank, coff, chlist, width, tag):
        def fn(e, bank=bank, coff=coff):
            inst = None
            for j, c0 in enumerate(chlist):
                m = min(128, width - c0)
                inst = e.tensor_copy(COLS[0:m, coff + j:coff + j + 1],
                                     PS[bank][0:m, j:j + 1])
            return inst
        S.v(fn, r=[f"ps{bank}", "cols"], w=["cols"], tag=tag)

    ev_cols(6, 0, schunks, TOKP, "p1.ev1")
    ev_cols(7, 32, schunks, TOKP, "p1.ev2")

    def col2row(dst_off, src_off, n, width, tag):
        def fn(e):
            insts = []
            for j in range(n):
                m = min(128, width - j * 128)
                insts.append(e.dma_start(
                    out=ROWS[0:1, dst_off + j * 128:dst_off + j * 128 + m],
                    in_=COLS[0:m, src_off + j:src_off + j + 1]))
            return insts
        return self_dS(fn, n, tag)

    def self_dS(fn, n, tag):
        return S.emit("gpsimd", fn, ("cols",), ("rows",), dma=True, tag=tag,
                      nmul=n)

    col2row(R_A, 0, nj, TOKP, "p1.ra")
    col2row(R_B, 32, nj, TOKP, "p1.rb")
    S.v(lambda e: e.tensor_scalar(ROWS[0:1, R_A:R_A + TOKP], ROWS[0:1, R_A:R_A + TOKP],
                                  1.0 / D, None, ALU.mult),
        r=["rows"], w=["rows"], tag="p1.mean")
    S.v(lambda e: e.tensor_tensor(ROWS[0:1, R_C:R_C + TOKP], ROWS[0:1, R_A:R_A + TOKP],
                                  ROWS[0:1, R_A:R_A + TOKP], ALU.mult),
        r=["rows"], w=["rows"], tag="p1.m2")
    S.v(lambda e: e.scalar_tensor_tensor(ROWS[0:1, R_B:R_B + TOKP], ROWS[0:1, R_B:R_B + TOKP],
                                         1.0 / D, ROWS[0:1, R_C:R_C + TOKP],
                                         op0=ALU.mult, op1=ALU.subtract),
        r=["rows"], w=["rows"], tag="p1.var")
    S.s(lambda e: e.activation(ROWS[0:1, R_C:R_C + TOKP], ROWS[0:1, R_B:R_B + TOKP],
                               AF.Ln, bias=EPSC[0:1, :]),
        r=["rows", "epsc"], w=["rows"], tag="p1.ln")
    S.s(lambda e: e.activation(ROWS[0:1, R_C:R_C + TOKP], ROWS[0:1, R_C:R_C + TOKP],
                               AF.Exp, scale=-0.5),
        r=["rows"], w=["rows"], tag="p1.s")
    S.v(lambda e: e.tensor_tensor(ROWS[0:1, R_B:R_B + TOKP], ROWS[0:1, R_B:R_B + TOKP],
                                  ROWS[0:1, R_C:R_C + TOKP], ALU.mult),
        r=["rows"], w=["rows"], tag="p1.vs")
    S.v(lambda e: e.tensor_tensor(ROWS[0:1, R_B:R_B + TOKP], ROWS[0:1, R_B:R_B + TOKP],
                                  ROWS[0:1, R_C:R_C + TOKP], ALU.mult),
        r=["rows"], w=["rows"], tag="p1.vss")
    S.s(lambda e: e.activation(ROWS[0:1, R_B:R_B + TOKP], ROWS[0:1, R_B:R_B + TOKP],
                               AF.Ln, bias=EPSC[0:1, :]),
        r=["rows", "epsc"], w=["rows"], tag="p1.ln2")
    S.s(lambda e: e.activation(ROWS[0:1, R_B:R_B + TOKP], ROWS[0:1, R_B:R_B + TOKP],
                               AF.Exp, scale=-0.5),
        r=["rows"], w=["rows"], tag="p1.rmsi")
    S.v(lambda e: e.tensor_tensor(ROWS[0:1, R_B:R_B + TOKP], ROWS[0:1, R_B:R_B + TOKP],
                                  ROWS[0:1, R_C:R_C + TOKP], ALU.mult),
        r=["rows"], w=["rows"], tag="p1.srr")

    def rep_row(row_off, dst, dst_tok, width, tag):
        mch = list(range(0, width, 512))
        for j, c0 in enumerate(mch):
            w_ = min(512, width - c0)
            bank = j % 2
            S.t(lambda e, c0=c0, w_=w_, bank=bank, row_off=row_off: e.matmul(
                    PS[bank][:, 0:w_], ONE1[:, :],
                    ROWS[0:1, row_off + c0:row_off + c0 + w_], start=True, stop=True),
                r=["rows", "one1"], w=[f"ps{bank}"], tag=f"{tag}.{j}")
            S.v(lambda e, c0=c0, w_=w_, bank=bank: e.tensor_copy(
                    dst[:, c0:c0 + w_], PS[bank][:, 0:w_]),
                r=[f"ps{bank}"], w=list(dst_tok), tag=f"{tag}.ev{j}")

    rep_row(R_A, A4, ("a4:0", "a4:1"), TOKP, "p1.repm")
    rep_row(R_B, A5, ("a5:0", "a5:1"), TOKP, "p1.reps")

    for k in range(KD):
        S.dW(lambda e, k=k: e.dma_start(out=ST2[:, 0:TOKP],
                                        in_=x_in[k * 128:(k + 1) * 128, :]),
             r=["st2f"], w=["st2f"], tag=f"p1b.ldx{k}")
        S.v(lambda e: e.tensor_tensor(ST1[:, 0:TOKP], ST2[:, 0:TOKP], A4[:, 0:TOKP],
                                      ALU.subtract),
            r=["st2f", "a4:0", "a4:1", "st1x"], w=["st1x"], tag=f"p1b.sub{k}")
        S.v(lambda e, k=k: e.scalar_tensor_tensor(ST1[:, 0:TOKP], ST1[:, 0:TOKP],
                                                  L1C[:, k:k + 1], A5[:, 0:TOKP],
                                                  op0=ALU.mult, op1=ALU.mult),
            r=["st1x", "l1c", "a5:0", "a5:1"], w=["st1x"], tag=f"p1b.r1{k}")
        S.s(lambda e, k=k: e.activation(A2[:, k * TOKP:(k + 1) * TOKP], ST1[:, 0:TOKP],
                                        AF.Copy),
            r=["st1x"], w=["a2"], tag=f"p1b.hi{k}")
        S.v(lambda e, k=k: e.tensor_tensor(A3[:, k * TOKP:(k + 1) * TOKP], ST1[:, 0:TOKP],
                                           A2[:, k * TOKP:(k + 1) * TOKP], ALU.subtract),
            r=["st1x", "a2"], w=["a3"], tag=f"p1b.lo{k}")

    r1h = lambda k: A2[:, k * TOKP:(k + 1) * TOKP]
    r1l = lambda k: A3[:, k * TOKP:(k + 1) * TOKP]
    r1toks = ["a2", "a3"]

    def load_panel_pcols(wap, pt, ncols_w, par, tag):
        def fn(e, par=par):
            return e.dma_start(out=WF[:, par * 2048:par * 2048 + KD * 128],
                               in_=wap[pt * 128:(pt + 1) * 128, :])
        return S.dW(fn, r=[f"wf:{par}"], w=[f"wf:{par}"], tag=tag)

    def split_panel(par, ncols, tag, lo_eng="vector"):
        S.s(lambda e, par=par, ncols=ncols: e.activation(
                WH[:, par * 2048:par * 2048 + ncols],
                WF[:, par * 2048:par * 2048 + ncols], AF.Copy),
            r=[f"wf:{par}", f"whl:{par}"], w=[f"whl:{par}"], tag=tag + ".hi")
        S.emit(lo_eng, lambda e, par=par, ncols=ncols: e.tensor_tensor(
                WL[:, par * 2048:par * 2048 + ncols],
                WF[:, par * 2048:par * 2048 + ncols],
                WH[:, par * 2048:par * 2048 + ncols], ALU.subtract),
            [f"wf:{par}", f"whl:{par}"], [f"whl:{par}", f"wf:{par}"], tag=tag + ".lo")

    def wtile(par, k):
        return (WH[:, par * 2048 + k * 128:par * 2048 + (k + 1) * 128],
                WL[:, par * 2048 + k * 128:par * 2048 + (k + 1) * 128])

    def mm3(e, psum_ap, wh, wl, ah, al, first, last):
        e.matmul(psum_ap, wh, ah, start=first, stop=False)
        e.matmul(psum_ap, wh, al, start=False, stop=False)
        return e.matmul(psum_ap, wl, ah, start=False, stop=last)

    def proj_mm(par, ah, al, atoks, c0, c1, bank, tag, nk=None):
        nk_ = nk or KD
        def fn(e):
            inst = None
            for k in range(nk_):
                wh, wl = wtile(par, k)
                inst = mm3(e, PS[bank][:, 0:c1 - c0], wh, wl,
                           ah(k)[:, c0:c1], al(k)[:, c0:c1], k == 0, k == nk_ - 1)
            return inst
        return S.t(fn, r=[f"whl:{par}"] + atoks, w=[f"ps{bank}"], tag=tag)

    stg = [0]

    def stage():
        stg[0] ^= 1
        return (ST1, "st1x") if stg[0] else (ST2, "st2f")

    bankrot = [0]

    def load_proj_tile(pt):
        par = pt % 2
        load_panel_pcols(w1p, pt, KD * 128, par, f"p2.w{pt}")

    def do_proj_tile(kind, pt):
        par = pt % 2
        split_panel(par, KD * 128, f"p2.s{pt}", lo_eng="gpsimd")
        if kind == "x":
            chunks = [(c0, min(c0 + 512, TOKP)) for c0 in range(0, TOKP, 512)]
        else:
            chunks = [(3 + i * TC, 3 + (i + 1) * TC) for i in range(NTC)]
        for (c0, c1) in chunks:
            bank = bankrot[0] % 4
            bankrot[0] += 1
            proj_mm(par, r1h, r1l, r1toks, c0, c1, bank, f"p2.mm{pt}.{c0}")
            w_ = c1 - c0
            STG, stok = stage()
            S.s(lambda e, bank=bank, STG=STG, w_=w_: e.activation(
                    STG[:, 0:w_], PS[bank][:, 0:w_], AF.Copy),
                r=[f"ps{bank}", stok], w=[stok], tag=f"p2.ev{pt}.{c0}")
            if kind == "x":
                xi = pt - PT_X0
                S.dS(lambda e, xi=xi, c0=c0, w_=w_, STG=STG: e.dma_start(
                        out=xp_d[xi * 128:(xi + 1) * 128, c0:c0 + w_], in_=STG[:, 0:w_]),
                     r=[stok], w=[f"xp:{xi}"], tag=f"p2.sp{pt}.{c0}")
            elif kind == "B":
                bi = pt - PT_B0
                S.dS(lambda e, bi=bi, c0=c0, w_=w_, STG=STG: e.dma_start(
                        out=b_d[bi * 128:(bi + 1) * 128, c0 - 3:c0 - 3 + w_],
                        in_=STG[:, 0:w_]),
                     r=[stok], w=[f"bd:{bi}"], tag=f"p2.sp{pt}.{c0}")
            elif kind == "z":
                zi = pt - PT_Z0
                S.dS(lambda e, zi=zi, c0=c0, w_=w_, STG=STG: e.dma_start(
                        out=z_d[zi * 128:(zi + 1) * 128, c0 - 3:c0 - 3 + w_],
                        in_=STG[:, 0:w_]),
                     r=[stok], w=[f"zd:{zi}"], tag=f"p2.sp{pt}.{c0}")
            elif kind == "C":
                ci = pt - PT_C0
                def spc(e, ci=ci, c0=c0, w_=w_, STG=STG):
                    dst = ap4(cpl_d, (ci * HPC) * DST * TOK + (c0 - 3),
                              [[DST * TOK, HPC], [TOK, DST], [1, w_]])
                    return e.dma_start(out=dst, in_=STG[:, 0:w_])
                S.dS(spc, r=[stok], w=[f"cpl:{ci}"], tag=f"p2.sp{pt}.{c0}")
            elif kind == "dtr":
                S.s(lambda e, c0=c0, w_=w_, STG=STG: e.activation(
                        DTRH[:, c0 - 3:c0 - 3 + w_], STG[:, 0:w_], AF.Copy),
                    r=[stok, "dtr"], w=["dtr"], tag=f"p2.dh.{c0}")
                S.v(lambda e, c0=c0, w_=w_, STG=STG: e.tensor_tensor(
                        DTRL[:, c0 - 3:c0 - 3 + w_], STG[:, 0:w_],
                        DTRH[:, c0 - 3:c0 - 3 + w_], ALU.subtract),
                    r=[stok, "dtr"], w=["dtr"], tag=f"p2.dl.{c0}")

    for i in range(NXT):
        load_proj_tile(PT_X0 + i)
        do_proj_tile("x", PT_X0 + i)
    load_proj_tile(PT_DT)
    do_proj_tile("dtr", PT_DT)
    for i in range(NXT):
        load_proj_tile(PT_B0 + i)
        do_proj_tile("B", PT_B0 + i)

    nhalf = NCT // 2
    for half in range(2):
        par = half
        def ldw(e, half=half):
            sap = ap4(dtwt, half * nhalf * 128, [[DI, 128], [1, nhalf * 128]])
            return e.dma_start(out=WF[:, half * 2048:half * 2048 + nhalf * 128],
                               in_=sap)
        S.dW(ldw, r=[f"wf:{half}"], w=[f"wf:{half}"], tag=f"p2d.w{half}")
        split_panel(par, nhalf * 128, f"p2d.s{half}")
        for j in range(nhalf):
            ct = half * nhalf + j
            for i in range(NTC):
                bank = 4 + ((ct * NTC + i) % 2)
                def fmm(e, par=par, j=j, bank=bank, i=i):
                    wh, wl = wtile(par, j)
                    return mm3(e, PS[bank][:, 0:TC], wh, wl,
                               DTRH[:, i * TC:(i + 1) * TC],
                               DTRL[:, i * TC:(i + 1) * TC], True, True)
                S.t(fmm, r=[f"whl:{par}", "dtr"], w=[f"ps{bank}"],
                    tag=f"p2d.mm{ct}.{i}")
                SA, atok = stage()
                SB, btok = stage()
                S.v(lambda e, ct=ct, bank=bank, SA=SA: e.tensor_scalar(
                        SA[:, 0:TC], PS[bank][:, 0:TC], DTBC[:, ct:ct + 1], None,
                        ALU.add),
                    r=[f"ps{bank}", "dtbc", atok], w=[atok], tag=f"p2d.t{ct}.{i}")
                S.s(lambda e, SA=SA, SB=SB: e.activation(SB[:, 0:TC], SA[:, 0:TC],
                                                         AF.Abs),
                    r=[atok, btok], w=[btok], tag=f"p2d.ab{ct}.{i}")
                S.s(lambda e, SB=SB: e.activation(SB[:, 0:TC], SB[:, 0:TC], AF.Exp,
                                                  scale=-1.0),
                    r=[btok], w=[btok], tag=f"p2d.ex{ct}.{i}")
                S.s(lambda e, SB=SB: e.activation(SB[:, 0:TC], SB[:, 0:TC], AF.Ln,
                                                  bias=ONEC[:, 0:1]),
                    r=[btok, "onec"], w=[btok], tag=f"p2d.ln{ct}.{i}")
                S.s(lambda e, SA=SA: e.activation(SA[:, 0:TC], SA[:, 0:TC], AF.Relu),
                    r=[atok], w=[atok], tag=f"p2d.rl{ct}.{i}")
                S.v(lambda e, SA=SA, SB=SB: e.tensor_tensor(
                        SA[:, 0:TC], SA[:, 0:TC], SB[:, 0:TC], ALU.add),
                    r=[atok, btok], w=[atok], tag=f"p2d.sp{ct}.{i}")
                S.dS(lambda e, ct=ct, i=i, SA=SA: e.dma_start(
                        out=dt_d[ct * 128:(ct + 1) * 128, i * TC:(i + 1) * TC],
                        in_=SA[:, 0:TC]),
                     r=[atok], w=[f"dtd:{ct}"], tag=f"p2d.spl{ct}.{i}")

    def scan_ct(ct):
        hxb0 = (ct * HPC) // REP
        xpt = (hxb0 * DST) // 128
        S.dW(lambda e, ct=ct: e.dma_start(out=A1[:, A1_dt:A1_dt + TOK],
                                          in_=dt_d[ct * 128:(ct + 1) * 128, :]),
             r=[f"dtd:{ct}", "a1dt"], w=["a1dt"], tag=f"sc.dt{ct}")
        PB = REP * DST
        def ldxe(e, hxb0=hxb0, PB=PB):
            insts = []
            for hx in range(NHXB):
                sap = ap4(xp_d, (hxb0 + hx) * DST * TOKP,
                          [[0, REP], [TOKP, DST], [1, TOKP]])
                insts.append(e.dma_start(
                    out=A1[hx * PB:(hx + 1) * PB, A1_xe:A1_xe + TOKP], in_=sap))
            return insts
        S.dW(ldxe, r=[f"xp:{xpt}", "a1xe"], w=["a1xe"], tag=f"sc.xe{ct}", nmul=NHXB)
        S.v(lambda e, ct=ct: e.tensor_scalar(A1[:, A1_xc:A1_xc + TOK],
                                             A1[:, A1_xe:A1_xe + TOK],
                                             PK[:, ct * DCONV:ct * DCONV + 1], None,
                                             ALU.mult),
            r=["a1xe", "pk", "a1xc"], w=["a1xc"], tag=f"sc.c0.{ct}")
        for kk in range(1, DCONV):
            S.v(lambda e, ct=ct, kk=kk: e.scalar_tensor_tensor(
                    A1[:, A1_xc:A1_xc + TOK], A1[:, A1_xe + kk:A1_xe + kk + TOK],
                    PK[:, ct * DCONV + kk:ct * DCONV + kk + 1],
                    A1[:, A1_xc:A1_xc + TOK], op0=ALU.mult, op1=ALU.add),
                r=["a1xe", "pk", "a1xc"], w=["a1xc"], tag=f"sc.c{kk}.{ct}")
        S.s(lambda e, ct=ct: e.activation(A1[:, A1_xc:A1_xc + TOK],
                                          A1[:, A1_xc:A1_xc + TOK], AF.Silu,
                                          bias=CB[:, ct:ct + 1]),
            r=["a1xc", "cb"], w=["a1xc"], tag=f"sc.silu{ct}")
        S.v(lambda e, ct=ct: e.tensor_scalar(A1[:, A1_xe:A1_xe + TOK],
                                             A1[:, A1_xc:A1_xc + TOK],
                                             DPC[:, ct:ct + 1], None, ALU.mult),
            r=["a1xc", "dpc", "a1xe"], w=["a1xe"], tag=f"sc.xcd{ct}")
        S.dS(lambda e, ct=ct: e.dma_start(out=xcd_d[ct * 128:(ct + 1) * 128, :],
                                          in_=A1[:, A1_xe:A1_xe + TOK]),
             r=["a1xe"], w=[f"xcd:{ct}"], tag=f"sc.spxcd{ct}")
        S.v(lambda e: e.tensor_tensor(A1[:, A1_xc:A1_xc + TOK], A1[:, A1_dt:A1_dt + TOK],
                                      A1[:, A1_xc:A1_xc + TOK], ALU.mult),
            r=["a1dt", "a1xc"], w=["a1xc"], tag=f"sc.dtx{ct}")
        S.v(lambda e: e.tensor_tensor_scan(A1[:, A1_a:A1_a + TOK], A1[:, A1_dt:A1_dt + TOK],
                                           A1[:, A1_dt:A1_dt + TOK], initial=0.0,
                                           op0=ALU.add, op1=ALU.bypass),
            r=["a1dt", "a1a"], w=["a1a"], tag=f"sc.cum{ct}")
        S.dS(lambda e, ct=ct: e.dma_start(out=a_d[ct * 128:(ct + 1) * 128, :],
                                          in_=A1[:, A1_a:A1_a + TOK]),
             r=["a1a"], w=[f"ad:{ct}"], tag=f"sc.spa{ct}")
        S.v(lambda e, ct=ct: e.tensor_copy(ATOT[:, ct:ct + 1],
                                           A1[:, A1_a + TOK - 1:A1_a + TOK]),
            r=["a1a", "atot"], w=["atot"], tag=f"sc.atot{ct}")
        PB = REP * DST
        for np_ in range(DST // 2):
            n0 = 2 * np_
            par = np_ % 3
            base = A1_n0 + par * 4 * TOK
            bA, bB = base, base + 2 * TOK
            bH = bA
            ntA = f"a1nA:{par}"
            ntB = f"a1nB:{par}"
            def dA2(e, ct=ct, n0=n0, bA=bA):
                e.activation(A1[:, bA:bA + TOK], A1[:, A1_dt:A1_dt + TOK], AF.Exp,
                             scale=ACOL[:, ct * DST + n0:ct * DST + n0 + 1])
                return e.activation(A1[:, bA + TOK:bA + 2 * TOK],
                                    A1[:, A1_dt:A1_dt + TOK], AF.Exp,
                                    scale=ACOL[:, ct * DST + n0 + 1:ct * DST + n0 + 2])
            S.s(dA2, r=["a1dt", "acol", ntA], w=[ntA], tag=f"sc.dA{ct}.{n0}")
            def ldb(e, n0=n0, hxb0=hxb0, bB=bB):
                insts = []
                for hx in range(NHXB):
                    sap = ap4(b_d, ((hxb0 + hx) * DST + n0) * TOK,
                              [[0, PB], [TOK, 2], [1, TOK]])
                    insts.append(e.dma_start(
                        out=A1[hx * PB:(hx + 1) * PB, bB:bB + 2 * TOK], in_=sap))
                return insts
            S.emit("sync", ldb, [f"bd:{(hxb0 * DST) // 128}", ntB], [ntB],
                   dma=True, tag=f"sc.B{ct}.{n0}", nmul=NHXB)
            def dbx2(e, bB=bB):
                i0 = A1[:, A1_xc:A1_xc + TOK].unsqueeze(1).broadcast_to((128, 2, TOK))
                i1 = A1[:, bB:bB + 2 * TOK].rearrange("p (n t) -> p n t", n=2)
                o = A1[:, bB:bB + 2 * TOK].rearrange("p (n t) -> p n t", n=2)
                return e.tensor_tensor(o, i0, i1, ALU.mult)
            S.v(dbx2, r=["a1xc", ntB], w=[ntB], tag=f"sc.dBx{ct}.{n0}")
            for h2 in range(2):
                S.v(lambda e, bA=bA, bB=bB, bH=bH, h2=h2: e.tensor_tensor_scan(
                        A1[:, bH + h2 * TOK:bH + (h2 + 1) * TOK],
                        A1[:, bA + h2 * TOK:bA + (h2 + 1) * TOK],
                        A1[:, bB + h2 * TOK:bB + (h2 + 1) * TOK],
                        initial=0.0, op0=ALU.mult, op1=ALU.add),
                    r=[ntA, ntB], w=[ntA], tag=f"sc.tts{ct}.{n0 + h2}")
            def sph2(e, ct=ct, n0=n0, bH=bH):
                dst = ap4(h_d, ct * 128 * DST * TOK + n0 * TOK,
                          [[DST * TOK, 128], [TOK, 2], [1, TOK]])
                return e.dma_start(out=dst, in_=A1[:, bH:bH + 2 * TOK])
            S.dS(sph2, r=[ntA], w=[f"hd:{ct}"], tag=f"sc.sph{ct}.{n0}")
            def sco2(e, ct=ct, n0=n0, bH=bH):
                e.tensor_copy(SCO[:, ct * DST + n0:ct * DST + n0 + 1],
                              A1[:, bH + TOK - 1:bH + TOK])
                return e.tensor_copy(SCO[:, ct * DST + n0 + 1:ct * DST + n0 + 2],
                                     A1[:, bH + 2 * TOK - 1:bH + 2 * TOK])
            S.g(sco2, r=[ntA, "sco"], w=["sco"], tag=f"sc.sco{ct}.{n0}")

    zc_tiles = [("z", PT_Z0 + i) for i in range(DI // 128)] + \
               [("C", PT_C0 + i) for i in range(DI // 128)]
    for i in range(min(2, len(zc_tiles))):
        load_proj_tile(zc_tiles[i][1])
    gidx = [0]
    def emit_group():
        g = gidx[0]
        for j in range(2 * g, 2 * g + 2):
            if j < len(zc_tiles):
                do_proj_tile(*zc_tiles[j])
            jl = j + 2
            if jl < len(zc_tiles):
                load_proj_tile(zc_tiles[jl][1])
        gidx[0] += 1
    for ct in range(NCT):
        emit_group()
        scan_ct(ct)
    while 2 * gidx[0] < len(zc_tiles):
        emit_group()

    S.dS(lambda e: e.dma_start(
            out=ap4(cc_in.ap(), 0, [[NCT * DST, 128], [1, NCT * DST]]),
            in_=SCO[:, :]),
         r=["sco"], w=["ccin"], tag="p3.spS")
    S.dS(lambda e: e.dma_start(
            out=ap4(cc_in.ap(), DI * DST, [[NCT, 128], [1, NCT]]),
            in_=ATOT[:, :]),
         r=["atot"], w=["ccin"], tag="p3.spT")
    S.emit("gpsimd", lambda e: e.collective_compute(
            "AllGather", ALU.bypass, replica_groups=[list(range(NCORES))],
            ins=[cc_in.ap().opt()], outs=[cc_out.ap().opt()]),
        r=["ccin"], w=["ccout"], tag="p3.ag", force=True)
    for j in range(NCORES):
        def ldt(e, j=j):
            src = ap4(cc_out.ap(), j * CCN + DI * DST, [[NCT, 128], [1, NCT]])
            return e.dma_start(out=TK8[:, j * NCT:(j + 1) * NCT], in_=src)
        S.dR(ldt, r=["ccout", "sco"], w=["sco"], tag=f"p3.t{j}")
    for j in range(NCORES):
        S.v(lambda e, j=j: e.tensor_scalar(GJ[:, :], TK8[:, 0:NCT],
                                           GMS[:, j * 8:j * 8 + 1], None, ALU.mult),
            r=["sco", "gms", "gj"], w=["gj"], tag=f"p3.g0.{j}")
        for k in range(1, NCORES):
            S.v(lambda e, j=j, k=k: e.scalar_tensor_tensor(
                    GJ[:, :], TK8[:, k * NCT:(k + 1) * NCT],
                    GMS[:, j * 8 + k:j * 8 + k + 1], GJ[:, :],
                    op0=ALU.mult, op1=ALU.add),
                r=["sco", "gms", "gj"], w=["gj"], tag=f"p3.g{k}.{j}")
        S.v(lambda e: e.tensor_tensor(
                TRM[:, :].rearrange("p (c n) -> p c n", n=DST),
                ACOL[:, :].rearrange("p (c n) -> p c n", n=DST),
                GJ[:, :].unsqueeze(-1).broadcast_to((128, NCT, DST)), ALU.mult),
            r=["acol", "gj", "yct"], w=["yct"], tag=f"p3.arg{j}")
        S.s(lambda e: e.activation(TRM[:, :], TRM[:, :], AF.Exp),
            r=["yct"], w=["yct"], tag=f"p3.exp{j}")
        def lds(e, j=j):
            src = ap4(cc_out.ap(), j * CCN, [[NCT * DST, 128], [1, NCT * DST]])
            return e.dma_start(out=ST1[:, 0:NCT * DST], in_=src)
        S.dR(lds, r=["ccout", "st1x"], w=["st1x"], tag=f"p3.s{j}")
        S.v(lambda e, j=j: e.scalar_tensor_tensor(
                TRM[:, :], TRM[:, :], M2S[:, j:j + 1], ST1[:, 0:NCT * DST],
                op0=ALU.mult, op1=ALU.mult),
            r=["yct", "m2s", "st1x"], w=["yct"], tag=f"p3.msk{j}")
        S.v(lambda e: e.tensor_tensor(HIN[:, :], HIN[:, :], TRM[:, :], ALU.add),
            r=["hin", "yct"], w=["hin"], tag=f"p3.acc{j}")

    PGS = min(6, cfg.get("PGS", 6))
    npg = (KD + PGS - 1) // PGS

    def ldwout(e, kct, par):
        sap = ap4(woutt, kct * 128 * D, [[D, 128], [1, D]])
        return e.dma_start(out=WF[:, par * 2048:par * 2048 + D], in_=sap)

    def p4_wload(kct, th, pg):
        par = kct % 2
        if th == 0 and pg == 0:
            S.dW(lambda e, kct=kct, par=par: ldwout(e, kct, par),
                 r=[f"wf:{par}"], w=[f"wf:{par}"], tag=f"p4.w{kct}")
            split_panel(par, D, f"p4.ws{kct}", lo_eng="gpsimd")
            if npg > 1 or NTH > 1:
                S.dS(lambda e, kct=kct, par=par: e.dma_start(
                        out=wouth_d[kct * 128:(kct + 1) * 128, :],
                        in_=WH[:, par * 2048:par * 2048 + D]),
                     r=[f"whl:{par}"], w=["woh"], tag=f"p4.swh{kct}")
                S.dS(lambda e, kct=kct, par=par: e.dma_start(
                        out=woutl_d[kct * 128:(kct + 1) * 128, :],
                        in_=WL[:, par * 2048:par * 2048 + D]),
                     r=[f"whl:{par}"], w=["wol"], tag=f"p4.swl{kct}")
        else:
            S.dW(lambda e, kct=kct, par=par: e.dma_start(
                    out=WH[:, par * 2048:par * 2048 + D],
                    in_=wouth_d[kct * 128:(kct + 1) * 128, :]),
                 r=["woh", f"whl:{par}"], w=[f"whl:{par}"], tag=f"p4.rwh{kct}.{pg}.{th}")
            S.dW(lambda e, kct=kct, par=par: e.dma_start(
                    out=WL[:, par * 2048:par * 2048 + D],
                    in_=woutl_d[kct * 128:(kct + 1) * 128, :]),
                 r=["wol", f"whl:{par}"], w=[f"whl:{par}"], tag=f"p4.rwl{kct}.{pg}.{th}")

    def p4_mm(kct, th, pg):
        par = kct % 2
        p_lo = pg * PGS
        p_n = min(PGS, KD - p_lo)
        def fmm(e, kct=kct, par=par, p_lo=p_lo, p_n=p_n):
            inst = None
            for pi in range(p_n):
                p = p_lo + pi
                wh = WH[:, par * 2048 + p * 128:par * 2048 + (p + 1) * 128]
                wl = WL[:, par * 2048 + p * 128:par * 2048 + (p + 1) * 128]
                inst = mm3(e, PS[pi][:, 0:TH], wh, wl,
                           A2[:, kct * TH:(kct + 1) * TH],
                           A3[:, kct * TH:(kct + 1) * TH],
                           kct == 0, kct == NCT - 1)
            return inst
        S.t(fmm, r=[f"whl:{par}", "a2", "a3"],
            w=[f"ps{pi}" for pi in range(p_n)], tag=f"p4.mm{pg}.{kct}.{th}")

    def p4_evac(th, pg):
        t0 = th * TH
        p_lo = pg * PGS
        p_n = min(PGS, KD - p_lo)
        for pi in range(p_n):
            p = p_lo + pi
            S.dR(lambda e, p=p, t0=t0: e.dma_start(
                    out=ST1[:, 0:TH],
                    in_=x_in[p * 128:(p + 1) * 128, 3 + t0:3 + t0 + TH]),
                 r=["st1x"], w=["st1x"], tag=f"p4.x{p}.{th}")
            S.v(lambda e: e.tensor_tensor(ST2[:, 0:TH], ST1[:, 0:TH], PS[6][:, 0:TH],
                                          ALU.subtract),
                r=["st1x", "ps6", "st2f"], w=["st2f"], tag=f"p4.sub{p}.{th}")
            S.v(lambda e: e.tensor_tensor(ST2[:, 0:TH], ST2[:, 0:TH], PS[7][:, 0:TH],
                                          ALU.mult),
                r=["st2f", "ps7"], w=["st2f"], tag=f"p4.lnx{p}.{th}")
            S.v(lambda e, pi=pi, p=p, t0=t0: e.tensor_tensor(
                    hres(p)[:, t0:t0 + TH], PS[pi][:, 0:TH], ST2[:, 0:TH], ALU.add),
                r=[f"ps{pi}", "st2f", f"hr:{p}"], w=[f"hr:{p}"], tag=f"p4.hres{p}.{th}")

    for th in range(NTH):
        t0 = th * TH
        S.t(lambda e, t0=t0: e.matmul(PS[6][:, 0:TH], ONE1[:, :],
                                      ROWS[0:1, R_A + 3 + t0:R_A + 3 + t0 + TH],
                                      start=True, stop=True),
            r=["rows", "one1", "ps6"], w=["ps6"], tag=f"p4.repm.{th}")
        S.t(lambda e, t0=t0: e.matmul(PS[7][:, 0:TH], ONE1[:, :],
                                      ROWS[0:1, R_C + 3 + t0:R_C + 3 + t0 + TH],
                                      start=True, stop=True),
            r=["rows", "one1", "ps7"], w=["ps7"], tag=f"p4.reps.{th}")
        for ct in range(NCT):
            p4_wload(ct, th, 0)
            S.dR(lambda e, ct=ct, t0=t0: e.dma_start(
                    out=YCT[:, 0:TH], in_=xcd_d[ct * 128:(ct + 1) * 128, t0:t0 + TH]),
                 r=[f"xcd:{ct}", "yct"], w=["yct"], tag=f"p4.xcd{ct}.{th}")
            S.dR(lambda e, ct=ct, t0=t0: e.dma_start(
                    out=ST1[:, 0:TH], in_=a_d[ct * 128:(ct + 1) * 128, t0:t0 + TH]),
                 r=[f"ad:{ct}", "st1x"], w=["st1x"], tag=f"p4.a{ct}.{th}")
            hslots = [A4[:, 0:TH], A4[:, TH:2 * TH]]
            cslots = [A5[:, 0:TH], A5[:, TH:2 * TH]]
            htoks = ["a4:0", "a4:1"]
            ctoks = ["a5:0", "a5:1"]
            if TOK // 2 >= TH:
                hslots += [PHS[:, :].bitcast(F32)[:, 0:TH],
                           DTRH[:, :].bitcast(F32)[:, 0:TH]]
                cslots += [PLS[:, :].bitcast(F32)[:, 0:TH],
                           DTRL[:, :].bitcast(F32)[:, 0:TH]]
                htoks += ["ph:0", "ph:1"]
                ctoks += ["ph:0", "ph:1"]
            npar4 = len(hslots)
            for n in range(DST):
                par = n % npar4
                htok, ctok = htoks[par], ctoks[par]
                hsl = hslots[par]
                csl = cslots[par]
                def ldh(e, ct=ct, n=n, t0=t0, hsl=hsl):
                    sap = ap4(h_d, ct * 128 * DST * TOK + n * TOK + t0,
                              [[DST * TOK, 128], [1, TH]])
                    return e.dma_start(out=hsl, in_=sap)
                S.dW(ldh, r=[f"hd:{ct}", htok], w=[htok], tag=f"p4.h{ct}.{n}.{th}")
                def ldc(e, ct=ct, n=n, t0=t0, csl=csl):
                    sap = ap4(cpl_d, (ct * HPC) * DST * TOK + n * TOK + t0,
                              [[DST * TOK, HPC], [0, DST], [1, TH]])
                    return e.dma_start(out=csl, in_=sap)
                S.dW(ldc, r=[f"cpl:{ct}", ctok], w=[ctok], tag=f"p4.c{ct}.{n}.{th}")
                S.s(lambda e, ct=ct, n=n: e.activation(
                        ST2[:, 0:TH], ST1[:, 0:TH], AF.Exp,
                        scale=ACOL[:, ct * DST + n:ct * DST + n + 1]),
                    r=["st1x", "acol", "st2f"], w=["st2f"], tag=f"p4.f{ct}.{n}.{th}")
                S.v(lambda e, ct=ct, n=n, hsl=hsl: e.scalar_tensor_tensor(
                        hsl, ST2[:, 0:TH], HIN[:, ct * DST + n:ct * DST + n + 1],
                        hsl, op0=ALU.mult, op1=ALU.add),
                    r=["st2f", "hin", htok], w=[htok], tag=f"p4.hf{ct}.{n}.{th}")
                S.g(lambda e, hsl=hsl, csl=csl: e.tensor_tensor(
                        csl, hsl, csl, ALU.mult),
                    r=[htok, ctok], w=[ctok], tag=f"p4.ch{ct}.{n}.{th}")
                S.v(lambda e, csl=csl: e.tensor_tensor(
                        YCT[:, 0:TH], YCT[:, 0:TH], csl, ALU.add),
                    r=["yct", ctok], w=["yct"], tag=f"p4.y{ct}.{n}.{th}")
            S.dR(lambda e, ct=ct, t0=t0: e.dma_start(
                    out=ST1[:, 0:TH], in_=z_d[ct * 128:(ct + 1) * 128, t0:t0 + TH]),
                 r=[f"zd:{ct}", "st1x"], w=["st1x"], tag=f"p4.z{ct}.{th}")
            S.s(lambda e: e.activation(ST1[:, 0:TH], ST1[:, 0:TH], AF.Silu),
                r=["st1x"], w=["st1x"], tag=f"p4.sz{ct}.{th}")
            S.v(lambda e: e.tensor_tensor(YCT[:, 0:TH], YCT[:, 0:TH], ST1[:, 0:TH],
                                          ALU.mult),
                r=["yct", "st1x"], w=["yct"], tag=f"p4.mul{ct}.{th}")
            S.s(lambda e, ct=ct: e.activation(A2[:, ct * TH:(ct + 1) * TH], YCT[:, 0:TH],
                                              AF.Copy),
                r=["yct"], w=["a2"], tag=f"p4.hi{ct}.{th}")
            S.v(lambda e, ct=ct: e.tensor_tensor(A3[:, ct * TH:(ct + 1) * TH],
                                                 YCT[:, 0:TH],
                                                 A2[:, ct * TH:(ct + 1) * TH],
                                                 ALU.subtract),
                r=["yct", "a2"], w=["a3"], tag=f"p4.lo{ct}.{th}")
            p4_mm(ct, th, 0)
        p4_evac(th, 0)
        for pg in range(1, npg):
            for kct in range(NCT):
                p4_wload(kct, th, pg)
                p4_mm(kct, th, pg)
            p4_evac(th, pg)

    sch2 = list(range(0, TOK, 128))
    nj2 = len(sch2)
    for k in range(KD):
        S.s(lambda e, k=k: e.activation(ST1[:, 0:TOK], hres(k)[:, 0:TOK], AF.Square),
            r=[f"hr:{k}", "st1x"], w=["st1x"], tag=f"p5.sq{k}")
        def fmm(e, k=k):
            inst = None
            for j, c0 in enumerate(sch2):
                m = min(128, TOK - c0)
                inst = e.matmul(PS[6][0:m, j:j + 1], ST1[:, c0:c0 + m], ONEC[:, :],
                                start=(k == 0 and j == 0), stop=(k == KD - 1 and j == nj2 - 1),
                                skip_group_check=True)
            return inst
        S.t(fmm, r=["st1x", "onec", "ps6"], w=["ps6"], tag=f"p5.mm{k}")
    ev_cols(6, 0, sch2, TOK, "p5.ev")
    col2row(R_B, 0, nj2, TOK, "p5.r")
    S.s(lambda e: e.activation(ROWS[0:1, R_B:R_B + TOK], ROWS[0:1, R_B:R_B + TOK],
                               AF.Ln, scale=1.0 / D, bias=EPSC[0:1, :]),
        r=["rows", "epsc"], w=["rows"], tag="p5.ln")
    S.s(lambda e: e.activation(ROWS[0:1, R_B:R_B + TOK], ROWS[0:1, R_B:R_B + TOK],
                               AF.Exp, scale=-0.5),
        r=["rows"], w=["rows"], tag="p5.rsq")
    for j in range(NTC):
        S.t(lambda e, j=j: e.matmul(PS[j][:, 0:TC], ONE1[:, :],
                                    ROWS[0:1, R_B + j * TC:R_B + (j + 1) * TC],
                                    start=True, stop=True),
            r=["rows", "one1", f"ps{j}"], w=[f"ps{j}"], tag=f"p5.rep{j}")
        S.v(lambda e, j=j: e.tensor_copy(A4[:, j * TC:(j + 1) * TC], PS[j][:, 0:TC]),
            r=[f"ps{j}", "a4:0", "a4:1"], w=["a4:0", "a4:1"], tag=f"p5.repev{j}")
    for k in range(KD):
        S.v(lambda e, k=k: e.scalar_tensor_tensor(ST2[:, 0:TOK], hres(k)[:, 0:TOK],
                                                  L2C[:, k:k + 1], A4[:, 0:TOK],
                                                  op0=ALU.mult, op1=ALU.mult),
            r=[f"hr:{k}", "l2c", "a4:0", "a4:1", "st2f"], w=["st2f"], tag=f"p5.r2{k}")
        S.s(lambda e, k=k: e.activation(A2[:, k * TOK:(k + 1) * TOK], ST2[:, 0:TOK],
                                        AF.Copy),
            r=["st2f"], w=["a2"], tag=f"p5.hi{k}")
        S.v(lambda e, k=k: e.tensor_tensor(A3[:, k * TOK:(k + 1) * TOK], ST2[:, 0:TOK],
                                           A2[:, k * TOK:(k + 1) * TOK], ALU.subtract),
            r=["st2f", "a2"], w=["a3"], tag=f"p5.lo{k}")
    r2h = lambda k: A2[:, k * TOK:(k + 1) * TOK]
    r2l = lambda k: A3[:, k * TOK:(k + 1) * TOK]
    r2toks = ["a2", "a3"]

    def gup_phase(wg_ap, wu_ap, NP, spill_h, spill_l, sh_tok, sl_tok, scaled,
                  gsc, usc, pref):
        for i in range(NP):
            par = 0
            load_panel_pcols(wg_ap, i, KD * 128, par, f"{pref}.wg{i}")
            split_panel(par, KD * 128, f"{pref}.sg{i}")
            for tc in range(NTC):
                bank = 4 * (i % 2)
                proj_mm(par, r2h, r2l, r2toks, tc * TC, (tc + 1) * TC, bank,
                        f"{pref}.gmm{i}.{tc}")
                if scaled:
                    S.s(lambda e, bank=bank, tc=tc, i=i: e.activation(
                            ST1[:, tc * TC:(tc + 1) * TC], PS[bank][:, 0:TC], AF.Silu,
                            scale=gsc[:, i:i + 1]),
                        r=[f"ps{bank}", "fgsc", "st1x"], w=["st1x"],
                        tag=f"{pref}.si{i}.{tc}")
                else:
                    S.s(lambda e, bank=bank, tc=tc: e.activation(
                            ST1[:, tc * TC:(tc + 1) * TC], PS[bank][:, 0:TC], AF.Silu),
                        r=[f"ps{bank}", "st1x"], w=["st1x"], tag=f"{pref}.si{i}.{tc}")
            par = 1
            load_panel_pcols(wu_ap, i, KD * 128, par, f"{pref}.wu{i}")
            split_panel(par, KD * 128, f"{pref}.su{i}")
            for tc in range(NTC):
                bank = 4 * (i % 2) + 2
                proj_mm(par, r2h, r2l, r2toks, tc * TC, (tc + 1) * TC, bank,
                        f"{pref}.umm{i}.{tc}")
                if scaled:
                    S.v(lambda e, bank=bank, tc=tc, i=i: e.scalar_tensor_tensor(
                            ST2[:, tc * TC:(tc + 1) * TC], PS[bank][:, 0:TC],
                            usc[:, i:i + 1], ST1[:, tc * TC:(tc + 1) * TC],
                            op0=ALU.mult, op1=ALU.mult),
                        r=[f"ps{bank}", "fusc", "st1x", "st2f"], w=["st2f"],
                        tag=f"{pref}.pr{i}.{tc}")
                else:
                    S.v(lambda e, bank=bank, tc=tc: e.tensor_tensor(
                            ST2[:, tc * TC:(tc + 1) * TC], PS[bank][:, 0:TC],
                            ST1[:, tc * TC:(tc + 1) * TC], ALU.mult),
                        r=[f"ps{bank}", "st1x", "st2f"], w=["st2f"],
                        tag=f"{pref}.pr{i}.{tc}")
            ppar = i % 2
            ptok = f"ph:{ppar}"
            S.s(lambda e, ppar=ppar: e.activation(midhi(ppar), ST2[:, 0:TOK], AF.Copy),
                r=["st2f", ptok], w=[ptok], tag=f"{pref}.phi{i}")
            S.v(lambda e, ppar=ppar: e.tensor_tensor(midlo(ppar), ST2[:, 0:TOK],
                                                     midhi(ppar), ALU.subtract),
                r=["st2f", ptok], w=[ptok], tag=f"{pref}.plo{i}")
            S.dS(lambda e, i=i, ppar=ppar: e.dma_start(
                    out=spill_h[i * 128:(i + 1) * 128, :], in_=midhi(ppar)),
                 r=[ptok], w=[f"{sh_tok}:{i}"], tag=f"{pref}.sph{i}")
            S.dS(lambda e, i=i, ppar=ppar: e.dma_start(
                    out=spill_l[i * 128:(i + 1) * 128, :], in_=midlo(ppar)),
                 r=[ptok], w=[f"{sl_tok}:{i}"], tag=f"{pref}.spl{i}")

    gup_phase(wgp, wup, NMI, midh_d, midl_d, "mh", "ml", False, None, None, "p6")

    for k in range(KD):
        S.dR(lambda e, k=k: e.dma_start(out=ST1[:, 0:TOK],
                                        in_=x_in[k * 128:(k + 1) * 128, 3:3 + TOK]),
             r=["st1x"], w=["st1x"], tag=f"p6.x{k}")
        S.v(lambda e, k=k: e.tensor_tensor(hres(k)[:, 0:TOK], hres(k)[:, 0:TOK],
                                           ST1[:, 0:TOK], ALU.add),
            r=[f"hr:{k}", "st1x"], w=[f"hr:{k}"], tag=f"p6.hx{k}")

    def down_phase(wd_ap, wh_spill, wl_spill, whtok, wltok, NP, mh_tok, ml_tok,
                   mh_dram, ml_dram, evac, pref):
        first = [True]
        PGS2 = min(8, cfg.get("PGS", 8))
        npg = (KD + PGS2 - 1) // PGS2
        WFB = WF[:, :].bitcast(BF16)
        whslots = [WH[:, 0:2048], WH[:, 2048:4096], WFB[:, 0:2048], WFB[:, 4096:6144]]
        wlslots = [WL[:, 0:2048], WL[:, 2048:4096], WFB[:, 2048:4096], WFB[:, 6144:8192]]
        wtoks = [["whl:0"], ["whl:1"], ["wf:0"], ["wf:1"]]
        mhslots = [PHS[:, 0:TOK], DTRH[:, 0:TOK]]
        mlslots = [PLS[:, 0:TOK], DTRL[:, 0:TOK]]
        mtoks = [["ph:0"], ["ph:1"]]
        if TOK * 2 <= 2064:
            A4B = A4[:, :].bitcast(BF16)
            A5B2 = A5[:, :].bitcast(BF16)
            mhslots += [A4B[:, 0:TOK], A5B2[:, 0:TOK]]
            mlslots += [A4B[:, TOK:2 * TOK], A5B2[:, TOK:2 * TOK]]
            mtoks += [["a4:0", "a4:1"], ["a5:0", "a5:1"]]
        nwpar = 4
        nmpar = len(mhslots)
        for pg in range(npg):
            p_lo = pg * PGS2
            p_n = min(PGS2, KD - p_lo)
            for tc in range(NTC):
                for i in range(NP):
                    if first[0]:
                        par = i % 2
                        def ldwd(e, i=i, par=par):
                            sap = ap4(wd_ap, i * 128 * D, [[D, 128], [1, D]])
                            return e.dma_start(out=WF[:, par * 2048:par * 2048 + D],
                                               in_=sap)
                        S.dW(ldwd, r=[f"wf:{par}"], w=[f"wf:{par}"], tag=f"{pref}.wd{i}")
                        split_panel(par, D, f"{pref}.wds{i}")
                        if npg * NTC > 1:
                            S.dS(lambda e, i=i, par=par: e.dma_start(
                                    out=wh_spill[i * 128:(i + 1) * 128, :],
                                    in_=WH[:, par * 2048:par * 2048 + D]),
                                 r=[f"whl:{par}"], w=[whtok], tag=f"{pref}.swh{i}")
                            S.dS(lambda e, i=i, par=par: e.dma_start(
                                    out=wl_spill[i * 128:(i + 1) * 128, :],
                                    in_=WL[:, par * 2048:par * 2048 + D]),
                                 r=[f"whl:{par}"], w=[wltok], tag=f"{pref}.swl{i}")
                        wh_ap = WH[:, par * 2048:par * 2048 + D]
                        wl_ap = WL[:, par * 2048:par * 2048 + D]
                        wtk = [f"whl:{par}"]
                    else:
                        wpar = i % nwpar
                        wh_ap = whslots[wpar]
                        wl_ap = wlslots[wpar]
                        wtk = wtoks[wpar]
                        S.dW(lambda e, i=i, wh_ap=wh_ap: e.dma_start(
                                out=wh_ap[:, 0:D], in_=wh_spill[i * 128:(i + 1) * 128, :]),
                             r=[whtok] + wtk, w=wtk, tag=f"{pref}.rwh{i}.{pg}.{tc}")
                        S.dW(lambda e, i=i, wl_ap=wl_ap: e.dma_start(
                                out=wl_ap[:, 0:D], in_=wl_spill[i * 128:(i + 1) * 128, :]),
                             r=[wltok] + wtk, w=wtk, tag=f"{pref}.rwl{i}.{pg}.{tc}")
                    mpar = i % nmpar
                    mh_ap = mhslots[mpar]
                    ml_ap = mlslots[mpar]
                    mtk = mtoks[mpar]
                    S.dR(lambda e, i=i, tc=tc, mh_ap=mh_ap: e.dma_start(
                            out=mh_ap[:, 0:TC],
                            in_=mh_dram[i * 128:(i + 1) * 128, tc * TC:(tc + 1) * TC]),
                         r=[f"{mh_tok}:{i}"] + mtk, w=mtk,
                         tag=f"{pref}.rmh{i}.{pg}.{tc}")
                    S.dR(lambda e, i=i, tc=tc, ml_ap=ml_ap: e.dma_start(
                            out=ml_ap[:, 0:TC],
                            in_=ml_dram[i * 128:(i + 1) * 128, tc * TC:(tc + 1) * TC]),
                         r=[f"{ml_tok}:{i}"] + mtk, w=mtk,
                         tag=f"{pref}.rml{i}.{pg}.{tc}")
                    def fmm(e, i=i, wh_ap=wh_ap, wl_ap=wl_ap, mh_ap=mh_ap,
                            ml_ap=ml_ap, p_lo=p_lo, p_n=p_n, NP=NP):
                        inst = None
                        for pi in range(p_n):
                            p = p_lo + pi
                            wh = wh_ap[:, p * 128:(p + 1) * 128]
                            wl = wl_ap[:, p * 128:(p + 1) * 128]
                            inst = mm3(e, PS[pi][:, 0:TC], wh, wl,
                                       mh_ap[:, 0:TC], ml_ap[:, 0:TC],
                                       i == 0, i == NP - 1)
                        return inst
                    S.t(fmm, r=wtk + mtk,
                        w=[f"ps{pi}" for pi in range(p_n)],
                        tag=f"{pref}.dmm{pg}.{tc}.{i}")
                for pi in range(p_n):
                    evac(p_lo + pi, pi, tc)
                first[0] = False

    def evac6(p, pi, tc):
        S.v(lambda e: e.tensor_tensor(
                hres(p)[:, tc * TC:(tc + 1) * TC], PS[pi][:, 0:TC],
                hres(p)[:, tc * TC:(tc + 1) * TC], ALU.add),
            r=[f"ps{pi}", f"hr:{p}"], w=[f"hr:{p}"], tag=f"p6.x1{p}.{tc}")

    down_phase(wdt, wdh_d, wdl_d, "wdh", "wdl", NMI, "mh", "ml",
               midh_d, midl_d, evac6, "p6d")

    for k in range(KD):
        S.s(lambda e, k=k: e.activation(ST1[:, 0:TOK], hres(k)[:, 0:TOK], AF.Square),
            r=[f"hr:{k}", "st1x"], w=["st1x"], tag=f"p7.sq{k}")
        def fmm(e, k=k):
            inst = None
            for j, c0 in enumerate(sch2):
                m = min(128, TOK - c0)
                e.matmul(PS[6][0:m, j:j + 1], hres(k)[:, c0:c0 + m], ONEC[:, :],
                         start=(k == 0 and j == 0), stop=(k == KD - 1 and j == nj2 - 1),
                         skip_group_check=True)
                inst = e.matmul(PS[7][0:m, j:j + 1], ST1[:, c0:c0 + m], ONEC[:, :],
                                start=(k == 0 and j == 0), stop=(k == KD - 1 and j == nj2 - 1),
                                skip_group_check=True)
            return inst
        S.t(fmm, r=[f"hr:{k}", "st1x", "onec", "ps6", "ps7"], w=["ps6", "ps7"],
            tag=f"p7.mm{k}")
    ev_cols(6, 0, sch2, TOK, "p7.ev1")
    ev_cols(7, 32, sch2, TOK, "p7.ev2")
    col2row(R_A, 0, nj2, TOK, "p7.ra")
    col2row(R_B, 32, nj2, TOK, "p7.rb")
    S.v(lambda e: e.tensor_scalar(ROWS[0:1, R_A:R_A + TOK], ROWS[0:1, R_A:R_A + TOK],
                                  1.0 / D, None, ALU.mult),
        r=["rows"], w=["rows"], tag="p7.mean")
    S.v(lambda e: e.tensor_tensor(ROWS[0:1, R_C:R_C + TOK], ROWS[0:1, R_A:R_A + TOK],
                                  ROWS[0:1, R_A:R_A + TOK], ALU.mult),
        r=["rows"], w=["rows"], tag="p7.m2")
    S.v(lambda e: e.scalar_tensor_tensor(ROWS[0:1, R_B:R_B + TOK], ROWS[0:1, R_B:R_B + TOK],
                                         1.0 / D, ROWS[0:1, R_C:R_C + TOK],
                                         op0=ALU.mult, op1=ALU.subtract),
        r=["rows"], w=["rows"], tag="p7.var")
    S.s(lambda e: e.activation(ROWS[0:1, R_C:R_C + TOK], ROWS[0:1, R_B:R_B + TOK],
                               AF.Ln, bias=EPSC[0:1, :]),
        r=["rows", "epsc"], w=["rows"], tag="p7.ln")
    S.s(lambda e: e.activation(ROWS[0:1, R_C:R_C + TOK], ROWS[0:1, R_C:R_C + TOK],
                               AF.Exp, scale=-0.5),
        r=["rows"], w=["rows"], tag="p7.s")
    for j in range(NTC):
        S.t(lambda e, j=j: e.matmul(PS[0][:, 0:TC], ONE1[:, :],
                                    ROWS[0:1, R_A + j * TC:R_A + (j + 1) * TC],
                                    start=True, stop=True),
            r=["rows", "one1", "ps0"], w=["ps0"], tag=f"p7.repm{j}")
        S.v(lambda e, j=j: e.tensor_copy(A4[:, j * TC:(j + 1) * TC], PS[0][:, 0:TC]),
            r=["ps0", "a4:0", "a4:1"], w=["a4:0", "a4:1"], tag=f"p7.evm{j}")
        S.t(lambda e, j=j: e.matmul(PS[1][:, 0:TC], ONE1[:, :],
                                    ROWS[0:1, R_C + j * TC:R_C + (j + 1) * TC],
                                    start=True, stop=True),
            r=["rows", "one1", "ps1"], w=["ps1"], tag=f"p7.repsd{j}")
        S.v(lambda e, j=j: e.tensor_copy(A5[:, j * TC:(j + 1) * TC], PS[1][:, 0:TC]),
            r=["ps1", "a5:0", "a5:1"], w=["a5:0", "a5:1"], tag=f"p7.evsd{j}")
    for k in range(KD):
        S.v(lambda e, k=k: e.tensor_tensor(ST2[:, 0:TOK], hres(k)[:, 0:TOK],
                                           A4[:, 0:TOK], ALU.subtract),
            r=[f"hr:{k}", "a4:0", "a4:1", "st2f"], w=["st2f"], tag=f"p7.sub{k}")
        S.v(lambda e: e.tensor_tensor(ST2[:, 0:TOK], ST2[:, 0:TOK], A5[:, 0:TOK],
                                      ALU.mult),
            r=["st2f", "a5:0", "a5:1"], w=["st2f"], tag=f"p7.l2{k}")
        S.s(lambda e, k=k: e.activation(A2[:, k * TOK:(k + 1) * TOK], ST2[:, 0:TOK],
                                        AF.Copy),
            r=["st2f"], w=["a2"], tag=f"p7.hi{k}")
        S.v(lambda e, k=k: e.tensor_tensor(A3[:, k * TOK:(k + 1) * TOK], ST2[:, 0:TOK],
                                           A2[:, k * TOK:(k + 1) * TOK], ALU.subtract),
            r=["st2f", "a2"], w=["a3"], tag=f"p7.lo{k}")

    gup_phase(fgp, fup, NFI, fmh_d, fml_d, "fh", "fl", True, FGSC, FUSC, "p7")

    outtoks = []

    def evac7(p, pi, tc):
        S.v(lambda e, pi=pi, p=p, tc=tc: e.scalar_tensor_tensor(
                ST2[:, 0:TC], PS[pi][:, 0:TC], FDSC[:, p:p + 1],
                hres(p)[:, tc * TC:(tc + 1) * TC], op0=ALU.mult, op1=ALU.add),
            r=[f"ps{pi}", "fdsc", f"hr:{p}", "st2f"], w=["st2f"],
            tag=f"p7.x2{p}.{tc}")
        ot = f"out:{p}:{tc}"
        S.dS(lambda e, p=p, tc=tc: e.dma_start(
                out=out_d[p * 128:(p + 1) * 128, tc * TC:(tc + 1) * TC],
                in_=ST2[:, 0:TC]),
             r=["st2f"], w=[ot], tag=f"p7.out{p}.{tc}")
        outtoks.append(ot)

    down_phase(fdt, fdh_d, fdl_d, "fdh", "fdl", NFI, "fh", "fl",
               fmh_d, fml_d, evac7, "p7d")

    S.g(lambda e: e.memset(ONEC[:, :], 1.0), r=outtoks, w=["onec"], tag="final")

    S.finalize(nc)
    return nc


def _prep_inputs(cfg, A):
    D, TOK = cfg["D"], cfg["TOK"]
    x = np.asarray(A["L0_attn_hidden_states"], dtype=np.float32)
    B = x.shape[0]
    per = NCORES // B
    assert x.shape[1] == per * TOK
    ct = np.ascontiguousarray

    KD = D // 128

    def pack_panels(w):
        wt = np.asarray(w, np.float32).T
        Dd, P = wt.shape
        npt = P // 128
        kd = Dd // 128
        v = wt.reshape(kd, 128, npt, 128).transpose(2, 1, 0, 3)
        return np.ascontiguousarray(v.reshape(npt * 128, kd * 128))

    shared = dict(
        w1p=pack_panels(A["L0_attn_in_proj_w"]),
        dtwt=ct(np.asarray(A["L0_attn_dt_proj_w"], np.float32).T),
        dtb=ct(np.asarray(A["L0_attn_dt_proj_b"], np.float32)),
        convw=ct(np.asarray(A["L0_attn_conv_w"], np.float32).reshape(cfg["DI"], cfg["DCONV"])),
        convb=ct(np.asarray(A["L0_attn_conv_b"], np.float32)),
        alog=ct(np.asarray(A["L0_attn_A_log"], np.float32)),
        dpar=ct(np.asarray(A["L0_attn_D_param"], np.float32)),
        woutt=ct(np.asarray(A["L0_attn_out_proj_w"], np.float32).T),
        ln1w=ct(np.asarray(A["L0_attn_ln1_w"], np.float32)),
        ln2w=ct(np.asarray(A["L0_attn_ln2_w"], np.float32)),
        wgp=pack_panels(A["L0_attn_gate_w"]),
        wup=pack_panels(A["L0_attn_up_w"]),
        wdt=ct(np.asarray(A["L0_attn_down_w"], np.float32).T),
        fgp=pack_panels(A["L0_ffn_gate_w"]),
        fup=pack_panels(A["L0_ffn_up_w"]),
        fdt=ct(np.asarray(A["L0_ffn_down_w"], np.float32).T),
        fgs=ct(np.asarray(A["L0_ffn_gate_s"], np.float32)),
        fus=ct(np.asarray(A["L0_ffn_up_s"], np.float32)),
        fds=ct(np.asarray(A["L0_ffn_down_s"], np.float32)),
    )
    in_maps = []
    for q in range(NCORES):
        b, q4 = q // per, q % per
        lo = q4 * TOK
        x3 = np.zeros((3, D), np.float32) if q4 == 0 else x[b, lo - 3:lo]
        xs = np.concatenate([x3, x[b, lo:lo + TOK]], axis=0)
        gm = np.zeros((8, 8), np.float32)
        m2 = np.zeros((8,), np.float32)
        for j in range(per):
            jj = b * per + j
            if j < q4:
                m2[jj] = 1.0
                for k in range(j + 1, q4):
                    gm[jj, b * per + k] = 1.0
        im = dict(shared)
        im["x_in"] = ct(xs.T)
        im["gmc"] = ct(np.repeat(gm.reshape(1, 64), 128, axis=0))
        im["m2c"] = ct(np.repeat(m2.reshape(1, 8), 128, axis=0))
        in_maps.append(im)
    return in_maps


def _assemble(cfg, results):
    per = NCORES // 2
    outs = [np.asarray(r["out"]) for r in results]
    bs = [np.concatenate([outs[b * per + q].T for q in range(per)], axis=0)
          for b in range(2)]
    return np.stack(bs, axis=0).astype(np.float32)


def kernel(**inputs):
    cfg = real_cfg()
    in_maps = _prep_inputs(cfg, inputs)
    nc = build_nc(cfg)
    res = run_bass_kernel_spmd(nc, in_maps, core_ids=list(range(NCORES)))
    return _assemble(cfg, res.results)
